# version 18
# baseline (speedup 1.0000x reference)
import contextlib

import numpy as np
import ml_dtypes

import concourse.bass as bass
import concourse.mybir as mybir
from concourse.bass_utils import run_bass_kernel_spmd

F32 = mybir.dt.float32
BF16 = mybir.dt.bfloat16
ALU = mybir.AluOpType
ACT = mybir.ActivationFunctionType
AX = mybir.AxisListType

B, T, D = 4, 4096, 1024
TQ = T // 2
H, HD, G = 16, 64, 4
DK = G * HD
DF = 2816
NCORES = 8
EPS = 1e-5

PAIRS = [(0, 4), (1, 5), (2, 6), (3, 7), (8, 12), (9, 13), (10, 14), (11, 15)]
NP_ = len(PAIRS)
NQC = TQ // 512
NKT = T // 128
FTC = 2
FTT = 8
NCH = [(0, 512), (512, 512), (1024, 512), (1536, 512), (2048, 512), (2560, 256)]

STREAMS = {"sync": "dm", "tensor": "tp", "vector": "vp", "scalar": "sp",
           "gpsimd": "gp"}
SEMS = ("dm", "da", "tp", "vp", "sp", "gp", "hzv", "hzs", "hzt", "hzg")

DEBUG_TAPS = False
DEBUG_OUT = False
STOP_AFTER = None
_PHASES = ["P0", "P1", "P2", "P3a", "P3b"]


def _on(ph):
    return STOP_AFTER is None or _PHASES.index(ph) <= _PHASES.index(STOP_AFTER)


class Plan:
    def __init__(self):
        self.ops = []
        self.ctr = {s: 0 for s in SEMS}
        self.pending = {}

    def barrier(self):
        snap = [(s, v) for s, v in self.ctr.items() if v > 0]
        for eng in STREAMS:
            self.pending[eng] = list(snap)

    def op(self, engine, body, waits, dma=False, noinc=False):
        sem = STREAMS[engine]
        dma_sem = None
        if dma:
            dma_sem = "da" if engine == "scalar" else "dm"
        waits = list(waits) + self.pending.pop(engine, [])
        w = []
        for sv in waits:
            if sv is None:
                continue
            s, v = sv
            if v <= 0:
                continue
            if s not in ("dm", "da") and s == sem:
                continue
            w.append((s, v))
        self.ops.append((engine, w, body, dma_sem, noinc))
        if noinc:
            return None
        if dma_sem:
            self.ctr[dma_sem] += 16
            return (dma_sem, self.ctr[dma_sem])
        self.ctr[sem] += 1
        return (sem, self.ctr[sem])

    def emit(self, engine, eng, sems):
        hz = {"vector": "hzv", "scalar": "hzs", "tensor": "hzt",
              "gpsimd": "hzg"}.get(engine, "hzv")
        rec = _Rec(eng, sems[hz], sems[STREAMS[engine]])
        hw = {}
        for (oe, waits, body, dma_sem, noinc) in self.ops:
            if oe != engine:
                continue
            for (s, v) in waits:
                if hw.get(s, -1) >= v:
                    continue
                eng.wait_ge(sems[s], v)
                hw[s] = v
            rec.last = None
            body(rec)
            if noinc:
                continue
            if dma_sem:
                rec.last.then_inc(sems[dma_sem], 16)
            elif rec.last is not None:
                rec.last.then_inc(sems[STREAMS[engine]], 1)
                rec.scount += 1
                rec.inced = rec.last
            else:
                eng.nop().then_inc(sems[STREAMS[engine]], 1)
                rec.scount += 1


class _Rec:

    _OPS = frozenset((
        "tensor_copy", "tensor_tensor", "tensor_add", "tensor_sub",
        "tensor_mul", "tensor_max", "tensor_scalar", "tensor_single_scalar",
        "tensor_scalar_add", "tensor_scalar_sub", "tensor_scalar_mul",
        "tensor_scalar_max", "tensor_scalar_min", "tensor_relu",
        "tensor_reduce", "reduce_max", "reduce_sum", "scalar_tensor_tensor",
        "reciprocal", "memset", "memzero", "activation", "sign", "square",
        "sqrt", "mul", "copy", "add", "bn_stats", "bn_aggr", "select",
        "transpose", "iota",
    ))

    def __init__(self, eng, hz_sem, stream_sem):
        self._e = eng
        self._hz = hz_sem
        self._hzc = 0
        self._ssem = stream_sem
        self.scount = 0
        self.inced = None
        self.last = None
        self.lastf = None
        self._dirty_w = set()
        self._dirty_r = set()

    @staticmethod
    def _apname(a):
        t = getattr(a, "tensor", None)
        if t is None:
            return None
        n = getattr(t, "name", None)
        return n if isinstance(n, str) else None

    def _fence(self):
        if self.lastf is not None:
            if self.lastf is self.inced:
                self._e.wait_ge(self._ssem, self.scount)
            else:
                self._hzc += 1
                self.lastf.then_inc(self._hz, 1)
                self._e.wait_ge(self._hz, self._hzc)
        self._dirty_w.clear()
        self._dirty_r.clear()

    def __getattr__(self, k):
        a = getattr(self._e, k)
        if not callable(a):
            return a
        if k not in self._OPS:
            def w0(*args, **kw):
                r = a(*args, **kw)
                if hasattr(r, "then_inc"):
                    self.last = r
                return r
            return w0

        def w(*args, **kw):
            outs = set()
            ins = set()
            for i, ar in enumerate(args):
                n = self._apname(ar)
                if n:
                    (outs if i == 0 else ins).add(n)
            for key, ar in kw.items():
                n = self._apname(ar)
                if n:
                    (outs if key == "accum_out" else ins).add(n)
            if (ins & self._dirty_w) or (outs & (self._dirty_w
                                                | self._dirty_r)):
                self._fence()
            r = a(*args, **kw)
            if hasattr(r, "then_inc"):
                self.last = r
                self.lastf = r
            self._dirty_w |= outs
            self._dirty_r |= ins
            return r
        return w


class Tok:
    __slots__ = ("w", "r")

    def __init__(self):
        self.w = None
        self.r = None


def mm3_list(hi_l, lo_l, hi_r, lo_r):
    return ((hi_l, hi_r), (hi_l, lo_r), (lo_l, hi_r))


def build_program():
    nc = bass.Bass("TRN2", target_bir_lowering=False,
                   detect_race_conditions=False)

    x_all = nc.dram_tensor("x_all", [T, D], F32, kind="ExternalInput")
    x_own = nc.dram_tensor("x_own", [TQ, D], F32, kind="ExternalInput")
    wq_h = nc.dram_tensor("wq", [D, D], F32, kind="ExternalInput")
    wk_h = nc.dram_tensor("wk", [D, DK], F32, kind="ExternalInput")
    wv_h = nc.dram_tensor("wv", [D, DK], F32, kind="ExternalInput")
    wo_h = nc.dram_tensor("wo", [D, D], F32, kind="ExternalInput")
    wg_h = nc.dram_tensor("wg", [DF, D], F32, kind="ExternalInput")
    wu_h = nc.dram_tensor("wu", [DF, D], F32, kind="ExternalInput")
    wd_h = nc.dram_tensor("wd", [D, DF], F32, kind="ExternalInput")
    out_h = nc.dram_tensor("out", [TQ, D], F32, kind="ExternalOutput")
    _k = dict(kind="ExternalOutput") if DEBUG_OUT else {}

    def _k2():
        return dict(kind="ExternalOutput")

    x1_hbm = nc.dram_tensor("x1_hbm", [TQ, D], F32, **_k)
    xa16h = nc.dram_tensor("xa16h", [T, D], BF16, **_k)
    xa16l = nc.dram_tensor("xa16l", [T, D], BF16)
    xo16h = nc.dram_tensor("xo16h", [TQ, D], BF16)
    xo16l = nc.dram_tensor("xo16l", [TQ, D], BF16)
    v16h_hbm = nc.dram_tensor("v16h_hbm", [DK, T], BF16, **_k)
    v16l_hbm = nc.dram_tensor("v16l_hbm", [DK, T], BF16)
    wqs_hi_h = nc.dram_tensor("wqs_hi", [D, D], BF16, **_k)
    wqs_lo_h = nc.dram_tensor("wqs_lo", [D, D], BF16)
    wos_hi_h = nc.dram_tensor("wos_hi", [D, D], BF16)
    wos_lo_h = nc.dram_tensor("wos_lo", [D, D], BF16)
    wgq_hi_h = nc.dram_tensor("wgq_hi", [DF, D], BF16, **_k)
    wgq_lo_h = nc.dram_tensor("wgq_lo", [DF, D], BF16)
    wuq_hi_h = nc.dram_tensor("wuq_hi", [DF, D], BF16)
    wuq_lo_h = nc.dram_tensor("wuq_lo", [DF, D], BF16)
    wdq_hi_h = nc.dram_tensor("wdq_hi", [D, DF], BF16)
    wdq_lo_h = nc.dram_tensor("wdq_lo", [D, DF], BF16)
    x1q16h = nc.dram_tensor("x1q16h", [TQ, D], BF16)
    x1q16l = nc.dram_tensor("x1q16l", [TQ, D], BF16)
    hq16h = nc.dram_tensor("hq16h", [1024, 512], BF16)
    hq16l = nc.dram_tensor("hq16l", [1024, 512], BF16)
    if DEBUG_TAPS:
        dbg_qph = nc.dram_tensor("dbg_qph", [128, 512], BF16, **_k2())
        dbg_qpl = nc.dram_tensor("dbg_qpl", [128, 512], BF16, **_k2())
        dbg_ph = nc.dram_tensor("dbg_ph", [128, 2, 512], BF16, **_k2())
        dbg_pf = nc.dram_tensor("dbg_pf", [128, 2, 512], F32, **_k2())
        dbg_at = nc.dram_tensor("dbg_at", [128, 512], F32, **_k2())
        dbg_rd = nc.dram_tensor("dbg_rd", [1, 512], F32, **_k2())
        dbg_y = nc.dram_tensor("dbg_y", [128, 1024], F32, **_k2())
        dbg_kh = nc.dram_tensor("dbg_kh", [128, 2, 4096], BF16, **_k2())

    es = contextlib.ExitStack()

    def mk_sb(stack):
        def f(name, shape, dtype):
            return stack.enter_context(nc.sbuf_tensor(name, shape, dtype))
        return f

    with es:
        sb = mk_sb(es)
        ones64 = sb("ones64", [1, 64], BF16)
        qA = sb("qA", [128, 512], F32)
        qB = sb("qB", [128, 512], F32)
        qC = sb("qC", [128, 512], F32)
        qsc = sb("qsc", [128, 32], F32)
        qr2 = sb("qr2", [128, 32], F32)
        x_stage = sb("x_stage", [128, D], F32)
        y_sb = sb("y_sb", [128, D], F32)
        x1_sb = sb("x1_sb", [128, D], F32)
        sq_scr = sb("sq_scr", [128, D], BF16)
        ln1 = sb("ln1", [128, 1], F32)
        ln2 = sb("ln2", [128, 1], F32)
        ln3 = sb("ln3", [128, 1], F32)
        ln4 = sb("ln4", [128, 1], F32)
        ln5 = sb("ln5", [128, 1], F32)

        ps_a = es.enter_context(nc.psum_tensor("ps_a", [128, 512], F32))
        ps_b = es.enter_context(nc.psum_tensor("ps_b", [128, 512], F32))
        ps_c = es.enter_context(nc.psum_tensor("ps_c", [128, 512], F32))
        ps_d = es.enter_context(nc.psum_tensor("ps_d", [128, 512], F32))
        ps_o = es.enter_context(nc.psum_tensor("ps_o", [128, 512], F32))
        ps_ob = es.enter_context(nc.psum_tensor("ps_ob", [128, 512], F32))
        ps_q = es.enter_context(nc.psum_tensor("ps_q", [128, 512], F32))
        ps_rt = es.enter_context(nc.psum_tensor("ps_rt", [128, 512], F32))

        P = Plan()
        toks = {}

        def tok(name):
            if name not in toks:
                toks[name] = Tok()
            return toks[name]

        def rd(*names):
            return [tok(n).w for n in names]

        def wr(*names):
            return [tok(n).r for n in names]

        def _mark(sv, reads, writes):
            for n in writes:
                tok(n).w = sv
            for n in reads:
                tok(n).r = sv
            return sv

        def dma(src_fn, dst_fn, reads=(), writes=(), waits=()):
            w = list(waits) + rd(*reads) + wr(*writes)
            sv = P.op("sync", lambda e, a=src_fn, b=dst_fn:
                      e.dma_start(out=b(), in_=a()), w, dma=True)
            return _mark(sv, reads, writes)

        def dma_t(src_fn, dst_fn, reads=(), writes=(), waits=()):
            w = list(waits) + rd(*reads) + wr(*writes)
            sv = P.op("scalar", lambda e, a=src_fn, b=dst_fn:
                      e.dma_start_transpose(b(), a()), w, dma=True)
            return _mark(sv, reads, writes)

        def comp(engine, body, reads=(), writes=(), waits=()):
            w = list(waits) + rd(*reads) + wr(*writes)
            sv = P.op(engine, body, w)
            return _mark(sv, reads, writes)

        def emit_ln(store_tok):
            comp("vector",
                 lambda e: e.tensor_reduce(ln1[:], y_sb[:], AX.X, ALU.add),
                 writes=["ln1"], waits=rd("ydc0", "ydc1"))
            comp("scalar",
                 lambda e: e.activation(sq_scr[:], y_sb[:], ACT.Square,
                                        accum_out=ln2[:]),
                 writes=["ln2"], waits=rd("ydc0", "ydc1"))

            def ln_var(e):
                e.tensor_scalar(ln3[:], ln1[:], 1.0 / D, None, ALU.mult)
                e.tensor_tensor(ln4[:], ln3[:], ln3[:], ALU.mult)
                e.scalar_tensor_tensor(ln4[:], ln2[:], 1.0 / D, ln4[:],
                                       ALU.mult, ALU.subtract)
                return e.tensor_scalar(ln4[:], ln4[:], EPS, None, ALU.add)
            comp("vector", ln_var, reads=["ln1", "ln2"], writes=["ln3", "ln4"])
            comp("scalar", lambda e: e.activation(ln5[:], ln4[:], ACT.Sqrt),
                 reads=["ln4"], writes=["ln5"])

            def ln_fin(e):
                e.reciprocal(ln4[:], ln5[:])
                return e.scalar_tensor_tensor(ln3[:], ln3[:], -1.0, ln4[:],
                                              ALU.mult, ALU.mult)
            comp("vector", ln_fin, reads=["ln5"], writes=["lnf"])

            def ln_app(e):
                return e.activation(x1_sb[:], y_sb[:], ACT.Identity,
                                    bias=ln3[:], scale=ln4[:])
            sv = comp("scalar", ln_app, reads=["lnf"], writes=[store_tok])
            for n in ("ydc0", "ydc1", "ln3", "ln4"):
                tok(n).r = sv

        def x_split_store(src_h, h16, l16, row0):
            dma(lambda: src_h[row0:row0 + 128, :], lambda: x_stage[:],
                writes=["xs0"])

            def split(e):
                e.tensor_copy(xt_hi[:], x_stage[:])
                return e.tensor_tensor(xt_lo[:], x_stage[:], xt_hi[:],
                                       ALU.subtract)
            comp("gpsimd", split, reads=["xs0"], writes=["xt0"])
            dma(lambda: xt_hi[:], lambda: h16[row0:row0 + 128, :],
                reads=["xt0"], writes=["x16s"])
            dma(lambda: xt_lo[:], lambda: l16[row0:row0 + 128, :],
                reads=["xt0"], writes=["x16s"])

        def xfm_load(h16, l16, row0):
            for cc in range(8):
                dma_t(lambda h16=h16, row0=row0, cc=cc:
                      h16[row0:row0 + 512, cc * 128:(cc + 1) * 128],
                      lambda cc=cc: xfm_hi[:, cc, :],
                      writes=["xfm"], waits=rd("x16s"))
                dma_t(lambda l16=l16, row0=row0, cc=cc:
                      l16[row0:row0 + 512, cc * 128:(cc + 1) * 128],
                      lambda cc=cc: xfm_lo[:, cc, :],
                      writes=["xfm"], waits=rd("x16s"))

        esA = contextlib.ExitStack()
        sbA = mk_sb(esA)
        K_st = sbA("K_st", [128, 4, T], BF16)
        wst3 = sbA("wst3", [128, 2, 512], F32)
        q3h = sbA("q3h", [128, 2, 512], BF16)
        q3l = sbA("q3l", [128, 2, 512], BF16)
        xt_hi = sbA("xt_hi", [128, D], BF16)
        xt_lo = sbA("xt_lo", [128, D], BF16)
        xfm_hi = sbA("xfm_hi", [128, 8, 512], BF16)
        xfm_lo = sbA("xfm_lo", [128, 8, 512], BF16)
        ktmp = sbA("ktmp", [128, 512], BF16)
        Vp_hi = sbA("Vp_hi", [128, NKT, 4, 66], BF16)
        Vp_lo = sbA("Vp_lo", [128, NKT, 4, 64], BF16)
        Qstk = sbA("Qstk", [128, 2, 2, 512], BF16)
        Qlo2 = sbA("Qlo2", [128, 2, 2, 512], BF16)
        p_buf = sbA("p_buf", [128, 2, 2, 512], F32)
        p16h = sbA("p16h", [128, 3, 2, 512], BF16)
        p16l = sbA("p16l", [128, 3, 2, 512], BF16)
        attn_hi = sbA("attn_hi", [128, NP_, 512], BF16)
        attn_lo = sbA("attn_lo", [128, NP_, 512], BF16)
        attn_st = sbA("attn_st", [128, 512], F32)
        rden_r = sbA("rden_r", [1, 512], F32)

        D_sb = sbA("D_sb", [64, 512], F32)
        rd16h = D_sb[:].bitcast(BF16)[0:1, 0:512]
        rd16l = D_sb[:].bitcast(BF16)[0:1, 512:1024]
        vstg = sbA("vstg", [128, 2, 2, 256], BF16)

        esP1 = contextlib.ExitStack()
        sbP1 = mk_sb(esP1)
        wstage = sbP1("wstage", [128, 4096], F32)
        wk_hi = sbP1("wk_hi", [128, 8, DK], BF16)
        wk_lo = sbP1("wk_lo", [128, 8, DK], BF16)
        wv_hi = sbP1("wv_hi", [128, 8, DK], BF16)
        wv_lo = sbP1("wv_lo", [128, 8, DK], BF16)
        v16h = sbP1("v16h", [128, 512], BF16)
        v16l = sbP1("v16l", [128, 512], BF16)

        comp("vector", lambda e: e.memset(ones64[:], 1.0), writes=["ones64"])

        def vaug_init(e):
            e.memset(Vp_hi[:, :, :, 64:66], 0.0)
            return e.memset(Vp_hi[:, :, :, 64:65], 1.0)
        comp("vector", vaug_init, writes=["vaug"])

        for wh, shh, slh in ((wq_h, wqs_hi_h, wqs_lo_h),
                             (wo_h, wos_hi_h, wos_lo_h)):
            for hv in range(2):
                dma(lambda wh=wh, hv=hv:
                    wh[hv * 512:(hv + 1) * 512, :]
                    .rearrange("(cc p) d -> p cc d", p=128),
                    lambda: wstage[:].rearrange("p (cc d) -> p cc d", cc=4),
                    writes=["wstage"])

                def wsp(e):
                    flat = K_st[:].rearrange("p a b -> p (a b)")
                    khf = flat[:, 0:4096]
                    klf = flat[:, 8192:12288]
                    e.tensor_copy(khf, wstage[:])
                    return e.tensor_tensor(klf, wstage[:], khf,
                                           ALU.subtract)
                comp("vector", wsp, reads=["wstage"], writes=["kbuf"])

                dma(lambda: K_st[:].rearrange("p a b -> p (a b)")
                    [:, 0:4096].rearrange("p (c d) -> p c d", c=4),
                    lambda shh=shh, hv=hv:
                    shh[hv * 512:(hv + 1) * 512, :]
                    .rearrange("(cc p) d -> p cc d", p=128),
                    reads=["kbuf"], writes=["wscr"])
                dma(lambda: K_st[:].rearrange("p a b -> p (a b)")
                    [:, 8192:12288].rearrange("p (c d) -> p c d", c=4),
                    lambda slh=slh, hv=hv:
                    slh[hv * 512:(hv + 1) * 512, :]
                    .rearrange("(cc p) d -> p cc d", p=128),
                    reads=["kbuf"], writes=["wscr"])

        for wname, wh, whi, wlo in (("wk", wk_h, wk_hi, wk_lo),
                                    ("wv", wv_h, wv_hi, wv_lo)):
            st3 = wstage[:, 0:8 * DK].rearrange("p (cc d) -> p cc d", d=DK)
            dma(lambda wh=wh: wh[:].rearrange("(cc p) d -> p cc d", p=128),
                lambda st3=st3: st3, writes=["wstage"])

            def wsplit(e, whi=whi, wlo=wlo):
                flat = wstage[:, 0:8 * DK]
                hi_f = whi[:].rearrange("p c d -> p (c d)")
                lo_f = wlo[:].rearrange("p c d -> p (c d)")
                e.tensor_copy(hi_f, flat)
                return e.tensor_tensor(lo_f, flat, hi_f, ALU.subtract)
            comp("vector", wsplit, reads=["wstage"], writes=[wname])

        def q_stage1(e, src, n):
            nb = n // 16
            sc, r2 = qsc[:, 0:nb], qr2[:, 0:nb]
            src3 = src.rearrange("p (nb b) -> p nb b", b=16)
            e.tensor_reduce(sc, src3, AX.X, ALU.max,
                            apply_absolute_value=True)
            e.reciprocal(r2, sc)
            e.tensor_scalar(r2, r2, 6.0, None, ALU.mult)
            e.reciprocal(sc, r2)
            e.tensor_scalar(sc, sc, 1e-12, None, ALU.max)
            e.reciprocal(r2, sc)
            return e.tensor_scalar(r2, r2, 2.0, None, ALU.mult)

        def q_stage2(e, src, n):
            nb = n // 16
            A_, B_ = qA[:, 0:n], qB[:, 0:n]
            r2bc = qr2[:, 0:nb].unsqueeze(2).broadcast_to([128, nb, 16])
            A3 = A_.rearrange("p (nb b) -> p nb b", b=16)
            e.scalar_tensor_tensor(A_, src, -1.0, src, ALU.mult, ALU.max)
            e.tensor_tensor(A3, A3, r2bc, ALU.mult)
            e.tensor_scalar(B_, A_, 0.5, None, ALU.is_gt)
            e.scalar_tensor_tensor(B_, A_, 1.5, B_, ALU.is_gt, ALU.add)
            e.scalar_tensor_tensor(B_, A_, 2.5, B_, ALU.is_gt, ALU.add)
            return e.scalar_tensor_tensor(B_, A_, 3.5, B_, ALU.is_gt,
                                          ALU.add)

        def q_stage3(e, src, n, hi_out, lo_out):
            nb = n // 16
            A_, B_, C_ = qA[:, 0:n], qB[:, 0:n], qC[:, 0:n]
            scbc = qsc[:, 0:nb].unsqueeze(2).broadcast_to([128, nb, 16])
            A3 = A_.rearrange("p (nb b) -> p nb b", b=16)
            e.tensor_scalar(C_, A_, 5.0, None, ALU.is_gt)
            e.scalar_tensor_tensor(C_, A_, 7.0, C_, ALU.is_gt, ALU.add)
            e.scalar_tensor_tensor(B_, C_, 2.0, B_, ALU.mult, ALU.add)
            e.tensor_scalar(C_, A_, 10.0, 4.0, ALU.is_gt, ALU.mult)
            e.tensor_tensor(B_, B_, C_, ALU.add)
            e.tensor_scalar(C_, src, 0.0, None, ALU.is_ge)
            e.scalar_tensor_tensor(A_, C_, -0.5, B_, ALU.add, ALU.mult)
            return e.tensor_tensor(A3, A3, scbc, ALU.mult)

        def q_split_g(e, n, hi_out, lo_out):
            A_ = qA[:, 0:n]
            e.tensor_copy(hi_out, A_)
            return e.tensor_tensor(lo_out, A_, hi_out, ALU.subtract)

        def quant_seq(e, src, n, hi_out, lo_out):
            q_stage1(e, src, n)
            q_stage2(e, src, n)
            q_stage3(e, src, n, hi_out, lo_out)
            return q_split_g(e, n, hi_out, lo_out)

        _p3a = []

        def _p3a_tile(idx, wh, hh, ll, r0, c0, n, stok="scr", lw=()):
            p = idx % 2

            def em1(p=p, wh=wh, r0=r0, c0=c0, n=n, lw=tuple(lw)):
                dma(lambda: wh[r0:r0 + 128, c0:c0 + n],
                    lambda: wst3[:, p, 0:n], writes=[f"fst{p}"],
                    waits=rd(*lw))
                comp("vector",
                     lambda e, p=p, n=n: q_stage1(e, wst3[:, p, 0:n], n),
                     reads=[f"fst{p}"], writes=["qstg"])

            def em2(p=p, n=n):
                comp("vector",
                     lambda e, p=p, n=n: q_stage2(e, wst3[:, p, 0:n], n),
                     reads=["qstg"], writes=["qstg"])

            def em3(p=p, hh=hh, ll=ll, r0=r0, c0=c0, n=n, stok=stok):
                comp("vector",
                     lambda e, p=p, n=n:
                     q_stage3(e, wst3[:, p, 0:n], n, None, None),
                     reads=["qstg", f"fst{p}"], writes=["qstg"])
                comp("gpsimd",
                     lambda e, p=p, n=n:
                     q_split_g(e, n, q3h[:, p, 0:n], q3l[:, p, 0:n]),
                     reads=["qstg"], writes=[f"q3{p}"])
                dma(lambda: q3h[:, p, 0:n],
                    lambda: hh[r0:r0 + 128, c0:c0 + n],
                    reads=[f"q3{p}"], writes=[stok])
                dma(lambda: q3l[:, p, 0:n],
                    lambda: ll[r0:r0 + 128, c0:c0 + n],
                    reads=[f"q3{p}"], writes=[stok])
            _p3a.extend((em1, em2, em3))

        _qt_n = [0]
        if _on("P3a"):
            for wh, hh, ll in ((wg_h, wgq_hi_h, wgq_lo_h),
                               (wu_h, wuq_hi_h, wuq_lo_h)):
                for ch in range(22):
                    for s2 in range(2):
                        _p3a_tile(_qt_n[0], wh, hh, ll, ch * 128,
                                  s2 * 512, 512)
                        _qt_n[0] += 1
            for ch in range(8):
                for s2 in range(5):
                    _p3a_tile(_qt_n[0], wd_h, wdq_hi_h, wdq_lo_h,
                              ch * 128, s2 * 512, 512)
                    _qt_n[0] += 1
                _p3a_tile(_qt_n[0], wd_h, wdq_hi_h, wdq_lo_h,
                          ch * 128, 2560, 256)
                _qt_n[0] += 1
        _p3a_i = [0]

        def drain_p3a(k):
            while k > 0 and _p3a_i[0] < len(_p3a):
                _p3a[_p3a_i[0]]()
                _p3a_i[0] += 1
                k -= 1

        for tc in range(8 if _on("P1") else 0):
            for tt in range(4):
                x_split_store(x_all, xa16h, xa16l, (tc * 4 + tt) * 128)
            xfm_load(xa16h, xa16l, tc * 512)
            ts = slice(tc * 512, (tc + 1) * 512)
            for mt in range(2):
                def k_mm(e, mt=mt):
                    for cc in range(8):
                        for i3, (l, r) in enumerate(mm3_list(
                                wk_hi[:, cc, mt * 128:(mt + 1) * 128],
                                wk_lo[:, cc, mt * 128:(mt + 1) * 128],
                                xfm_hi[:, cc, :], xfm_lo[:, cc, :])):
                            e.matmul(ps_q[:], l, r,
                                     start=(cc == 0 and i3 == 0),
                                     stop=(cc == 7 and i3 == 2))
                    return None
                comp("tensor", k_mm, reads=["xfm"], writes=["ps_q"],
                     waits=rd("wk"))

                def k_hi_ev(e, mt=mt, ts=ts):
                    e.activation(K_st[0:64, 2 * mt, ts], ps_q[0:64, :],
                                 ACT.Copy)
                    e.activation(K_st[0:64, 2 * mt + 1, ts], ps_q[64:128, :],
                                 ACT.Copy)
                    return e.activation(ktmp[64:128, :], ps_q[64:128, :],
                                        ACT.Copy)
                comp("scalar", k_hi_ev, writes=[f"ksth{mt}"],
                     waits=rd("ps_q") + wr("kbuf"))

                def k_lo_ev(e, mt=mt, ts=ts):
                    e.tensor_tensor(K_st[64:128, 2 * mt, ts], ps_q[0:64, :],
                                    K_st[0:64, 2 * mt, ts], ALU.subtract)
                    return e.tensor_tensor(K_st[64:128, 2 * mt + 1, ts],
                                           ps_q[64:128, :], ktmp[64:128, :],
                                           ALU.subtract)
                comp("vector", k_lo_ev, reads=["ps_q"], writes=[f"kst{mt}"],
                     waits=rd(f"ksth{mt}") + wr("kbuf"))
            drain_p3a(2)
            for mt in range(2):
                def v_mm(e, mt=mt):
                    for cc in range(8):
                        for i3, (l, r) in enumerate(mm3_list(
                                wv_hi[:, cc, mt * 128:(mt + 1) * 128],
                                wv_lo[:, cc, mt * 128:(mt + 1) * 128],
                                xfm_hi[:, cc, :], xfm_lo[:, cc, :])):
                            e.matmul(ps_q[:], l, r,
                                     start=(cc == 0 and i3 == 0),
                                     stop=(cc == 7 and i3 == 2))
                    return None
                comp("tensor", v_mm, reads=["xfm"], writes=["ps_q"],
                     waits=rd("wv"))

                def v_split(e):
                    e.tensor_copy(v16h[:], ps_q[:])
                    return e.tensor_tensor(v16l[:], ps_q[:], v16h[:],
                                           ALU.subtract)
                comp("vector", v_split, reads=["ps_q"], writes=["v16"])
                dma(lambda mt=mt, ts=ts: v16h[:],
                    lambda mt=mt, ts=ts: v16h_hbm[mt * 128:(mt + 1) * 128, ts],
                    reads=["v16"], writes=["v16s"])
                dma(lambda mt=mt, ts=ts: v16l[:],
                    lambda mt=mt, ts=ts: v16l_hbm[mt * 128:(mt + 1) * 128, ts],
                    reads=["v16"], writes=["v16s"])
        if _on("P1"):
            for kt in range(NKT):
                pp = kt % 2
                for i, (hbm, vdst) in enumerate(((v16h_hbm, Vp_hi),
                                                 (v16l_hbm, Vp_lo))):
                    dma_t(lambda kt=kt, hbm=hbm:
                          hbm[:, kt * 128:(kt + 1) * 128],
                          lambda pp=pp, i=i: vstg[:, pp, i, :],
                          writes=[f"vstg{pp}{i}"],
                          waits=rd("v16s") + wr(f"vstg{pp}{i}"))

                    def vslot(e, kt=kt, pp=pp, i=i, vdst=vdst):
                        return e.tensor_copy(
                            vdst[:, kt, :, 0:64],
                            vstg[:, pp, i, :].rearrange(
                                "p (a b) -> p a b", a=4))
                    comp("gpsimd", vslot, reads=[f"vstg{pp}{i}"],
                         writes=["vp"], waits=rd("vaug"))

        esP1.close()
        P.barrier()
        esP2 = contextlib.ExitStack()
        sbP2 = mk_sb(esP2)
        wqp_hi = sbP2("wqp_hi", [128, 2, 8, 128], BF16)
        wqp_lo = sbP2("wqp_lo", [128, 2, 8, 128], BF16)
        wo_hi = sbP2("wo_hi", [128, 8, D], BF16)
        wo_lo = sbP2("wo_lo", [128, 8, D], BF16)

        def wq_load(pair):
            hA, hB = PAIRS[pair]
            buf = pair % 2
            for hf, hd in ((0, hA), (1, hB)):
                for hbm, dst in ((wqs_hi_h, wqp_hi), (wqs_lo_h, wqp_lo)):
                    dma(lambda hbm=hbm, hd=hd:
                        hbm[:, hd * 64:(hd + 1) * 64]
                        .rearrange("(cc p) d -> p cc d", p=128),
                        lambda dst=dst, buf=buf, hf=hf:
                        dst[:, buf, :, hf * 64:hf * 64 + 64],
                        writes=[f"wqp{buf}"],
                        waits=rd("wscr") + wr(f"wqp{buf}"))
        for hbm, sbuf in ((wos_hi_h, wo_hi), (wos_lo_h, wo_lo)):
            for pr, (hA, hB) in enumerate(PAIRS):
                for hf, hd in ((0, hA), (1, hB)):
                    dma(lambda hbm=hbm, hd=hd: hbm[hd * 64:(hd + 1) * 64, :],
                        lambda sbuf=sbuf, pr=pr, hf=hf:
                            sbuf[hf * 64:(hf + 1) * 64, pr, :],
                        writes=["wo"], waits=rd("wscr"))

        SCL = float(HD) ** -0.5

        def q_proj(pair):
            buf = pair % 2
            sl = pair % 2

            def q_mm(e, buf=buf):
                for cc in range(8):
                    for i3, (l, r) in enumerate(mm3_list(
                            wqp_hi[:, buf, cc, :], wqp_lo[:, buf, cc, :],
                            xfm_hi[:, cc, :], xfm_lo[:, cc, :])):
                        e.matmul(ps_q[:], l, r,
                                 start=(cc == 0 and i3 == 0),
                                 stop=(cc == 7 and i3 == 2))
                return None
            comp("tensor", q_mm, reads=["xfm", f"wqp{buf}"],
                 writes=["ps_q"],
                 waits=wr(f"qphi{sl}", f"qplov{sl}", f"qplo{sl}"))
            if pair + 2 < NP_:
                wq_load(pair + 2)

            def q_hi_ev(e, sl=sl):
                e.activation(Qstk[0:64, sl, 0, :], ps_q[0:64, :], ACT.Copy,
                             scale=SCL)
                e.activation(Qstk[64:128, sl, 0, :], ps_q[0:64, :], ACT.Copy,
                             scale=SCL)
                e.activation(Qstk[64:128, sl, 1, :], ps_q[64:128, :],
                             ACT.Copy, scale=SCL)
                return e.activation(Qstk[0:64, sl, 1, :], ps_q[64:128, :],
                                    ACT.Copy, scale=SCL)
            comp("scalar", q_hi_ev, writes=[f"qphi{sl}"],
                 waits=rd("ps_q"))

            def q_lo_ev(e, sl=sl):
                e.scalar_tensor_tensor(
                    Qlo2[0:64, sl, 0, :], ps_q[0:64, :], SCL,
                    Qstk[0:64, sl, 0, :], ALU.mult, ALU.subtract)
                return e.scalar_tensor_tensor(
                    Qlo2[64:128, sl, 1, :], ps_q[64:128, :], SCL,
                    Qstk[64:128, sl, 1, :], ALU.mult, ALU.subtract)
            comp("vector", q_lo_ev, reads=["ps_q"],
                 writes=[f"qplov{sl}"], waits=rd(f"qphi{sl}"))

            def q_lo_rep(e, sl=sl):
                e.tensor_copy(Qlo2[64:128, sl, 0, :], Qlo2[0:64, sl, 0, :])
                return e.tensor_copy(Qlo2[0:64, sl, 1, :],
                                     Qlo2[64:128, sl, 1, :])
            comp("gpsimd", q_lo_rep, reads=[f"qplov{sl}"],
                 writes=[f"qplo{sl}"])

        PVLAG = 2
        for qc in range(NQC if _on("P2") else 0):
            for tt in range(4):
                x_split_store(x_own, xo16h, xo16l, (qc * 4 + tt) * 128)
            xfm_load(xo16h, xo16l, qc * 512)
            wq_load(0)
            wq_load(1)
            q_proj(0)
            q_proj(1)
            def finalize(fp, qc=qc):
                for hf, (po, pon) in enumerate(((ps_o, "ps_o"),
                                                (ps_ob, "ps_ob"))):
                    def recip(e, po=po):
                        e.reciprocal(rden_r[:], po[64:65, :])
                        e.tensor_copy(rd16h, rden_r[:])
                        return e.tensor_tensor(rd16l, rden_r[:],
                                               rd16h, ALU.subtract)
                    comp("vector", recip, reads=[pon], writes=["rden"])

                    def d_mm(e):
                        e.matmul(ps_rt[0:64, :], ones64[:], rd16h,
                                 start=True, stop=False)
                        return e.matmul(ps_rt[0:64, :], ones64[:],
                                        rd16l, start=False, stop=True)
                    comp("tensor", d_mm, reads=["rden"], writes=["ps_rt"],
                         waits=rd("ones64"))
                    comp("vector",
                         lambda e: e.tensor_copy(D_sb[:], ps_rt[0:64, :]),
                         reads=["ps_rt"], writes=["D_sb"])

                    def n_ev(e, hf=hf, po=po):
                        return e.tensor_tensor(
                            attn_st[hf * 64:hf * 64 + 64, :],
                            po[0:64, :], D_sb[:], ALU.mult)
                    comp("vector", n_ev, reads=[pon, "D_sb"],
                         writes=["atst"])

                def pair_ev(e, fp=fp):
                    e.tensor_copy(attn_hi[:, fp, :], attn_st[:])
                    return e.tensor_tensor(attn_lo[:, fp, :], attn_st[:],
                                           attn_hi[:, fp, :], ALU.subtract)
                comp("gpsimd", pair_ev, reads=["atst"], writes=[f"at{fp}"])
                if DEBUG_TAPS and qc == 0 and fp == 0:
                    dma(lambda: attn_st[:], lambda: dbg_at[:], reads=["at0"])
                    dma(lambda: rden_r[:], lambda: dbg_rd[:],
                        reads=["rden"])

            for pair in range(NP_):
                hA, hB = PAIRS[pair]
                gt = 0 if hA < 8 else 1
                sl = pair % 2

                def emit_pv(kt, gt=gt):
                    par = kt % 3

                    def pv_mm(e, gt=gt, kt=kt, par=par):
                        pvc = ((Vp_hi, p16h, 66), (Vp_lo, p16h, 64),
                               (Vp_hi, p16l, 66))
                        for i3, (vv, pp, w) in enumerate(pvc):
                            e.matmul(ps_o[0:w, :],
                                     vv[:, kt, gt * 2, 0:w],
                                     pp[:, par, 0, :],
                                     start=(kt == 0 and i3 == 0),
                                     stop=(kt == NKT - 1 and i3 == 2),
                                     tile_position=(0, 0))
                            e.matmul(ps_ob[0:w, :],
                                     vv[:, kt, gt * 2 + 1, 0:w],
                                     pp[:, par, 1, :],
                                     start=(kt == 0 and i3 == 0),
                                     stop=(kt == NKT - 1 and i3 == 2),
                                     tile_position=(0, 0))
                        return None
                    extra = rd("vp")
                    if kt == 0:
                        extra += wr("ps_o", "ps_ob")
                    wrts = ["ps_o", "ps_ob"] if kt == NKT - 1 else []
                    comp("tensor", pv_mm,
                         reads=[f"ph{par}", f"plv{par}", f"plg{par}"],
                         writes=wrts, waits=extra)

                for kt in range(NKT):
                    par = kt % 3
                    pb2 = kt % 2
                    bk = kt % 2
                    sA, sAn = (ps_a, "ps_a") if bk == 0 else (ps_b, "ps_b")
                    sB, sBn = (ps_c, "ps_c") if bk == 0 else (ps_d, "ps_d")
                    ks = slice(kt * 128, (kt + 1) * 128)

                    def sc_mm(e, gt=gt, sl=sl, ks=ks, sA=sA, sB=sB):
                        e.matmul(sA[:], K_st[:, gt * 2, ks],
                                 Qstk[:, sl, 0, :], start=True, stop=False)
                        e.matmul(sB[:], K_st[:, gt * 2 + 1, ks],
                                 Qstk[:, sl, 1, :], start=True, stop=False)
                        e.matmul(sA[:], K_st[0:64, gt * 2, ks],
                                 Qlo2[0:64, sl, 0, :], start=False,
                                 stop=True, tile_position=(0, 0))
                        e.matmul(sB[:], K_st[0:64, gt * 2 + 1, ks],
                                 Qlo2[0:64, sl, 1, :], start=False,
                                 stop=True, tile_position=(0, 0))
                        return None
                    extra = []
                    if kt == 0:
                        extra = rd("kst0", "kst1")
                    comp("tensor", sc_mm, reads=[f"qphi{sl}", f"qplo{sl}"],
                         writes=[sAn, sBn], waits=extra)

                    def expo(e, sA=sA, sB=sB, par=par, pb2=pb2):
                        e.activation(p_buf[:, pb2, 0, :], sA[:], ACT.Exp)
                        e.activation(p16h[:, par, 0, :], sA[:], ACT.Exp)
                        e.activation(p_buf[:, pb2, 1, :], sB[:], ACT.Exp)
                        return e.activation(p16h[:, par, 1, :], sB[:],
                                            ACT.Exp)
                    comp("scalar", expo, reads=[sAn, sBn],
                         writes=[f"pbf{pb2}", f"pbg{pb2}", f"ph{par}"])

                    def pspl_sub0(e, par=par, pb2=pb2):
                        return e.tensor_tensor(p16l[:, par, 0, :],
                                               p_buf[:, pb2, 0, :],
                                               p16h[:, par, 0, :],
                                               ALU.subtract)
                    comp("vector", pspl_sub0,
                         reads=[f"pbf{pb2}", f"ph{par}"],
                         writes=[f"plv{par}"])

                    def pspl_sub1(e, par=par, pb2=pb2):
                        return e.tensor_tensor(p16l[:, par, 1, :],
                                               p_buf[:, pb2, 1, :],
                                               p16h[:, par, 1, :],
                                               ALU.subtract)
                    comp("gpsimd", pspl_sub1,
                         reads=[f"pbg{pb2}", f"ph{par}"],
                         writes=[f"plg{par}"])

                    if kt == 1 and pair > 0:
                        finalize(pair - 1)
                        if pair + 1 < NP_:
                            q_proj(pair + 1)
                    if kt % 2 == 1:
                        drain_p3a(1)
                    if kt >= PVLAG:
                        emit_pv(kt - PVLAG)
                for kt in range(NKT - PVLAG, NKT):
                    emit_pv(kt)
                if DEBUG_TAPS and qc == 0 and pair == 0:
                    dma(lambda: Qstk[:, 0, 0, :], lambda: dbg_qph[:],
                        reads=["qphi0"])
                    dma(lambda: Qlo2[:, 0, 0, :], lambda: dbg_qpl[:],
                        reads=["qplo0"])
                    dma(lambda: p16h[:, 1, :, :], lambda: dbg_ph[:],
                        waits=rd("ph1"))
                    dma(lambda: p_buf[:, 1, :, :], lambda: dbg_pf[:],
                        waits=rd("pbf1"))
                    dma(lambda: K_st[:, 0:2, :], lambda: dbg_kh[:],
                        waits=rd("kst0", "kst1"))
            finalize(NP_ - 1)


            for qs in range(4):
                j = qc * 4 + qs
                dma(lambda r0=j * 128: x_own[r0:r0 + 128, :],
                    lambda: x_stage[:], writes=["xs0"])
                for dc in range(2):
                    def wo_mm(e, qs=qs, dc=dc):
                        qsl = slice(qs * 128, (qs + 1) * 128)
                        dsl = slice(dc * 512, (dc + 1) * 512)
                        for pr in range(NP_):
                            for i3, (l, r) in enumerate(mm3_list(
                                    attn_hi[:, pr, qsl], attn_lo[:, pr, qsl],
                                    wo_hi[:, pr, dsl], wo_lo[:, pr, dsl])):
                                e.matmul(ps_q[:], l, r,
                                         start=(pr == 0 and i3 == 0),
                                         stop=(pr == NP_ - 1 and i3 == 2))
                        return None
                    comp("tensor", wo_mm,
                         reads=[f"at{pr}" for pr in range(NP_)],
                         writes=["ps_q"], waits=rd("wo"))

                    def y_ev(e, dc=dc):
                        dsl = slice(dc * 512, (dc + 1) * 512)
                        return e.tensor_tensor(y_sb[:, dsl], x_stage[:, dsl],
                                               ps_q[:], ALU.add)
                    comp("vector", y_ev, reads=["ps_q", "xs0"],
                         writes=[f"ydc{dc}"])

                if DEBUG_TAPS and j == 0:
                    dma(lambda: y_sb[:], lambda: dbg_y[:],
                        waits=rd("ydc0", "ydc1"))
                emit_ln("x1s0")
                dma(lambda: x1_sb[:],
                    lambda r0=j * 128: x1_hbm[r0:r0 + 128, :],
                    reads=["x1s0"], writes=["x1h"])
            if _on("P3b"):
                for rb in range(4):
                    for s2 in range(2):
                        _p3a_tile(_qt_n[0], x1_hbm, x1q16h, x1q16l,
                                  (qc * 4 + rb) * 128, s2 * 512, 512,
                                  stok="x1qs", lw=("x1h",))
                        _qt_n[0] += 1

        drain_p3a(len(_p3a))

        esP2.close()
        esA.close()
        P.barrier()
        P.barrier()
        esP3b = contextlib.ExitStack()
        sbP3b = mk_sb(esP3b)
        x1q_hi = sbP3b("x1q_hi_s", [128, 8, 1024], BF16)
        x1q_lo = sbP3b("x1q_lo_s", [128, 8, 1024], BF16)
        wgq_hi = sbP3b("wgq_hi_s", [128, 2, 8, 512], BF16)
        wgq_lo = sbP3b("wgq_lo_s", [128, 2, 8, 512], BF16)
        wuq_hi = sbP3b("wuq_hi_s", [128, 2, 8, 512], BF16)
        wuq_lo = sbP3b("wuq_lo_s", [128, 2, 8, 512], BF16)
        wdq_hi = sbP3b("wdq_hi_s", [128, 2, 4, 1024], BF16)
        wdq_lo = sbP3b("wdq_lo_s", [128, 2, 4, 1024], BF16)
        dacc = sbP3b("dacc", [128, FTT, D], F32)
        gs_sb = sbP3b("gs_sb", [128, 512], F32)
        h_sb = sbP3b("h_sb", [128, 512], F32)
        hq_hi = sbP3b("hq_hi", [128, 512], BF16)
        hq_lo = sbP3b("hq_lo", [128, 512], BF16)
        hqf_hi = sbP3b("hqf_hi", [128, 4, 1024], BF16)
        hqf_lo = sbP3b("hqf_lo", [128, 4, 1024], BF16)

        def _hq_stages(nn):
            subs = [(s * 512, min(512, nn - s * 512))
                    for s in range((nn + 511) // 512)]
            for b0, w in subs:
                comp("vector",
                     lambda e, b0=b0, w=w:
                     q_stage1(e, h_sb[:, b0:b0 + w], w),
                     reads=["h_sb"], writes=["qstg"])
                comp("vector",
                     lambda e, b0=b0, w=w:
                     q_stage2(e, h_sb[:, b0:b0 + w], w),
                     reads=["qstg"], writes=["qstg"])
                comp("vector",
                     lambda e, b0=b0, w=w:
                     q_stage3(e, h_sb[:, b0:b0 + w], w, None, None),
                     reads=["qstg", "h_sb"], writes=["qstg"])
                comp("gpsimd",
                     lambda e, b0=b0, w=w:
                     q_split_g(e, w, hq_hi[:, b0:b0 + w],
                               hq_lo[:, b0:b0 + w]),
                     reads=["qstg"], writes=["hq"])

        for tc in range(FTC if _on("P3b") else 0):
            for cc in range(8):
                dma_t(lambda tc=tc, cc=cc:
                      x1q16h[tc * 1024:(tc + 1) * 1024,
                             cc * 128:(cc + 1) * 128],
                      lambda cc=cc: x1q_hi[:, cc, :],
                      writes=["x1q"], waits=rd("x1qs") + wr("x1q"))
                dma_t(lambda tc=tc, cc=cc:
                      x1q16l[tc * 1024:(tc + 1) * 1024,
                             cc * 128:(cc + 1) * 128],
                      lambda cc=cc: x1q_lo[:, cc, :],
                      writes=["x1q"])

            dn_pend = []
            for nci, (n0, nn) in enumerate(NCH):
                nnt = nn // 128
                bp = nci % 2
                for cc in range(8):
                    dma_t(lambda n0=n0, nn=nn, cc=cc:
                          wgq_hi_h[n0:n0 + nn, cc * 128:(cc + 1) * 128],
                          lambda nn=nn, cc=cc, bp=bp:
                          wgq_hi[:, bp, cc, 0:nn],
                          writes=[f"wgq{bp}"],
                          waits=rd("scr") + wr(f"wgq{bp}"))
                    dma_t(lambda n0=n0, nn=nn, cc=cc:
                          wgq_lo_h[n0:n0 + nn, cc * 128:(cc + 1) * 128],
                          lambda nn=nn, cc=cc, bp=bp:
                          wgq_lo[:, bp, cc, 0:nn],
                          writes=[f"wgq{bp}"])
                    dma_t(lambda n0=n0, nn=nn, cc=cc:
                          wuq_hi_h[n0:n0 + nn, cc * 128:(cc + 1) * 128],
                          lambda nn=nn, cc=cc, bp=bp:
                          wuq_hi[:, bp, cc, 0:nn],
                          writes=[f"wgq{bp}"])
                    dma_t(lambda n0=n0, nn=nn, cc=cc:
                          wuq_lo_h[n0:n0 + nn, cc * 128:(cc + 1) * 128],
                          lambda nn=nn, cc=cc, bp=bp:
                          wuq_lo[:, bp, cc, 0:nn],
                          writes=[f"wgq{bp}"])
                for nt in range(nnt):
                    dma_t(lambda n0=n0, nt=nt:
                          wdq_hi_h[:, n0 + nt * 128:n0 + (nt + 1) * 128],
                          lambda nt=nt, bp=bp: wdq_hi[:, bp, nt, :],
                          writes=[f"wdq{bp}"],
                          waits=rd("scr") + wr(f"wdq{bp}"))
                    dma_t(lambda n0=n0, nt=nt:
                          wdq_lo_h[:, n0 + nt * 128:n0 + (nt + 1) * 128],
                          lambda nt=nt, bp=bp: wdq_lo[:, bp, nt, :],
                          writes=[f"wdq{bp}"])

                for tt in range(FTT):
                    par = tt % 2
                    g_ps, gn = ((ps_a, "ps_a") if par == 0
                                else (ps_b, "ps_b"))
                    u_ps, un = ((ps_c, "ps_c") if par == 0
                                else (ps_d, "ps_d"))
                    tsl = slice(tt * 128, (tt + 1) * 128)

                    def gu_mm(e, g_ps=g_ps, u_ps=u_ps, tsl=tsl, nn=nn,
                              bp=bp):
                        for cc in range(8):
                            for i3, (l, r) in enumerate(mm3_list(
                                    x1q_hi[:, cc, tsl],
                                    x1q_lo[:, cc, tsl],
                                    wgq_hi[:, bp, cc, 0:nn],
                                    wgq_lo[:, bp, cc, 0:nn])):
                                e.matmul(g_ps[:, 0:nn], l, r,
                                         start=(cc == 0 and i3 == 0),
                                         stop=(cc == 7 and i3 == 2))
                            for i3, (l, r) in enumerate(mm3_list(
                                    x1q_hi[:, cc, tsl],
                                    x1q_lo[:, cc, tsl],
                                    wuq_hi[:, bp, cc, 0:nn],
                                    wuq_lo[:, bp, cc, 0:nn])):
                                e.matmul(u_ps[:, 0:nn], l, r,
                                         start=(cc == 0 and i3 == 0),
                                         stop=(cc == 7 and i3 == 2))
                        return None
                    comp("tensor", gu_mm, reads=[f"wgq{bp}", "x1q"],
                         writes=[gn, un])

                    def silu(e, g_ps=g_ps, nn=nn):
                        return e.activation(gs_sb[:, 0:nn], g_ps[:, 0:nn],
                                            ACT.Silu)
                    comp("scalar", silu, reads=[gn], writes=["gs"])

                    def hmul(e, u_ps=u_ps, nn=nn):
                        return e.tensor_tensor(h_sb[:, 0:nn],
                                               gs_sb[:, 0:nn],
                                               u_ps[:, 0:nn], ALU.mult)
                    comp("vector", hmul, reads=[un, "gs"],
                         writes=["h_sb"])

                    _hq_stages(nn)

                    dma(lambda nn=nn: hq_hi[:, 0:nn],
                        lambda tt=tt, nn=nn:
                        hq16h[tt * 128:(tt + 1) * 128, 0:nn],
                        reads=["hq"], writes=["hqs"])
                    dma(lambda nn=nn: hq_lo[:, 0:nn],
                        lambda tt=tt, nn=nn:
                        hq16l[tt * 128:(tt + 1) * 128, 0:nn],
                        reads=["hq"], writes=["hqs"])

                    for _ in range(2):
                        if dn_pend:
                            dn_pend.pop(0)()
                while dn_pend:
                    dn_pend.pop(0)()

                for nt in range(nnt):
                    dma_t(lambda nt=nt:
                          hq16h[:, nt * 128:(nt + 1) * 128],
                          lambda nt=nt: hqf_hi[:, nt, :],
                          reads=["hqs"], writes=["hqf"])
                    dma_t(lambda nt=nt:
                          hq16l[:, nt * 128:(nt + 1) * 128],
                          lambda nt=nt: hqf_lo[:, nt, :],
                          reads=["hqs"], writes=["hqf"])

                def mk_dn(tt, dc, nci=nci, nnt=nnt, bp=bp):
                    def emit():
                        tsl = slice(tt * 128, (tt + 1) * 128)
                        po, pon = ((ps_o, "ps_o") if dc == 0
                                   else (ps_ob, "ps_ob"))

                        def dn_mm(e, tsl=tsl, dc=dc, nnt=nnt, bp=bp,
                                  po=po):
                            dsl = slice(dc * 512, (dc + 1) * 512)
                            for nt in range(nnt):
                                for i3, (l, r) in enumerate(mm3_list(
                                        hqf_hi[:, nt, tsl],
                                        hqf_lo[:, nt, tsl],
                                        wdq_hi[:, bp, nt, dsl],
                                        wdq_lo[:, bp, nt, dsl])):
                                    e.matmul(po[:], l, r,
                                             start=(nt == 0 and i3 == 0),
                                             stop=(nt == nnt - 1
                                                   and i3 == 2))
                            return None
                        rds = (["hqf", f"wdq{bp}"]
                               if (tt == FTT - 1 and dc == 1) else [])
                        comp("tensor", dn_mm, reads=rds, writes=[pon],
                             waits=rd("hqf", f"wdq{bp}"))

                        def d_add(e, tt=tt, dc=dc, nci=nci, po=po):
                            dsl = slice(dc * 512, (dc + 1) * 512)
                            if nci == 0:
                                return e.tensor_copy(dacc[:, tt, dsl],
                                                     po[:])
                            return e.tensor_tensor(dacc[:, tt, dsl],
                                                   dacc[:, tt, dsl],
                                                   po[:], ALU.add)
                        comp("vector", d_add, reads=[pon],
                             writes=[f"da{tt}_{dc}"])
                    return emit
                dn_pend = [mk_dn(tt, dc)
                           for tt in range(FTT) for dc in range(2)]
            while dn_pend:
                dn_pend.pop(0)()

            for tt in range(FTT):
                j = tc * FTT + tt
                r0 = j * 128
                dma(lambda r0=r0: x1_hbm[r0:r0 + 128, :],
                    lambda: x_stage[:], writes=["xs0"], waits=rd("x1h"))

                def y2(e, tt=tt):
                    return e.tensor_tensor(y_sb[:], x_stage[:],
                                           dacc[:, tt, :], ALU.add)
                comp("vector", y2,
                     reads=["xs0", f"da{tt}_0", f"da{tt}_1"],
                     writes=["ydc0", "ydc1"])

                emit_ln("x1s0")
                dma(lambda: x1_sb[:],
                    lambda r0=r0: out_h[r0:r0 + 128, :], reads=["x1s0"])

        final = [("dm", P.ctr["dm"]), ("da", P.ctr["da"])]
        P.op("sync", lambda e: None, final, noinc=True)

        sems = {s: es.enter_context(nc.semaphore(s)) for s in SEMS}
        blk = es.enter_context(nc.Block())

        @blk.sync
        def _(e):
            P.emit("sync", e, sems)

        @blk.tensor
        def _(e):
            P.emit("tensor", e, sems)

        @blk.vector
        def _(e):
            P.emit("vector", e, sems)

        @blk.scalar
        def _(e):
            P.emit("scalar", e, sems)

        @blk.gpsimd
        def _(e):
            P.emit("gpsimd", e, sems)

        esP3b.close()

    return nc


def host_consts():
    return {}


_PROG = None


def _get_prog():
    global _PROG
    if _PROG is None:
        _PROG = build_program()
    return _PROG


def kernel(L0_attn_x, L0_attn_Wq, L0_attn_Wk, L0_attn_Wv, L0_attn_Wo,
           L0_ffn_w_gate, L0_ffn_w_up, L0_ffn_w_down):
    x = np.ascontiguousarray(np.asarray(L0_attn_x, dtype=np.float32))
    wq = np.ascontiguousarray(np.asarray(L0_attn_Wq, dtype=np.float32))
    wk = np.ascontiguousarray(np.asarray(L0_attn_Wk, dtype=np.float32))
    wv = np.ascontiguousarray(np.asarray(L0_attn_Wv, dtype=np.float32))
    wo = np.ascontiguousarray(np.asarray(L0_attn_Wo, dtype=np.float32))
    wg = np.ascontiguousarray(np.asarray(L0_ffn_w_gate, dtype=np.float32))
    wu = np.ascontiguousarray(np.asarray(L0_ffn_w_up, dtype=np.float32))
    wd = np.ascontiguousarray(np.asarray(L0_ffn_w_down, dtype=np.float32))

    nc = _get_prog()
    in_maps = []
    for c in range(NCORES):
        b, half = c // 2, c % 2
        in_maps.append({
            "x_all": x[b],
            "x_own": np.ascontiguousarray(x[b, half * TQ:(half + 1) * TQ]),
            "wq": wq, "wk": wk, "wv": wv, "wo": wo,
            "wg": wg, "wu": wu, "wd": wd,
        })
    res = run_bass_kernel_spmd(nc, in_maps, core_ids=list(range(NCORES)))
    out = np.empty((B, T, D), dtype=np.float32)
    for c in range(NCORES):
        b, half = c // 2, c % 2
        out[b, half * TQ:(half + 1) * TQ] = np.asarray(res.results[c]["out"])
    return out


# version 19
# speedup vs baseline: 1.0176x; 1.0176x over previous
import contextlib

import numpy as np
import ml_dtypes

import concourse.bass as bass
import concourse.mybir as mybir
from concourse.bass_utils import run_bass_kernel_spmd

F32 = mybir.dt.float32
BF16 = mybir.dt.bfloat16
ALU = mybir.AluOpType
ACT = mybir.ActivationFunctionType
AX = mybir.AxisListType

B, T, D = 4, 4096, 1024
TQ = T // 2
H, HD, G = 16, 64, 4
DK = G * HD
DF = 2816
NCORES = 8
EPS = 1e-5

PAIRS = [(0, 4), (1, 5), (2, 6), (3, 7), (8, 12), (9, 13), (10, 14), (11, 15)]
NP_ = len(PAIRS)
NQC = TQ // 512
NKT = T // 128
FTC = 2
FTT = 8
NCH = [(0, 512), (512, 512), (1024, 512), (1536, 512), (2048, 512), (2560, 256)]

STREAMS = {"sync": "dm", "tensor": "tp", "vector": "vp", "scalar": "sp",
           "gpsimd": "gp"}
SEMS = ("dm", "da", "tp", "vp", "sp", "gp", "hzv", "hzs", "hzt", "hzg")

DEBUG_TAPS = False
DEBUG_OUT = False
STOP_AFTER = None
_PHASES = ["P0", "P1", "P2", "P3a", "P3b"]


def _on(ph):
    return STOP_AFTER is None or _PHASES.index(ph) <= _PHASES.index(STOP_AFTER)


class Plan:
    def __init__(self):
        self.ops = []
        self.ctr = {s: 0 for s in SEMS}
        self.pending = {}

    def barrier(self):
        snap = [(s, v) for s, v in self.ctr.items() if v > 0]
        for eng in STREAMS:
            self.pending[eng] = list(snap)

    def op(self, engine, body, waits, dma=False, noinc=False):
        sem = STREAMS[engine]
        dma_sem = None
        if dma:
            dma_sem = "da" if engine == "scalar" else "dm"
        waits = list(waits) + self.pending.pop(engine, [])
        w = []
        for sv in waits:
            if sv is None:
                continue
            s, v = sv
            if v <= 0:
                continue
            if s not in ("dm", "da") and s == sem:
                continue
            w.append((s, v))
        self.ops.append((engine, w, body, dma_sem, noinc))
        if noinc:
            return None
        if dma_sem:
            self.ctr[dma_sem] += 16
            return (dma_sem, self.ctr[dma_sem])
        self.ctr[sem] += 1
        return (sem, self.ctr[sem])

    def emit(self, engine, eng, sems):
        hz = {"vector": "hzv", "scalar": "hzs", "tensor": "hzt",
              "gpsimd": "hzg"}.get(engine, "hzv")
        rec = _Rec(eng, sems[hz], sems[STREAMS[engine]])
        hw = {}
        for (oe, waits, body, dma_sem, noinc) in self.ops:
            if oe != engine:
                continue
            for (s, v) in waits:
                if hw.get(s, -1) >= v:
                    continue
                eng.wait_ge(sems[s], v)
                hw[s] = v
            rec.last = None
            body(rec)
            if noinc:
                continue
            if dma_sem:
                rec.last.then_inc(sems[dma_sem], 16)
            elif rec.last is not None:
                rec.last.then_inc(sems[STREAMS[engine]], 1)
                rec.scount += 1
                rec.inced = rec.last
            else:
                eng.nop().then_inc(sems[STREAMS[engine]], 1)
                rec.scount += 1


class _Rec:

    _OPS = frozenset((
        "tensor_copy", "tensor_tensor", "tensor_add", "tensor_sub",
        "tensor_mul", "tensor_max", "tensor_scalar", "tensor_single_scalar",
        "tensor_scalar_add", "tensor_scalar_sub", "tensor_scalar_mul",
        "tensor_scalar_max", "tensor_scalar_min", "tensor_relu",
        "tensor_reduce", "reduce_max", "reduce_sum", "scalar_tensor_tensor",
        "reciprocal", "memset", "memzero", "activation", "sign", "square",
        "sqrt", "mul", "copy", "add", "bn_stats", "bn_aggr", "select",
        "transpose", "iota",
    ))

    def __init__(self, eng, hz_sem, stream_sem):
        self._e = eng
        self._hz = hz_sem
        self._hzc = 0
        self._ssem = stream_sem
        self.scount = 0
        self.inced = None
        self.last = None
        self.lastf = None
        self._dirty_w = set()
        self._dirty_r = set()

    @staticmethod
    def _apname(a):
        t = getattr(a, "tensor", None)
        if t is None:
            return None
        n = getattr(t, "name", None)
        return n if isinstance(n, str) else None

    def _fence(self):
        if self.lastf is not None:
            if self.lastf is self.inced:
                self._e.wait_ge(self._ssem, self.scount)
            else:
                self._hzc += 1
                self.lastf.then_inc(self._hz, 1)
                self._e.wait_ge(self._hz, self._hzc)
        self._dirty_w.clear()
        self._dirty_r.clear()

    def __getattr__(self, k):
        a = getattr(self._e, k)
        if not callable(a):
            return a
        if k not in self._OPS:
            def w0(*args, **kw):
                r = a(*args, **kw)
                if hasattr(r, "then_inc"):
                    self.last = r
                return r
            return w0

        def w(*args, **kw):
            outs = set()
            ins = set()
            for i, ar in enumerate(args):
                n = self._apname(ar)
                if n:
                    (outs if i == 0 else ins).add(n)
            for key, ar in kw.items():
                n = self._apname(ar)
                if n:
                    (outs if key == "accum_out" else ins).add(n)
            if (ins & self._dirty_w) or (outs & (self._dirty_w
                                                | self._dirty_r)):
                self._fence()
            r = a(*args, **kw)
            if hasattr(r, "then_inc"):
                self.last = r
                self.lastf = r
            self._dirty_w |= outs
            self._dirty_r |= ins
            return r
        return w


class Tok:
    __slots__ = ("w", "r")

    def __init__(self):
        self.w = None
        self.r = None


def mm3_list(hi_l, lo_l, hi_r, lo_r):
    return ((hi_l, hi_r), (hi_l, lo_r), (lo_l, hi_r))


def build_program():
    nc = bass.Bass("TRN2", target_bir_lowering=False,
                   detect_race_conditions=False)

    x_all = nc.dram_tensor("x_all", [T, D], F32, kind="ExternalInput")
    x_own = nc.dram_tensor("x_own", [TQ, D], F32, kind="ExternalInput")
    wq_h = nc.dram_tensor("wq", [D, D], F32, kind="ExternalInput")
    wk_h = nc.dram_tensor("wk", [D, DK], F32, kind="ExternalInput")
    wv_h = nc.dram_tensor("wv", [D, DK], F32, kind="ExternalInput")
    wo_h = nc.dram_tensor("wo", [D, D], F32, kind="ExternalInput")
    wg_h = nc.dram_tensor("wg", [DF, D], F32, kind="ExternalInput")
    wu_h = nc.dram_tensor("wu", [DF, D], F32, kind="ExternalInput")
    wd_h = nc.dram_tensor("wd", [D, DF], F32, kind="ExternalInput")
    out_h = nc.dram_tensor("out", [TQ, D], F32, kind="ExternalOutput")
    _k = dict(kind="ExternalOutput") if DEBUG_OUT else {}

    def _k2():
        return dict(kind="ExternalOutput")

    x1_hbm = nc.dram_tensor("x1_hbm", [TQ, D], F32, **_k)
    xa16h = nc.dram_tensor("xa16h", [T, D], BF16, **_k)
    xa16l = nc.dram_tensor("xa16l", [T, D], BF16)
    xo16h = nc.dram_tensor("xo16h", [TQ, D], BF16)
    xo16l = nc.dram_tensor("xo16l", [TQ, D], BF16)
    v16h_hbm = nc.dram_tensor("v16h_hbm", [DK, T], BF16, **_k)
    v16l_hbm = nc.dram_tensor("v16l_hbm", [DK, T], BF16)
    wqs_hi_h = nc.dram_tensor("wqs_hi", [D, D], BF16, **_k)
    wqs_lo_h = nc.dram_tensor("wqs_lo", [D, D], BF16)
    wos_hi_h = nc.dram_tensor("wos_hi", [D, D], BF16)
    wos_lo_h = nc.dram_tensor("wos_lo", [D, D], BF16)
    wgq_hi_h = nc.dram_tensor("wgq_hi", [DF, D], BF16, **_k)
    wgq_lo_h = nc.dram_tensor("wgq_lo", [DF, D], BF16)
    wuq_hi_h = nc.dram_tensor("wuq_hi", [DF, D], BF16)
    wuq_lo_h = nc.dram_tensor("wuq_lo", [DF, D], BF16)
    wdq_hi_h = nc.dram_tensor("wdq_hi", [D, DF], BF16)
    wdq_lo_h = nc.dram_tensor("wdq_lo", [D, DF], BF16)
    x1q16h = nc.dram_tensor("x1q16h", [TQ, D], BF16)
    x1q16l = nc.dram_tensor("x1q16l", [TQ, D], BF16)
    hq16h = nc.dram_tensor("hq16h", [1024, 512], BF16)
    hq16l = nc.dram_tensor("hq16l", [1024, 512], BF16)
    if DEBUG_TAPS:
        dbg_qph = nc.dram_tensor("dbg_qph", [128, 512], BF16, **_k2())
        dbg_qpl = nc.dram_tensor("dbg_qpl", [128, 512], BF16, **_k2())
        dbg_ph = nc.dram_tensor("dbg_ph", [128, 2, 512], BF16, **_k2())
        dbg_pf = nc.dram_tensor("dbg_pf", [128, 2, 512], F32, **_k2())
        dbg_at = nc.dram_tensor("dbg_at", [128, 512], F32, **_k2())
        dbg_rd = nc.dram_tensor("dbg_rd", [1, 512], F32, **_k2())
        dbg_y = nc.dram_tensor("dbg_y", [128, 1024], F32, **_k2())
        dbg_kh = nc.dram_tensor("dbg_kh", [128, 2, 4096], BF16, **_k2())

    es = contextlib.ExitStack()

    def mk_sb(stack):
        def f(name, shape, dtype):
            return stack.enter_context(nc.sbuf_tensor(name, shape, dtype))
        return f

    with es:
        sb = mk_sb(es)
        ones64 = sb("ones64", [1, 64], BF16)
        qA = sb("qA", [128, 512], F32)
        qB = sb("qB", [128, 512], F32)
        qC = sb("qC", [128, 512], F32)
        qsc = sb("qsc", [128, 32], F32)
        qr2 = sb("qr2", [128, 32], F32)
        x_stage = sb("x_stage", [128, D], F32)
        y_sb = sb("y_sb", [128, D], F32)
        x1_sb = sb("x1_sb", [128, D], F32)
        sq_scr = sb("sq_scr", [128, D], BF16)
        ln1 = sb("ln1", [128, 1], F32)
        ln2 = sb("ln2", [128, 1], F32)
        ln3 = sb("ln3", [128, 1], F32)
        ln4 = sb("ln4", [128, 1], F32)
        ln5 = sb("ln5", [128, 1], F32)

        ps_a = es.enter_context(nc.psum_tensor("ps_a", [128, 512], F32))
        ps_b = es.enter_context(nc.psum_tensor("ps_b", [128, 512], F32))
        ps_c = es.enter_context(nc.psum_tensor("ps_c", [128, 512], F32))
        ps_d = es.enter_context(nc.psum_tensor("ps_d", [128, 512], F32))
        ps_o = es.enter_context(nc.psum_tensor("ps_o", [128, 512], F32))
        ps_ob = es.enter_context(nc.psum_tensor("ps_ob", [128, 512], F32))
        ps_q = es.enter_context(nc.psum_tensor("ps_q", [128, 512], F32))
        ps_rt = es.enter_context(nc.psum_tensor("ps_rt", [128, 512], F32))

        P = Plan()
        toks = {}

        def tok(name):
            if name not in toks:
                toks[name] = Tok()
            return toks[name]

        def rd(*names):
            return [tok(n).w for n in names]

        def wr(*names):
            return [tok(n).r for n in names]

        def _mark(sv, reads, writes):
            for n in writes:
                tok(n).w = sv
            for n in reads:
                tok(n).r = sv
            return sv

        def dma(src_fn, dst_fn, reads=(), writes=(), waits=()):
            w = list(waits) + rd(*reads) + wr(*writes)
            sv = P.op("sync", lambda e, a=src_fn, b=dst_fn:
                      e.dma_start(out=b(), in_=a()), w, dma=True)
            return _mark(sv, reads, writes)

        def dma_t(src_fn, dst_fn, reads=(), writes=(), waits=()):
            w = list(waits) + rd(*reads) + wr(*writes)
            sv = P.op("scalar", lambda e, a=src_fn, b=dst_fn:
                      e.dma_start_transpose(b(), a()), w, dma=True)
            return _mark(sv, reads, writes)

        def comp(engine, body, reads=(), writes=(), waits=()):
            w = list(waits) + rd(*reads) + wr(*writes)
            sv = P.op(engine, body, w)
            return _mark(sv, reads, writes)

        def emit_ln(store_tok):
            comp("vector",
                 lambda e: e.tensor_reduce(ln1[:], y_sb[:], AX.X, ALU.add),
                 writes=["ln1"], waits=rd("ydc0", "ydc1"))
            comp("scalar",
                 lambda e: e.activation(sq_scr[:], y_sb[:], ACT.Square,
                                        accum_out=ln2[:]),
                 writes=["ln2"], waits=rd("ydc0", "ydc1"))

            def ln_var(e):
                e.tensor_scalar(ln3[:], ln1[:], 1.0 / D, None, ALU.mult)
                e.tensor_tensor(ln4[:], ln3[:], ln3[:], ALU.mult)
                e.scalar_tensor_tensor(ln4[:], ln2[:], 1.0 / D, ln4[:],
                                       ALU.mult, ALU.subtract)
                return e.tensor_scalar(ln4[:], ln4[:], EPS, None, ALU.add)
            comp("vector", ln_var, reads=["ln1", "ln2"], writes=["ln3", "ln4"])
            comp("scalar", lambda e: e.activation(ln5[:], ln4[:], ACT.Sqrt),
                 reads=["ln4"], writes=["ln5"])

            def ln_fin(e):
                e.reciprocal(ln4[:], ln5[:])
                return e.scalar_tensor_tensor(ln3[:], ln3[:], -1.0, ln4[:],
                                              ALU.mult, ALU.mult)
            comp("vector", ln_fin, reads=["ln5"], writes=["lnf"])

            def ln_app(e):
                return e.activation(x1_sb[:], y_sb[:], ACT.Identity,
                                    bias=ln3[:], scale=ln4[:])
            sv = comp("scalar", ln_app, reads=["lnf"], writes=[store_tok])
            for n in ("ydc0", "ydc1", "ln3", "ln4"):
                tok(n).r = sv

        def x_split_store(src_h, h16, l16, row0):
            dma(lambda: src_h[row0:row0 + 128, :], lambda: x_stage[:],
                writes=["xs0"])

            def split(e):
                e.tensor_copy(xt_hi[:], x_stage[:])
                return e.tensor_tensor(xt_lo[:], x_stage[:], xt_hi[:],
                                       ALU.subtract)
            comp("vector", split, reads=["xs0"], writes=["xt0"])
            dma(lambda: xt_hi[:], lambda: h16[row0:row0 + 128, :],
                reads=["xt0"], writes=["x16s"])
            dma(lambda: xt_lo[:], lambda: l16[row0:row0 + 128, :],
                reads=["xt0"], writes=["x16s"])

        def xfm_load(h16, l16, row0):
            for cc in range(8):
                dma_t(lambda h16=h16, row0=row0, cc=cc:
                      h16[row0:row0 + 512, cc * 128:(cc + 1) * 128],
                      lambda cc=cc: xfm_hi[:, cc, :],
                      writes=["xfm"], waits=rd("x16s"))
                dma_t(lambda l16=l16, row0=row0, cc=cc:
                      l16[row0:row0 + 512, cc * 128:(cc + 1) * 128],
                      lambda cc=cc: xfm_lo[:, cc, :],
                      writes=["xfm"], waits=rd("x16s"))

        esA = contextlib.ExitStack()
        sbA = mk_sb(esA)
        K_st = sbA("K_st", [128, 4, T], BF16)
        wst3 = sbA("wst3", [128, 2, 512], F32)
        q3h = sbA("q3h", [128, 2, 512], BF16)
        q3l = sbA("q3l", [128, 2, 512], BF16)
        xt_hi = sbA("xt_hi", [128, D], BF16)
        xt_lo = sbA("xt_lo", [128, D], BF16)
        xfm_hi = sbA("xfm_hi", [128, 8, 512], BF16)
        xfm_lo = sbA("xfm_lo", [128, 8, 512], BF16)
        ktmp = sbA("ktmp", [128, 512], BF16)
        Vp_hi = sbA("Vp_hi", [128, NKT, 4, 66], BF16)
        Vp_lo = sbA("Vp_lo", [128, NKT, 4, 64], BF16)
        Qstk = sbA("Qstk", [128, 2, 2, 512], BF16)
        Qlo2 = sbA("Qlo2", [128, 2, 2, 512], BF16)
        p_buf = sbA("p_buf", [128, 2, 2, 512], F32)
        p16h = sbA("p16h", [128, 3, 2, 512], BF16)
        p16l = sbA("p16l", [128, 3, 2, 512], BF16)
        attn_hi = sbA("attn_hi", [128, NP_, 512], BF16)
        attn_lo = sbA("attn_lo", [128, NP_, 512], BF16)
        attn_st = sbA("attn_st", [128, 512], F32)
        rden_r = sbA("rden_r", [1, 512], F32)

        D_sb = sbA("D_sb", [64, 512], F32)
        rd16h = D_sb[:].bitcast(BF16)[0:1, 0:512]
        rd16l = D_sb[:].bitcast(BF16)[0:1, 512:1024]
        vstg = sbA("vstg", [128, 2, 2, 256], BF16)

        esP1 = contextlib.ExitStack()
        sbP1 = mk_sb(esP1)
        wstage = sbP1("wstage", [128, 4096], F32)
        wk_hi = sbP1("wk_hi", [128, 8, DK], BF16)
        wk_lo = sbP1("wk_lo", [128, 8, DK], BF16)
        wv_hi = sbP1("wv_hi", [128, 8, DK], BF16)
        wv_lo = sbP1("wv_lo", [128, 8, DK], BF16)
        v16h = sbP1("v16h", [128, 512], BF16)
        v16l = sbP1("v16l", [128, 512], BF16)

        comp("vector", lambda e: e.memset(ones64[:], 1.0), writes=["ones64"])

        def vaug_init(e):
            e.memset(Vp_hi[:, :, :, 64:66], 0.0)
            return e.memset(Vp_hi[:, :, :, 64:65], 1.0)
        comp("vector", vaug_init, writes=["vaug"])

        for wh, shh, slh in ((wq_h, wqs_hi_h, wqs_lo_h),
                             (wo_h, wos_hi_h, wos_lo_h)):
            for hv in range(2):
                dma(lambda wh=wh, hv=hv:
                    wh[hv * 512:(hv + 1) * 512, :]
                    .rearrange("(cc p) d -> p cc d", p=128),
                    lambda: wstage[:].rearrange("p (cc d) -> p cc d", cc=4),
                    writes=["wstage"])

                def wsp(e):
                    flat = K_st[:].rearrange("p a b -> p (a b)")
                    khf = flat[:, 0:4096]
                    klf = flat[:, 8192:12288]
                    e.tensor_copy(khf, wstage[:])
                    return e.tensor_tensor(klf, wstage[:], khf,
                                           ALU.subtract)
                comp("vector", wsp, reads=["wstage"], writes=["kbuf"])

                dma(lambda: K_st[:].rearrange("p a b -> p (a b)")
                    [:, 0:4096].rearrange("p (c d) -> p c d", c=4),
                    lambda shh=shh, hv=hv:
                    shh[hv * 512:(hv + 1) * 512, :]
                    .rearrange("(cc p) d -> p cc d", p=128),
                    reads=["kbuf"], writes=["wscr"])
                dma(lambda: K_st[:].rearrange("p a b -> p (a b)")
                    [:, 8192:12288].rearrange("p (c d) -> p c d", c=4),
                    lambda slh=slh, hv=hv:
                    slh[hv * 512:(hv + 1) * 512, :]
                    .rearrange("(cc p) d -> p cc d", p=128),
                    reads=["kbuf"], writes=["wscr"])

        for wname, wh, whi, wlo in (("wk", wk_h, wk_hi, wk_lo),
                                    ("wv", wv_h, wv_hi, wv_lo)):
            st3 = wstage[:, 0:8 * DK].rearrange("p (cc d) -> p cc d", d=DK)
            dma(lambda wh=wh: wh[:].rearrange("(cc p) d -> p cc d", p=128),
                lambda st3=st3: st3, writes=["wstage"])

            def wsplit(e, whi=whi, wlo=wlo):
                flat = wstage[:, 0:8 * DK]
                hi_f = whi[:].rearrange("p c d -> p (c d)")
                lo_f = wlo[:].rearrange("p c d -> p (c d)")
                e.tensor_copy(hi_f, flat)
                return e.tensor_tensor(lo_f, flat, hi_f, ALU.subtract)
            comp("vector", wsplit, reads=["wstage"], writes=[wname])

        def q_stage1(e, src, n):
            nb = n // 16
            sc, r2 = qsc[:, 0:nb], qr2[:, 0:nb]
            src3 = src.rearrange("p (nb b) -> p nb b", b=16)
            e.tensor_reduce(sc, src3, AX.X, ALU.max,
                            apply_absolute_value=True)
            e.reciprocal(r2, sc)
            e.tensor_scalar(r2, r2, 6.0, None, ALU.mult)
            e.reciprocal(sc, r2)
            e.tensor_scalar(sc, sc, 1e-12, None, ALU.max)
            e.reciprocal(r2, sc)
            return e.tensor_scalar(r2, r2, 2.0, None, ALU.mult)

        def q_stage2(e, src, n):
            nb = n // 16
            A_, B_ = qA[:, 0:n], qB[:, 0:n]
            r2bc = qr2[:, 0:nb].unsqueeze(2).broadcast_to([128, nb, 16])
            A3 = A_.rearrange("p (nb b) -> p nb b", b=16)
            e.scalar_tensor_tensor(A_, src, -1.0, src, ALU.mult, ALU.max)
            e.tensor_tensor(A3, A3, r2bc, ALU.mult)
            e.tensor_scalar(B_, A_, 0.5, None, ALU.is_gt)
            e.scalar_tensor_tensor(B_, A_, 1.5, B_, ALU.is_gt, ALU.add)
            e.scalar_tensor_tensor(B_, A_, 2.5, B_, ALU.is_gt, ALU.add)
            return e.scalar_tensor_tensor(B_, A_, 3.5, B_, ALU.is_gt,
                                          ALU.add)

        def q_stage3(e, src, n, hi_out, lo_out):
            nb = n // 16
            A_, B_, C_ = qA[:, 0:n], qB[:, 0:n], qC[:, 0:n]
            scbc = qsc[:, 0:nb].unsqueeze(2).broadcast_to([128, nb, 16])
            A3 = A_.rearrange("p (nb b) -> p nb b", b=16)
            e.tensor_scalar(C_, A_, 5.0, None, ALU.is_gt)
            e.scalar_tensor_tensor(C_, A_, 7.0, C_, ALU.is_gt, ALU.add)
            e.scalar_tensor_tensor(B_, C_, 2.0, B_, ALU.mult, ALU.add)
            e.tensor_scalar(C_, A_, 10.0, 4.0, ALU.is_gt, ALU.mult)
            e.tensor_tensor(B_, B_, C_, ALU.add)
            e.tensor_scalar(C_, src, 0.0, None, ALU.is_ge)
            e.scalar_tensor_tensor(A_, C_, -0.5, B_, ALU.add, ALU.mult)
            return e.tensor_tensor(A3, A3, scbc, ALU.mult)

        def q_split_g(e, n, hi_out, lo_out):
            A_ = qA[:, 0:n]
            e.tensor_copy(hi_out, A_)
            return e.tensor_tensor(lo_out, A_, hi_out, ALU.subtract)

        def quant_seq(e, src, n, hi_out, lo_out):
            q_stage1(e, src, n)
            q_stage2(e, src, n)
            q_stage3(e, src, n, hi_out, lo_out)
            return q_split_g(e, n, hi_out, lo_out)

        _p3a = []

        def _p3a_tile(idx, wh, hh, ll, r0, c0, n, stok="scr", lw=()):
            p = idx % 2

            def em1(p=p, wh=wh, r0=r0, c0=c0, n=n, lw=tuple(lw)):
                dma(lambda: wh[r0:r0 + 128, c0:c0 + n],
                    lambda: wst3[:, p, 0:n], writes=[f"fst{p}"],
                    waits=rd(*lw))
                comp("vector",
                     lambda e, p=p, n=n: q_stage1(e, wst3[:, p, 0:n], n),
                     reads=[f"fst{p}"], writes=["qstg"])

            def em2(p=p, n=n):
                comp("vector",
                     lambda e, p=p, n=n: q_stage2(e, wst3[:, p, 0:n], n),
                     reads=["qstg"], writes=["qstg"])

            def em3(p=p, hh=hh, ll=ll, r0=r0, c0=c0, n=n, stok=stok):
                comp("vector",
                     lambda e, p=p, n=n:
                     q_stage3(e, wst3[:, p, 0:n], n, None, None),
                     reads=["qstg", f"fst{p}"], writes=["qstg"])
                comp("gpsimd",
                     lambda e, p=p, n=n:
                     q_split_g(e, n, q3h[:, p, 0:n], q3l[:, p, 0:n]),
                     reads=["qstg"], writes=[f"q3{p}"])
                dma(lambda: q3h[:, p, 0:n],
                    lambda: hh[r0:r0 + 128, c0:c0 + n],
                    reads=[f"q3{p}"], writes=[stok])
                dma(lambda: q3l[:, p, 0:n],
                    lambda: ll[r0:r0 + 128, c0:c0 + n],
                    reads=[f"q3{p}"], writes=[stok])
            _p3a.extend((em1, em2, em3))

        _qt_n = [0]
        if _on("P3a"):
            for wh, hh, ll in ((wg_h, wgq_hi_h, wgq_lo_h),
                               (wu_h, wuq_hi_h, wuq_lo_h)):
                for ch in range(22):
                    for s2 in range(2):
                        _p3a_tile(_qt_n[0], wh, hh, ll, ch * 128,
                                  s2 * 512, 512)
                        _qt_n[0] += 1
            for ch in range(8):
                for s2 in range(5):
                    _p3a_tile(_qt_n[0], wd_h, wdq_hi_h, wdq_lo_h,
                              ch * 128, s2 * 512, 512)
                    _qt_n[0] += 1
                _p3a_tile(_qt_n[0], wd_h, wdq_hi_h, wdq_lo_h,
                          ch * 128, 2560, 256)
                _qt_n[0] += 1
        _p3a_i = [0]

        def drain_p3a(k):
            while k > 0 and _p3a_i[0] < len(_p3a):
                _p3a[_p3a_i[0]]()
                _p3a_i[0] += 1
                k -= 1

        for tc in range(8 if _on("P1") else 0):
            for tt in range(4):
                x_split_store(x_all, xa16h, xa16l, (tc * 4 + tt) * 128)
            xfm_load(xa16h, xa16l, tc * 512)
            ts = slice(tc * 512, (tc + 1) * 512)
            for mt in range(2):
                def k_mm(e, mt=mt):
                    for cc in range(8):
                        for i3, (l, r) in enumerate(mm3_list(
                                wk_hi[:, cc, mt * 128:(mt + 1) * 128],
                                wk_lo[:, cc, mt * 128:(mt + 1) * 128],
                                xfm_hi[:, cc, :], xfm_lo[:, cc, :])):
                            e.matmul(ps_q[:], l, r,
                                     start=(cc == 0 and i3 == 0),
                                     stop=(cc == 7 and i3 == 2))
                    return None
                comp("tensor", k_mm, reads=["xfm"], writes=["ps_q"],
                     waits=rd("wk"))

                def k_hi_ev(e, mt=mt, ts=ts):
                    e.activation(K_st[0:64, 2 * mt, ts], ps_q[0:64, :],
                                 ACT.Copy)
                    e.activation(K_st[0:64, 2 * mt + 1, ts], ps_q[64:128, :],
                                 ACT.Copy)
                    return e.activation(ktmp[64:128, :], ps_q[64:128, :],
                                        ACT.Copy)
                comp("scalar", k_hi_ev, writes=[f"ksth{mt}"],
                     waits=rd("ps_q") + wr("kbuf"))

                def k_lo_ev(e, mt=mt, ts=ts):
                    e.tensor_tensor(K_st[64:128, 2 * mt, ts], ps_q[0:64, :],
                                    K_st[0:64, 2 * mt, ts], ALU.subtract)
                    return e.tensor_tensor(K_st[64:128, 2 * mt + 1, ts],
                                           ps_q[64:128, :], ktmp[64:128, :],
                                           ALU.subtract)
                comp("vector", k_lo_ev, reads=["ps_q"], writes=[f"kst{mt}"],
                     waits=rd(f"ksth{mt}") + wr("kbuf"))
            drain_p3a(2)
            for mt in range(2):
                def v_mm(e, mt=mt):
                    for cc in range(8):
                        for i3, (l, r) in enumerate(mm3_list(
                                wv_hi[:, cc, mt * 128:(mt + 1) * 128],
                                wv_lo[:, cc, mt * 128:(mt + 1) * 128],
                                xfm_hi[:, cc, :], xfm_lo[:, cc, :])):
                            e.matmul(ps_q[:], l, r,
                                     start=(cc == 0 and i3 == 0),
                                     stop=(cc == 7 and i3 == 2))
                    return None
                comp("tensor", v_mm, reads=["xfm"], writes=["ps_q"],
                     waits=rd("wv"))

                def v_split(e):
                    e.tensor_copy(v16h[:], ps_q[:])
                    return e.tensor_tensor(v16l[:], ps_q[:], v16h[:],
                                           ALU.subtract)
                comp("vector", v_split, reads=["ps_q"], writes=["v16"])
                dma(lambda mt=mt, ts=ts: v16h[:],
                    lambda mt=mt, ts=ts: v16h_hbm[mt * 128:(mt + 1) * 128, ts],
                    reads=["v16"], writes=["v16s"])
                dma(lambda mt=mt, ts=ts: v16l[:],
                    lambda mt=mt, ts=ts: v16l_hbm[mt * 128:(mt + 1) * 128, ts],
                    reads=["v16"], writes=["v16s"])
        if _on("P1"):
            for kt in range(NKT):
                pp = kt % 2
                for i, (hbm, vdst) in enumerate(((v16h_hbm, Vp_hi),
                                                 (v16l_hbm, Vp_lo))):
                    dma_t(lambda kt=kt, hbm=hbm:
                          hbm[:, kt * 128:(kt + 1) * 128],
                          lambda pp=pp, i=i: vstg[:, pp, i, :],
                          writes=[f"vstg{pp}{i}"],
                          waits=rd("v16s") + wr(f"vstg{pp}{i}"))

                    def vslot(e, kt=kt, pp=pp, i=i, vdst=vdst):
                        return e.tensor_copy(
                            vdst[:, kt, :, 0:64],
                            vstg[:, pp, i, :].rearrange(
                                "p (a b) -> p a b", a=4))
                    comp("gpsimd", vslot, reads=[f"vstg{pp}{i}"],
                         writes=["vp"], waits=rd("vaug"))

        esP1.close()
        P.barrier()
        esP2 = contextlib.ExitStack()
        sbP2 = mk_sb(esP2)
        wqp_hi = sbP2("wqp_hi", [128, 2, 8, 128], BF16)
        wqp_lo = sbP2("wqp_lo", [128, 2, 8, 128], BF16)
        wo_hi = sbP2("wo_hi", [128, 8, D], BF16)
        wo_lo = sbP2("wo_lo", [128, 8, D], BF16)

        def wq_load(pair):
            hA, hB = PAIRS[pair]
            buf = pair % 2
            for hf, hd in ((0, hA), (1, hB)):
                for hbm, dst in ((wqs_hi_h, wqp_hi), (wqs_lo_h, wqp_lo)):
                    dma(lambda hbm=hbm, hd=hd:
                        hbm[:, hd * 64:(hd + 1) * 64]
                        .rearrange("(cc p) d -> p cc d", p=128),
                        lambda dst=dst, buf=buf, hf=hf:
                        dst[:, buf, :, hf * 64:hf * 64 + 64],
                        writes=[f"wqp{buf}"],
                        waits=rd("wscr") + wr(f"wqp{buf}"))
        for hbm, sbuf in ((wos_hi_h, wo_hi), (wos_lo_h, wo_lo)):
            for pr, (hA, hB) in enumerate(PAIRS):
                for hf, hd in ((0, hA), (1, hB)):
                    dma(lambda hbm=hbm, hd=hd: hbm[hd * 64:(hd + 1) * 64, :],
                        lambda sbuf=sbuf, pr=pr, hf=hf:
                            sbuf[hf * 64:(hf + 1) * 64, pr, :],
                        writes=["wo"], waits=rd("wscr"))

        SCL = float(HD) ** -0.5

        def q_proj(pair):
            buf = pair % 2
            sl = pair % 2

            def q_mm(e, buf=buf):
                for cc in range(8):
                    for i3, (l, r) in enumerate(mm3_list(
                            wqp_hi[:, buf, cc, :], wqp_lo[:, buf, cc, :],
                            xfm_hi[:, cc, :], xfm_lo[:, cc, :])):
                        e.matmul(ps_q[:], l, r,
                                 start=(cc == 0 and i3 == 0),
                                 stop=(cc == 7 and i3 == 2))
                return None
            comp("tensor", q_mm, reads=["xfm", f"wqp{buf}"],
                 writes=["ps_q"],
                 waits=wr(f"qphi{sl}", f"qplov{sl}", f"qplo{sl}"))
            if pair + 2 < NP_:
                wq_load(pair + 2)

            def q_hi_ev(e, sl=sl):
                e.activation(Qstk[0:64, sl, 0, :], ps_q[0:64, :], ACT.Copy,
                             scale=SCL)
                e.activation(Qstk[64:128, sl, 0, :], ps_q[0:64, :], ACT.Copy,
                             scale=SCL)
                e.activation(Qstk[64:128, sl, 1, :], ps_q[64:128, :],
                             ACT.Copy, scale=SCL)
                return e.activation(Qstk[0:64, sl, 1, :], ps_q[64:128, :],
                                    ACT.Copy, scale=SCL)
            comp("scalar", q_hi_ev, writes=[f"qphi{sl}"],
                 waits=rd("ps_q"))

            def q_lo_ev(e, sl=sl):
                e.scalar_tensor_tensor(
                    Qlo2[0:64, sl, 0, :], ps_q[0:64, :], SCL,
                    Qstk[0:64, sl, 0, :], ALU.mult, ALU.subtract)
                return e.scalar_tensor_tensor(
                    Qlo2[64:128, sl, 1, :], ps_q[64:128, :], SCL,
                    Qstk[64:128, sl, 1, :], ALU.mult, ALU.subtract)
            comp("vector", q_lo_ev, reads=["ps_q"],
                 writes=[f"qplov{sl}"], waits=rd(f"qphi{sl}"))

            def q_lo_rep(e, sl=sl):
                e.tensor_copy(Qlo2[64:128, sl, 0, :], Qlo2[0:64, sl, 0, :])
                return e.tensor_copy(Qlo2[0:64, sl, 1, :],
                                     Qlo2[64:128, sl, 1, :])
            comp("gpsimd", q_lo_rep, reads=[f"qplov{sl}"],
                 writes=[f"qplo{sl}"])

        PVLAG = 2
        for qc in range(NQC if _on("P2") else 0):
            for tt in range(4):
                x_split_store(x_own, xo16h, xo16l, (qc * 4 + tt) * 128)
            xfm_load(xo16h, xo16l, qc * 512)
            wq_load(0)
            wq_load(1)
            q_proj(0)
            q_proj(1)
            def finalize(fp, qc=qc):
                for hf, (po, pon) in enumerate(((ps_o, "ps_o"),
                                                (ps_ob, "ps_ob"))):
                    def recip(e, po=po):
                        e.reciprocal(rden_r[:], po[64:65, :])
                        e.tensor_copy(rd16h, rden_r[:])
                        return e.tensor_tensor(rd16l, rden_r[:],
                                               rd16h, ALU.subtract)
                    comp("vector", recip, reads=[pon], writes=["rden"])

                    def d_mm(e):
                        e.matmul(ps_rt[0:64, :], ones64[:], rd16h,
                                 start=True, stop=False)
                        return e.matmul(ps_rt[0:64, :], ones64[:],
                                        rd16l, start=False, stop=True)
                    comp("tensor", d_mm, reads=["rden"], writes=["ps_rt"],
                         waits=rd("ones64"))
                    comp("vector",
                         lambda e: e.tensor_copy(D_sb[:], ps_rt[0:64, :]),
                         reads=["ps_rt"], writes=["D_sb"])

                    def n_ev(e, hf=hf, po=po):
                        return e.tensor_tensor(
                            attn_st[hf * 64:hf * 64 + 64, :],
                            po[0:64, :], D_sb[:], ALU.mult)
                    comp("vector", n_ev, reads=[pon, "D_sb"],
                         writes=["atst"])

                def pair_ev(e, fp=fp):
                    e.tensor_copy(attn_hi[:, fp, :], attn_st[:])
                    return e.tensor_tensor(attn_lo[:, fp, :], attn_st[:],
                                           attn_hi[:, fp, :], ALU.subtract)
                comp("vector", pair_ev, reads=["atst"], writes=[f"at{fp}"])
                if DEBUG_TAPS and qc == 0 and fp == 0:
                    dma(lambda: attn_st[:], lambda: dbg_at[:], reads=["at0"])
                    dma(lambda: rden_r[:], lambda: dbg_rd[:],
                        reads=["rden"])

            for pair in range(NP_):
                hA, hB = PAIRS[pair]
                gt = 0 if hA < 8 else 1
                sl = pair % 2

                def emit_pv(kt, gt=gt):
                    par = kt % 3

                    def pv_mm(e, gt=gt, kt=kt, par=par):
                        pvc = ((Vp_hi, p16h, 66), (Vp_lo, p16h, 64),
                               (Vp_hi, p16l, 66))
                        for i3, (vv, pp, w) in enumerate(pvc):
                            e.matmul(ps_o[0:w, :],
                                     vv[:, kt, gt * 2, 0:w],
                                     pp[:, par, 0, :],
                                     start=(kt == 0 and i3 == 0),
                                     stop=(kt == NKT - 1 and i3 == 2),
                                     tile_position=(0, 0))
                            e.matmul(ps_ob[0:w, :],
                                     vv[:, kt, gt * 2 + 1, 0:w],
                                     pp[:, par, 1, :],
                                     start=(kt == 0 and i3 == 0),
                                     stop=(kt == NKT - 1 and i3 == 2),
                                     tile_position=(0, 0))
                        return None
                    extra = rd("vp")
                    if kt == 0:
                        extra += wr("ps_o", "ps_ob")
                    wrts = ["ps_o", "ps_ob"] if kt == NKT - 1 else []
                    comp("tensor", pv_mm,
                         reads=[f"ph{par}", f"plv{par}", f"plg{par}"],
                         writes=wrts, waits=extra)

                for kt in range(NKT):
                    par = kt % 3
                    pb2 = kt % 2
                    bk = kt % 2
                    sA, sAn = (ps_a, "ps_a") if bk == 0 else (ps_b, "ps_b")
                    sB, sBn = (ps_c, "ps_c") if bk == 0 else (ps_d, "ps_d")
                    ks = slice(kt * 128, (kt + 1) * 128)

                    def sc_mm(e, gt=gt, sl=sl, ks=ks, sA=sA, sB=sB):
                        e.matmul(sA[:], K_st[:, gt * 2, ks],
                                 Qstk[:, sl, 0, :], start=True, stop=False)
                        e.matmul(sB[:], K_st[:, gt * 2 + 1, ks],
                                 Qstk[:, sl, 1, :], start=True, stop=False)
                        e.matmul(sA[:], K_st[0:64, gt * 2, ks],
                                 Qlo2[0:64, sl, 0, :], start=False,
                                 stop=True, tile_position=(0, 0))
                        e.matmul(sB[:], K_st[0:64, gt * 2 + 1, ks],
                                 Qlo2[0:64, sl, 1, :], start=False,
                                 stop=True, tile_position=(0, 0))
                        return None
                    extra = []
                    if kt == 0:
                        extra = rd("kst0", "kst1")
                    comp("tensor", sc_mm, reads=[f"qphi{sl}", f"qplo{sl}"],
                         writes=[sAn, sBn], waits=extra)

                    def expo(e, sA=sA, sB=sB, par=par, pb2=pb2):
                        e.activation(p_buf[:, pb2, 0, :], sA[:], ACT.Exp)
                        e.activation(p16h[:, par, 0, :], sA[:], ACT.Exp)
                        e.activation(p_buf[:, pb2, 1, :], sB[:], ACT.Exp)
                        return e.activation(p16h[:, par, 1, :], sB[:],
                                            ACT.Exp)
                    comp("scalar", expo, reads=[sAn, sBn],
                         writes=[f"pbf{pb2}", f"pbg{pb2}", f"ph{par}"])

                    def pspl_sub0(e, par=par, pb2=pb2):
                        return e.tensor_tensor(p16l[:, par, 0, :],
                                               p_buf[:, pb2, 0, :],
                                               p16h[:, par, 0, :],
                                               ALU.subtract)
                    comp("vector", pspl_sub0,
                         reads=[f"pbf{pb2}", f"ph{par}"],
                         writes=[f"plv{par}"])

                    def pspl_sub1(e, par=par, pb2=pb2):
                        return e.tensor_tensor(p16l[:, par, 1, :],
                                               p_buf[:, pb2, 1, :],
                                               p16h[:, par, 1, :],
                                               ALU.subtract)
                    comp("gpsimd", pspl_sub1,
                         reads=[f"pbg{pb2}", f"ph{par}"],
                         writes=[f"plg{par}"])

                    if kt == 1 and pair > 0:
                        finalize(pair - 1)
                        if pair + 1 < NP_:
                            q_proj(pair + 1)
                    if kt % 2 == 1:
                        drain_p3a(1)
                    if kt >= PVLAG:
                        emit_pv(kt - PVLAG)
                for kt in range(NKT - PVLAG, NKT):
                    emit_pv(kt)
                if DEBUG_TAPS and qc == 0 and pair == 0:
                    dma(lambda: Qstk[:, 0, 0, :], lambda: dbg_qph[:],
                        reads=["qphi0"])
                    dma(lambda: Qlo2[:, 0, 0, :], lambda: dbg_qpl[:],
                        reads=["qplo0"])
                    dma(lambda: p16h[:, 1, :, :], lambda: dbg_ph[:],
                        waits=rd("ph1"))
                    dma(lambda: p_buf[:, 1, :, :], lambda: dbg_pf[:],
                        waits=rd("pbf1"))
                    dma(lambda: K_st[:, 0:2, :], lambda: dbg_kh[:],
                        waits=rd("kst0", "kst1"))
            finalize(NP_ - 1)


            for qs in range(4):
                j = qc * 4 + qs
                dma(lambda r0=j * 128: x_own[r0:r0 + 128, :],
                    lambda: x_stage[:], writes=["xs0"])
                for dc in range(2):
                    def wo_mm(e, qs=qs, dc=dc):
                        qsl = slice(qs * 128, (qs + 1) * 128)
                        dsl = slice(dc * 512, (dc + 1) * 512)
                        for pr in range(NP_):
                            for i3, (l, r) in enumerate(mm3_list(
                                    attn_hi[:, pr, qsl], attn_lo[:, pr, qsl],
                                    wo_hi[:, pr, dsl], wo_lo[:, pr, dsl])):
                                e.matmul(ps_q[:], l, r,
                                         start=(pr == 0 and i3 == 0),
                                         stop=(pr == NP_ - 1 and i3 == 2))
                        return None
                    comp("tensor", wo_mm,
                         reads=[f"at{pr}" for pr in range(NP_)],
                         writes=["ps_q"], waits=rd("wo"))

                    def y_ev(e, dc=dc):
                        dsl = slice(dc * 512, (dc + 1) * 512)
                        return e.tensor_tensor(y_sb[:, dsl], x_stage[:, dsl],
                                               ps_q[:], ALU.add)
                    comp("vector", y_ev, reads=["ps_q", "xs0"],
                         writes=[f"ydc{dc}"])

                if DEBUG_TAPS and j == 0:
                    dma(lambda: y_sb[:], lambda: dbg_y[:],
                        waits=rd("ydc0", "ydc1"))
                emit_ln("x1s0")
                dma(lambda: x1_sb[:],
                    lambda r0=j * 128: x1_hbm[r0:r0 + 128, :],
                    reads=["x1s0"], writes=["x1h"])
            if _on("P3b"):
                for rb in range(4):
                    for s2 in range(2):
                        _p3a_tile(_qt_n[0], x1_hbm, x1q16h, x1q16l,
                                  (qc * 4 + rb) * 128, s2 * 512, 512,
                                  stok="x1qs", lw=("x1h",))
                        _qt_n[0] += 1

        drain_p3a(len(_p3a))

        esP2.close()
        esA.close()
        P.barrier()
        P.barrier()
        esP3b = contextlib.ExitStack()
        sbP3b = mk_sb(esP3b)
        x1q_hi = sbP3b("x1q_hi_s", [128, 8, 1024], BF16)
        x1q_lo = sbP3b("x1q_lo_s", [128, 8, 1024], BF16)
        wgq_hi = sbP3b("wgq_hi_s", [128, 2, 8, 512], BF16)
        wgq_lo = sbP3b("wgq_lo_s", [128, 2, 8, 512], BF16)
        wuq_hi = sbP3b("wuq_hi_s", [128, 2, 8, 512], BF16)
        wuq_lo = sbP3b("wuq_lo_s", [128, 2, 8, 512], BF16)
        wdq_hi = sbP3b("wdq_hi_s", [128, 2, 4, 1024], BF16)
        wdq_lo = sbP3b("wdq_lo_s", [128, 2, 4, 1024], BF16)
        dacc = sbP3b("dacc", [128, FTT, D], F32)
        gs_sb = sbP3b("gs_sb", [128, 512], F32)
        h_sb = sbP3b("h_sb", [128, 512], F32)
        hq_hi = sbP3b("hq_hi", [128, 512], BF16)
        hq_lo = sbP3b("hq_lo", [128, 512], BF16)
        hqf_hi = sbP3b("hqf_hi", [128, 4, 1024], BF16)
        hqf_lo = sbP3b("hqf_lo", [128, 4, 1024], BF16)

        def _hq_stages(nn):
            subs = [(s * 512, min(512, nn - s * 512))
                    for s in range((nn + 511) // 512)]
            for b0, w in subs:
                comp("vector",
                     lambda e, b0=b0, w=w:
                     q_stage1(e, h_sb[:, b0:b0 + w], w),
                     reads=["h_sb"], writes=["qstg"])
                comp("vector",
                     lambda e, b0=b0, w=w:
                     q_stage2(e, h_sb[:, b0:b0 + w], w),
                     reads=["qstg"], writes=["qstg"])
                comp("vector",
                     lambda e, b0=b0, w=w:
                     q_stage3(e, h_sb[:, b0:b0 + w], w, None, None),
                     reads=["qstg", "h_sb"], writes=["qstg"])
                comp("gpsimd",
                     lambda e, b0=b0, w=w:
                     q_split_g(e, w, hq_hi[:, b0:b0 + w],
                               hq_lo[:, b0:b0 + w]),
                     reads=["qstg"], writes=["hq"])

        for tc in range(FTC if _on("P3b") else 0):
            for cc in range(8):
                dma_t(lambda tc=tc, cc=cc:
                      x1q16h[tc * 1024:(tc + 1) * 1024,
                             cc * 128:(cc + 1) * 128],
                      lambda cc=cc: x1q_hi[:, cc, :],
                      writes=["x1q"], waits=rd("x1qs") + wr("x1q"))
                dma_t(lambda tc=tc, cc=cc:
                      x1q16l[tc * 1024:(tc + 1) * 1024,
                             cc * 128:(cc + 1) * 128],
                      lambda cc=cc: x1q_lo[:, cc, :],
                      writes=["x1q"])

            dn_pend = []
            for nci, (n0, nn) in enumerate(NCH):
                nnt = nn // 128
                bp = nci % 2
                for cc in range(8):
                    dma_t(lambda n0=n0, nn=nn, cc=cc:
                          wgq_hi_h[n0:n0 + nn, cc * 128:(cc + 1) * 128],
                          lambda nn=nn, cc=cc, bp=bp:
                          wgq_hi[:, bp, cc, 0:nn],
                          writes=[f"wgq{bp}"],
                          waits=rd("scr") + wr(f"wgq{bp}"))
                    dma_t(lambda n0=n0, nn=nn, cc=cc:
                          wgq_lo_h[n0:n0 + nn, cc * 128:(cc + 1) * 128],
                          lambda nn=nn, cc=cc, bp=bp:
                          wgq_lo[:, bp, cc, 0:nn],
                          writes=[f"wgq{bp}"])
                    dma_t(lambda n0=n0, nn=nn, cc=cc:
                          wuq_hi_h[n0:n0 + nn, cc * 128:(cc + 1) * 128],
                          lambda nn=nn, cc=cc, bp=bp:
                          wuq_hi[:, bp, cc, 0:nn],
                          writes=[f"wgq{bp}"])
                    dma_t(lambda n0=n0, nn=nn, cc=cc:
                          wuq_lo_h[n0:n0 + nn, cc * 128:(cc + 1) * 128],
                          lambda nn=nn, cc=cc, bp=bp:
                          wuq_lo[:, bp, cc, 0:nn],
                          writes=[f"wgq{bp}"])
                for nt in range(nnt):
                    dma_t(lambda n0=n0, nt=nt:
                          wdq_hi_h[:, n0 + nt * 128:n0 + (nt + 1) * 128],
                          lambda nt=nt, bp=bp: wdq_hi[:, bp, nt, :],
                          writes=[f"wdq{bp}"],
                          waits=rd("scr") + wr(f"wdq{bp}"))
                    dma_t(lambda n0=n0, nt=nt:
                          wdq_lo_h[:, n0 + nt * 128:n0 + (nt + 1) * 128],
                          lambda nt=nt, bp=bp: wdq_lo[:, bp, nt, :],
                          writes=[f"wdq{bp}"])

                for tt in range(FTT):
                    par = tt % 2
                    g_ps, gn = ((ps_a, "ps_a") if par == 0
                                else (ps_b, "ps_b"))
                    u_ps, un = ((ps_c, "ps_c") if par == 0
                                else (ps_d, "ps_d"))
                    tsl = slice(tt * 128, (tt + 1) * 128)

                    def gu_mm(e, g_ps=g_ps, u_ps=u_ps, tsl=tsl, nn=nn,
                              bp=bp):
                        for cc in range(8):
                            for i3, (l, r) in enumerate(mm3_list(
                                    x1q_hi[:, cc, tsl],
                                    x1q_lo[:, cc, tsl],
                                    wgq_hi[:, bp, cc, 0:nn],
                                    wgq_lo[:, bp, cc, 0:nn])):
                                e.matmul(g_ps[:, 0:nn], l, r,
                                         start=(cc == 0 and i3 == 0),
                                         stop=(cc == 7 and i3 == 2))
                            for i3, (l, r) in enumerate(mm3_list(
                                    x1q_hi[:, cc, tsl],
                                    x1q_lo[:, cc, tsl],
                                    wuq_hi[:, bp, cc, 0:nn],
                                    wuq_lo[:, bp, cc, 0:nn])):
                                e.matmul(u_ps[:, 0:nn], l, r,
                                         start=(cc == 0 and i3 == 0),
                                         stop=(cc == 7 and i3 == 2))
                        return None
                    comp("tensor", gu_mm, reads=[f"wgq{bp}", "x1q"],
                         writes=[gn, un])

                    def silu(e, g_ps=g_ps, nn=nn):
                        return e.activation(gs_sb[:, 0:nn], g_ps[:, 0:nn],
                                            ACT.Silu)
                    comp("scalar", silu, reads=[gn], writes=["gs"])

                    def hmul(e, u_ps=u_ps, nn=nn):
                        return e.tensor_tensor(h_sb[:, 0:nn],
                                               gs_sb[:, 0:nn],
                                               u_ps[:, 0:nn], ALU.mult)
                    comp("vector", hmul, reads=[un, "gs"],
                         writes=["h_sb"])

                    _hq_stages(nn)

                    dma(lambda nn=nn: hq_hi[:, 0:nn],
                        lambda tt=tt, nn=nn:
                        hq16h[tt * 128:(tt + 1) * 128, 0:nn],
                        reads=["hq"], writes=["hqs"])
                    dma(lambda nn=nn: hq_lo[:, 0:nn],
                        lambda tt=tt, nn=nn:
                        hq16l[tt * 128:(tt + 1) * 128, 0:nn],
                        reads=["hq"], writes=["hqs"])

                    for _ in range(2):
                        if dn_pend:
                            dn_pend.pop(0)()
                while dn_pend:
                    dn_pend.pop(0)()

                for nt in range(nnt):
                    dma_t(lambda nt=nt:
                          hq16h[:, nt * 128:(nt + 1) * 128],
                          lambda nt=nt: hqf_hi[:, nt, :],
                          reads=["hqs"], writes=["hqf"])
                    dma_t(lambda nt=nt:
                          hq16l[:, nt * 128:(nt + 1) * 128],
                          lambda nt=nt: hqf_lo[:, nt, :],
                          reads=["hqs"], writes=["hqf"])

                def mk_dn(tt, dc, nci=nci, nnt=nnt, bp=bp):
                    def emit():
                        tsl = slice(tt * 128, (tt + 1) * 128)
                        po, pon = ((ps_o, "ps_o") if dc == 0
                                   else (ps_ob, "ps_ob"))

                        def dn_mm(e, tsl=tsl, dc=dc, nnt=nnt, bp=bp,
                                  po=po):
                            dsl = slice(dc * 512, (dc + 1) * 512)
                            for nt in range(nnt):
                                for i3, (l, r) in enumerate(mm3_list(
                                        hqf_hi[:, nt, tsl],
                                        hqf_lo[:, nt, tsl],
                                        wdq_hi[:, bp, nt, dsl],
                                        wdq_lo[:, bp, nt, dsl])):
                                    e.matmul(po[:], l, r,
                                             start=(nt == 0 and i3 == 0),
                                             stop=(nt == nnt - 1
                                                   and i3 == 2))
                            return None
                        rds = (["hqf", f"wdq{bp}"]
                               if (tt == FTT - 1 and dc == 1) else [])
                        comp("tensor", dn_mm, reads=rds, writes=[pon],
                             waits=rd("hqf", f"wdq{bp}"))

                        def d_add(e, tt=tt, dc=dc, nci=nci, po=po):
                            dsl = slice(dc * 512, (dc + 1) * 512)
                            if nci == 0:
                                return e.tensor_copy(dacc[:, tt, dsl],
                                                     po[:])
                            return e.tensor_tensor(dacc[:, tt, dsl],
                                                   dacc[:, tt, dsl],
                                                   po[:], ALU.add)
                        comp("vector", d_add, reads=[pon],
                             writes=[f"da{tt}_{dc}"])
                    return emit
                dn_pend = [mk_dn(tt, dc)
                           for tt in range(FTT) for dc in range(2)]
            while dn_pend:
                dn_pend.pop(0)()

            for tt in range(FTT):
                j = tc * FTT + tt
                r0 = j * 128
                dma(lambda r0=r0: x1_hbm[r0:r0 + 128, :],
                    lambda: x_stage[:], writes=["xs0"], waits=rd("x1h"))

                def y2(e, tt=tt):
                    return e.tensor_tensor(y_sb[:], x_stage[:],
                                           dacc[:, tt, :], ALU.add)
                comp("vector", y2,
                     reads=["xs0", f"da{tt}_0", f"da{tt}_1"],
                     writes=["ydc0", "ydc1"])

                emit_ln("x1s0")
                dma(lambda: x1_sb[:],
                    lambda r0=r0: out_h[r0:r0 + 128, :], reads=["x1s0"])

        final = [("dm", P.ctr["dm"]), ("da", P.ctr["da"])]
        P.op("sync", lambda e: None, final, noinc=True)

        sems = {s: es.enter_context(nc.semaphore(s)) for s in SEMS}
        blk = es.enter_context(nc.Block())

        @blk.sync
        def _(e):
            P.emit("sync", e, sems)

        @blk.tensor
        def _(e):
            P.emit("tensor", e, sems)

        @blk.vector
        def _(e):
            P.emit("vector", e, sems)

        @blk.scalar
        def _(e):
            P.emit("scalar", e, sems)

        @blk.gpsimd
        def _(e):
            P.emit("gpsimd", e, sems)

        esP3b.close()

    return nc


def host_consts():
    return {}


_PROG = None


def _get_prog():
    global _PROG
    if _PROG is None:
        _PROG = build_program()
    return _PROG


def kernel(L0_attn_x, L0_attn_Wq, L0_attn_Wk, L0_attn_Wv, L0_attn_Wo,
           L0_ffn_w_gate, L0_ffn_w_up, L0_ffn_w_down):
    x = np.ascontiguousarray(np.asarray(L0_attn_x, dtype=np.float32))
    wq = np.ascontiguousarray(np.asarray(L0_attn_Wq, dtype=np.float32))
    wk = np.ascontiguousarray(np.asarray(L0_attn_Wk, dtype=np.float32))
    wv = np.ascontiguousarray(np.asarray(L0_attn_Wv, dtype=np.float32))
    wo = np.ascontiguousarray(np.asarray(L0_attn_Wo, dtype=np.float32))
    wg = np.ascontiguousarray(np.asarray(L0_ffn_w_gate, dtype=np.float32))
    wu = np.ascontiguousarray(np.asarray(L0_ffn_w_up, dtype=np.float32))
    wd = np.ascontiguousarray(np.asarray(L0_ffn_w_down, dtype=np.float32))

    nc = _get_prog()
    in_maps = []
    for c in range(NCORES):
        b, half = c // 2, c % 2
        in_maps.append({
            "x_all": x[b],
            "x_own": np.ascontiguousarray(x[b, half * TQ:(half + 1) * TQ]),
            "wq": wq, "wk": wk, "wv": wv, "wo": wo,
            "wg": wg, "wu": wu, "wd": wd,
        })
    res = run_bass_kernel_spmd(nc, in_maps, core_ids=list(range(NCORES)))
    out = np.empty((B, T, D), dtype=np.float32)
    for c in range(NCORES):
        b, half = c // 2, c % 2
        out[b, half * TQ:(half + 1) * TQ] = np.asarray(res.results[c]["out"])
    return out


# version 20
# speedup vs baseline: 1.0331x; 1.0152x over previous
import contextlib

import numpy as np
import ml_dtypes

import concourse.bass as bass
import concourse.mybir as mybir
from concourse.bass_utils import run_bass_kernel_spmd

F32 = mybir.dt.float32
BF16 = mybir.dt.bfloat16
ALU = mybir.AluOpType
ACT = mybir.ActivationFunctionType
AX = mybir.AxisListType

B, T, D = 4, 4096, 1024
TQ = T // 2
H, HD, G = 16, 64, 4
DK = G * HD
DF = 2816
NCORES = 8
EPS = 1e-5

PAIRS = [(0, 4), (1, 5), (2, 6), (3, 7), (8, 12), (9, 13), (10, 14), (11, 15)]
NP_ = len(PAIRS)
NQC = TQ // 512
NKT = T // 128
FTC = 2
FTT = 8
NCH = [(0, 512), (512, 512), (1024, 512), (1536, 512), (2048, 512), (2560, 256)]

STREAMS = {"sync": "dm", "tensor": "tp", "vector": "vp", "scalar": "sp",
           "gpsimd": "gp"}
SEMS = ("dm", "da", "tp", "vp", "sp", "gp", "hzv", "hzs", "hzt", "hzg")

DEBUG_TAPS = False
DEBUG_OUT = False
STOP_AFTER = None
_PHASES = ["P0", "P1", "P2", "P3a", "P3b"]


def _on(ph):
    return STOP_AFTER is None or _PHASES.index(ph) <= _PHASES.index(STOP_AFTER)


class Plan:
    def __init__(self):
        self.ops = []
        self.ctr = {s: 0 for s in SEMS}
        self.pending = {}

    def barrier(self):
        snap = [(s, v) for s, v in self.ctr.items() if v > 0]
        for eng in STREAMS:
            self.pending[eng] = list(snap)

    def op(self, engine, body, waits, dma=False, noinc=False):
        sem = STREAMS[engine]
        dma_sem = None
        if dma:
            dma_sem = "da" if engine == "scalar" else "dm"
        waits = list(waits) + self.pending.pop(engine, [])
        w = []
        for sv in waits:
            if sv is None:
                continue
            s, v = sv
            if v <= 0:
                continue
            if s not in ("dm", "da") and s == sem:
                continue
            w.append((s, v))
        self.ops.append((engine, w, body, dma_sem, noinc))
        if noinc:
            return None
        if dma_sem:
            self.ctr[dma_sem] += 16
            return (dma_sem, self.ctr[dma_sem])
        self.ctr[sem] += 1
        return (sem, self.ctr[sem])

    def emit(self, engine, eng, sems):
        hz = {"vector": "hzv", "scalar": "hzs", "tensor": "hzt",
              "gpsimd": "hzg"}.get(engine, "hzv")
        rec = _Rec(eng, sems[hz], sems[STREAMS[engine]])
        hw = {}
        for (oe, waits, body, dma_sem, noinc) in self.ops:
            if oe != engine:
                continue
            for (s, v) in waits:
                if hw.get(s, -1) >= v:
                    continue
                eng.wait_ge(sems[s], v)
                hw[s] = v
            rec.last = None
            body(rec)
            if noinc:
                continue
            if dma_sem:
                rec.last.then_inc(sems[dma_sem], 16)
            elif rec.last is not None:
                rec.last.then_inc(sems[STREAMS[engine]], 1)
                rec.scount += 1
                rec.inced = rec.last
            else:
                eng.nop().then_inc(sems[STREAMS[engine]], 1)
                rec.scount += 1


class _Rec:

    _OPS = frozenset((
        "tensor_copy", "tensor_tensor", "tensor_add", "tensor_sub",
        "tensor_mul", "tensor_max", "tensor_scalar", "tensor_single_scalar",
        "tensor_scalar_add", "tensor_scalar_sub", "tensor_scalar_mul",
        "tensor_scalar_max", "tensor_scalar_min", "tensor_relu",
        "tensor_reduce", "reduce_max", "reduce_sum", "scalar_tensor_tensor",
        "reciprocal", "memset", "memzero", "activation", "sign", "square",
        "sqrt", "mul", "copy", "add", "bn_stats", "bn_aggr", "select",
        "transpose", "iota",
    ))

    def __init__(self, eng, hz_sem, stream_sem):
        self._e = eng
        self._hz = hz_sem
        self._hzc = 0
        self._ssem = stream_sem
        self.scount = 0
        self.inced = None
        self.last = None
        self.lastf = None
        self._dirty_w = set()
        self._dirty_r = set()

    @staticmethod
    def _apname(a):
        t = getattr(a, "tensor", None)
        if t is None:
            return None
        n = getattr(t, "name", None)
        return n if isinstance(n, str) else None

    def _fence(self):
        if self.lastf is not None:
            if self.lastf is self.inced:
                self._e.wait_ge(self._ssem, self.scount)
            else:
                self._hzc += 1
                self.lastf.then_inc(self._hz, 1)
                self._e.wait_ge(self._hz, self._hzc)
        self._dirty_w.clear()
        self._dirty_r.clear()

    def __getattr__(self, k):
        a = getattr(self._e, k)
        if not callable(a):
            return a
        if k not in self._OPS:
            def w0(*args, **kw):
                r = a(*args, **kw)
                if hasattr(r, "then_inc"):
                    self.last = r
                return r
            return w0

        def w(*args, **kw):
            outs = set()
            ins = set()
            for i, ar in enumerate(args):
                n = self._apname(ar)
                if n:
                    (outs if i == 0 else ins).add(n)
            for key, ar in kw.items():
                n = self._apname(ar)
                if n:
                    (outs if key == "accum_out" else ins).add(n)
            if (ins & self._dirty_w) or (outs & (self._dirty_w
                                                | self._dirty_r)):
                self._fence()
            r = a(*args, **kw)
            if hasattr(r, "then_inc"):
                self.last = r
                self.lastf = r
            self._dirty_w |= outs
            self._dirty_r |= ins
            return r
        return w


class Tok:
    __slots__ = ("w", "r")

    def __init__(self):
        self.w = None
        self.r = None


def mm3_list(hi_l, lo_l, hi_r, lo_r):
    return ((hi_l, hi_r), (hi_l, lo_r), (lo_l, hi_r))


def build_program():
    nc = bass.Bass("TRN2", target_bir_lowering=False,
                   detect_race_conditions=False)

    x_all = nc.dram_tensor("x_all", [T, D], F32, kind="ExternalInput")
    x_own = nc.dram_tensor("x_own", [TQ, D], F32, kind="ExternalInput")
    wq_h = nc.dram_tensor("wq", [D, D], F32, kind="ExternalInput")
    wk_h = nc.dram_tensor("wk", [D, DK], F32, kind="ExternalInput")
    wv_h = nc.dram_tensor("wv", [D, DK], F32, kind="ExternalInput")
    wo_h = nc.dram_tensor("wo", [D, D], F32, kind="ExternalInput")
    wg_h = nc.dram_tensor("wg", [DF, D], F32, kind="ExternalInput")
    wu_h = nc.dram_tensor("wu", [DF, D], F32, kind="ExternalInput")
    wd_h = nc.dram_tensor("wd", [D, DF], F32, kind="ExternalInput")
    out_h = nc.dram_tensor("out", [TQ, D], F32, kind="ExternalOutput")
    _k = dict(kind="ExternalOutput") if DEBUG_OUT else {}

    def _k2():
        return dict(kind="ExternalOutput")

    x1_hbm = nc.dram_tensor("x1_hbm", [TQ, D], F32, **_k)
    xa16h = nc.dram_tensor("xa16h", [T, D], BF16, **_k)
    xa16l = nc.dram_tensor("xa16l", [T, D], BF16)
    xo16h = nc.dram_tensor("xo16h", [TQ, D], BF16)
    xo16l = nc.dram_tensor("xo16l", [TQ, D], BF16)
    v16h_hbm = nc.dram_tensor("v16h_hbm", [DK, T], BF16, **_k)
    v16l_hbm = nc.dram_tensor("v16l_hbm", [DK, T], BF16)
    wqs_hi_h = nc.dram_tensor("wqs_hi", [D, D], BF16, **_k)
    wqs_lo_h = nc.dram_tensor("wqs_lo", [D, D], BF16)
    wos_hi_h = nc.dram_tensor("wos_hi", [D, D], BF16)
    wos_lo_h = nc.dram_tensor("wos_lo", [D, D], BF16)
    wgq_hi_h = nc.dram_tensor("wgq_hi", [DF, D], BF16, **_k)
    wgq_lo_h = nc.dram_tensor("wgq_lo", [DF, D], BF16)
    wuq_hi_h = nc.dram_tensor("wuq_hi", [DF, D], BF16)
    wuq_lo_h = nc.dram_tensor("wuq_lo", [DF, D], BF16)
    wdq_hi_h = nc.dram_tensor("wdq_hi", [D, DF], BF16)
    wdq_lo_h = nc.dram_tensor("wdq_lo", [D, DF], BF16)
    x1q16h = nc.dram_tensor("x1q16h", [TQ, D], BF16)
    x1q16l = nc.dram_tensor("x1q16l", [TQ, D], BF16)
    hq16h = nc.dram_tensor("hq16h", [1024, 512], BF16)
    hq16l = nc.dram_tensor("hq16l", [1024, 512], BF16)
    if DEBUG_TAPS:
        dbg_qph = nc.dram_tensor("dbg_qph", [128, 512], BF16, **_k2())
        dbg_qpl = nc.dram_tensor("dbg_qpl", [128, 512], BF16, **_k2())
        dbg_ph = nc.dram_tensor("dbg_ph", [128, 2, 512], BF16, **_k2())
        dbg_pf = nc.dram_tensor("dbg_pf", [128, 2, 512], F32, **_k2())
        dbg_at = nc.dram_tensor("dbg_at", [128, 512], F32, **_k2())
        dbg_rd = nc.dram_tensor("dbg_rd", [1, 512], F32, **_k2())
        dbg_y = nc.dram_tensor("dbg_y", [128, 1024], F32, **_k2())
        dbg_kh = nc.dram_tensor("dbg_kh", [128, 2, 4096], BF16, **_k2())

    es = contextlib.ExitStack()

    def mk_sb(stack):
        def f(name, shape, dtype):
            return stack.enter_context(nc.sbuf_tensor(name, shape, dtype))
        return f

    with es:
        sb = mk_sb(es)
        ones64 = sb("ones64", [1, 64], BF16)
        qA = sb("qA", [128, 512], F32)
        qB = sb("qB", [128, 512], F32)
        qC = sb("qC", [128, 512], F32)
        qsc = sb("qsc", [128, 32], F32)
        qr2 = sb("qr2", [128, 32], F32)
        x_stage = sb("x_stage", [128, D], F32)
        y_sb = sb("y_sb", [128, D], F32)
        x1_sb = sb("x1_sb", [128, D], F32)
        sq_scr = sb("sq_scr", [128, D], BF16)
        ln1 = sb("ln1", [128, 1], F32)
        ln2 = sb("ln2", [128, 1], F32)
        ln3 = sb("ln3", [128, 1], F32)
        ln4 = sb("ln4", [128, 1], F32)
        ln5 = sb("ln5", [128, 1], F32)

        ps_a = es.enter_context(nc.psum_tensor("ps_a", [128, 512], F32))
        ps_b = es.enter_context(nc.psum_tensor("ps_b", [128, 512], F32))
        ps_c = es.enter_context(nc.psum_tensor("ps_c", [128, 512], F32))
        ps_d = es.enter_context(nc.psum_tensor("ps_d", [128, 512], F32))
        ps_o = es.enter_context(nc.psum_tensor("ps_o", [128, 512], F32))
        ps_ob = es.enter_context(nc.psum_tensor("ps_ob", [128, 512], F32))
        ps_q = es.enter_context(nc.psum_tensor("ps_q", [128, 512], F32))
        ps_rt = es.enter_context(nc.psum_tensor("ps_rt", [128, 512], F32))

        P = Plan()
        toks = {}

        def tok(name):
            if name not in toks:
                toks[name] = Tok()
            return toks[name]

        def rd(*names):
            return [tok(n).w for n in names]

        def wr(*names):
            return [tok(n).r for n in names]

        def _mark(sv, reads, writes):
            for n in writes:
                tok(n).w = sv
            for n in reads:
                tok(n).r = sv
            return sv

        def dma(src_fn, dst_fn, reads=(), writes=(), waits=()):
            w = list(waits) + rd(*reads) + wr(*writes)
            sv = P.op("sync", lambda e, a=src_fn, b=dst_fn:
                      e.dma_start(out=b(), in_=a()), w, dma=True)
            return _mark(sv, reads, writes)

        def dma_t(src_fn, dst_fn, reads=(), writes=(), waits=()):
            w = list(waits) + rd(*reads) + wr(*writes)
            sv = P.op("scalar", lambda e, a=src_fn, b=dst_fn:
                      e.dma_start_transpose(b(), a()), w, dma=True)
            return _mark(sv, reads, writes)

        def comp(engine, body, reads=(), writes=(), waits=()):
            w = list(waits) + rd(*reads) + wr(*writes)
            sv = P.op(engine, body, w)
            return _mark(sv, reads, writes)

        def emit_ln(store_tok):
            comp("vector",
                 lambda e: e.tensor_reduce(ln1[:], y_sb[:], AX.X, ALU.add),
                 writes=["ln1"], waits=rd("ydc0", "ydc1"))
            comp("scalar",
                 lambda e: e.activation(sq_scr[:], y_sb[:], ACT.Square,
                                        accum_out=ln2[:]),
                 writes=["ln2"], waits=rd("ydc0", "ydc1"))

            def ln_var(e):
                e.tensor_scalar(ln3[:], ln1[:], 1.0 / D, None, ALU.mult)
                e.tensor_tensor(ln4[:], ln3[:], ln3[:], ALU.mult)
                e.scalar_tensor_tensor(ln4[:], ln2[:], 1.0 / D, ln4[:],
                                       ALU.mult, ALU.subtract)
                return e.tensor_scalar(ln4[:], ln4[:], EPS, None, ALU.add)
            comp("vector", ln_var, reads=["ln1", "ln2"], writes=["ln3", "ln4"])
            comp("scalar", lambda e: e.activation(ln5[:], ln4[:], ACT.Sqrt),
                 reads=["ln4"], writes=["ln5"])

            def ln_fin(e):
                e.reciprocal(ln4[:], ln5[:])
                return e.scalar_tensor_tensor(ln3[:], ln3[:], -1.0, ln4[:],
                                              ALU.mult, ALU.mult)
            comp("vector", ln_fin, reads=["ln5"], writes=["lnf"])

            def ln_app(e):
                return e.activation(x1_sb[:], y_sb[:], ACT.Identity,
                                    bias=ln3[:], scale=ln4[:])
            sv = comp("scalar", ln_app, reads=["lnf"], writes=[store_tok])
            for n in ("ydc0", "ydc1", "ln3", "ln4"):
                tok(n).r = sv

        def x_split_store(src_h, h16, l16, row0):
            dma(lambda: src_h[row0:row0 + 128, :], lambda: x_stage[:],
                writes=["xs0"])

            def split(e):
                e.tensor_copy(xt_hi[:], x_stage[:])
                return e.tensor_tensor(xt_lo[:], x_stage[:], xt_hi[:],
                                       ALU.subtract)
            comp("vector", split, reads=["xs0"], writes=["xt0"])
            dma(lambda: xt_hi[:], lambda: h16[row0:row0 + 128, :],
                reads=["xt0"], writes=["x16s"])
            dma(lambda: xt_lo[:], lambda: l16[row0:row0 + 128, :],
                reads=["xt0"], writes=["x16s"])

        def xfm_load(h16, l16, row0):
            for cc in range(8):
                dma_t(lambda h16=h16, row0=row0, cc=cc:
                      h16[row0:row0 + 512, cc * 128:(cc + 1) * 128],
                      lambda cc=cc: xfm_hi[:, cc, :],
                      writes=["xfm"], waits=rd("x16s"))
                dma_t(lambda l16=l16, row0=row0, cc=cc:
                      l16[row0:row0 + 512, cc * 128:(cc + 1) * 128],
                      lambda cc=cc: xfm_lo[:, cc, :],
                      writes=["xfm"], waits=rd("x16s"))

        esA = contextlib.ExitStack()
        sbA = mk_sb(esA)
        K_st = sbA("K_st", [128, 4, T], BF16)
        wst3 = sbA("wst3", [128, 2, 512], F32)
        q3h = sbA("q3h", [128, 2, 512], BF16)
        q3l = sbA("q3l", [128, 2, 512], BF16)
        xt_hi = sbA("xt_hi", [128, D], BF16)
        xt_lo = sbA("xt_lo", [128, D], BF16)
        xfm_hi = sbA("xfm_hi", [128, 8, 512], BF16)
        xfm_lo = sbA("xfm_lo", [128, 8, 512], BF16)
        ktmp = sbA("ktmp", [128, 512], BF16)
        Vp_hi = sbA("Vp_hi", [128, NKT, 4, 66], BF16)
        Vp_lo = sbA("Vp_lo", [128, NKT, 4, 64], BF16)
        Qstk = sbA("Qstk", [128, 2, 2, 512], BF16)
        Qlo2 = sbA("Qlo2", [128, 2, 2, 512], BF16)
        p_buf = sbA("p_buf", [128, 2, 2, 512], F32)
        p16h = sbA("p16h", [128, 3, 2, 512], BF16)
        p16l = sbA("p16l", [128, 3, 2, 512], BF16)
        attn_hi = sbA("attn_hi", [128, NP_, 512], BF16)
        attn_lo = sbA("attn_lo", [128, NP_, 512], BF16)
        attn_st = sbA("attn_st", [128, 512], F32)
        rden_r = sbA("rden_r", [1, 512], F32)

        D_sb = sbA("D_sb", [64, 512], F32)
        rd16h = D_sb[:].bitcast(BF16)[0:1, 0:512]
        rd16l = D_sb[:].bitcast(BF16)[0:1, 512:1024]
        vstg = sbA("vstg", [128, 2, 2, 256], BF16)

        esP1 = contextlib.ExitStack()
        sbP1 = mk_sb(esP1)
        wstage = sbP1("wstage", [128, 4096], F32)
        wk_hi = sbP1("wk_hi", [128, 8, DK], BF16)
        wk_lo = sbP1("wk_lo", [128, 8, DK], BF16)
        wv_hi = sbP1("wv_hi", [128, 8, DK], BF16)
        wv_lo = sbP1("wv_lo", [128, 8, DK], BF16)
        v16h = sbP1("v16h", [128, 512], BF16)
        v16l = sbP1("v16l", [128, 512], BF16)

        comp("vector", lambda e: e.memset(ones64[:], 1.0), writes=["ones64"])

        def vaug_init(e):
            e.memset(Vp_hi[:, :, :, 64:66], 0.0)
            return e.memset(Vp_hi[:, :, :, 64:65], 1.0)
        comp("vector", vaug_init, writes=["vaug"])

        for wh, shh, slh in ((wq_h, wqs_hi_h, wqs_lo_h),
                             (wo_h, wos_hi_h, wos_lo_h)):
            for hv in range(2):
                dma(lambda wh=wh, hv=hv:
                    wh[hv * 512:(hv + 1) * 512, :]
                    .rearrange("(cc p) d -> p cc d", p=128),
                    lambda: wstage[:].rearrange("p (cc d) -> p cc d", cc=4),
                    writes=["wstage"])

                def wsp(e):
                    flat = K_st[:].rearrange("p a b -> p (a b)")
                    khf = flat[:, 0:4096]
                    klf = flat[:, 8192:12288]
                    e.tensor_copy(khf, wstage[:])
                    return e.tensor_tensor(klf, wstage[:], khf,
                                           ALU.subtract)
                comp("vector", wsp, reads=["wstage"], writes=["kbuf"])

                dma(lambda: K_st[:].rearrange("p a b -> p (a b)")
                    [:, 0:4096].rearrange("p (c d) -> p c d", c=4),
                    lambda shh=shh, hv=hv:
                    shh[hv * 512:(hv + 1) * 512, :]
                    .rearrange("(cc p) d -> p cc d", p=128),
                    reads=["kbuf"], writes=["wscr"])
                dma(lambda: K_st[:].rearrange("p a b -> p (a b)")
                    [:, 8192:12288].rearrange("p (c d) -> p c d", c=4),
                    lambda slh=slh, hv=hv:
                    slh[hv * 512:(hv + 1) * 512, :]
                    .rearrange("(cc p) d -> p cc d", p=128),
                    reads=["kbuf"], writes=["wscr"])

        for wname, wh, whi, wlo in (("wk", wk_h, wk_hi, wk_lo),
                                    ("wv", wv_h, wv_hi, wv_lo)):
            st3 = wstage[:, 0:8 * DK].rearrange("p (cc d) -> p cc d", d=DK)
            dma(lambda wh=wh: wh[:].rearrange("(cc p) d -> p cc d", p=128),
                lambda st3=st3: st3, writes=["wstage"])

            def wsplit(e, whi=whi, wlo=wlo):
                flat = wstage[:, 0:8 * DK]
                hi_f = whi[:].rearrange("p c d -> p (c d)")
                lo_f = wlo[:].rearrange("p c d -> p (c d)")
                e.tensor_copy(hi_f, flat)
                return e.tensor_tensor(lo_f, flat, hi_f, ALU.subtract)
            comp("vector", wsplit, reads=["wstage"], writes=[wname])

        def q_stage1(e, src, n):
            nb = n // 16
            sc, r2 = qsc[:, 0:nb], qr2[:, 0:nb]
            src3 = src.rearrange("p (nb b) -> p nb b", b=16)
            e.tensor_reduce(sc, src3, AX.X, ALU.max,
                            apply_absolute_value=True)
            e.reciprocal(r2, sc)
            e.tensor_scalar(r2, r2, 6.0, None, ALU.mult)
            e.reciprocal(sc, r2)
            e.tensor_scalar(sc, sc, 1e-12, None, ALU.max)
            e.reciprocal(r2, sc)
            return e.tensor_scalar(r2, r2, 2.0, None, ALU.mult)

        def q_stage2(e, src, n):
            nb = n // 16
            A_, B_ = qA[:, 0:n], qB[:, 0:n]
            r2bc = qr2[:, 0:nb].unsqueeze(2).broadcast_to([128, nb, 16])
            A3 = A_.rearrange("p (nb b) -> p nb b", b=16)
            e.scalar_tensor_tensor(A_, src, -1.0, src, ALU.mult, ALU.max)
            e.tensor_tensor(A3, A3, r2bc, ALU.mult)
            e.tensor_scalar(B_, A_, 0.5, None, ALU.is_gt)
            e.scalar_tensor_tensor(B_, A_, 1.5, B_, ALU.is_gt, ALU.add)
            e.scalar_tensor_tensor(B_, A_, 2.5, B_, ALU.is_gt, ALU.add)
            return e.scalar_tensor_tensor(B_, A_, 3.5, B_, ALU.is_gt,
                                          ALU.add)

        def q_stage3(e, src, n, hi_out, lo_out):
            nb = n // 16
            A_, B_, C_ = qA[:, 0:n], qB[:, 0:n], qC[:, 0:n]
            scbc = qsc[:, 0:nb].unsqueeze(2).broadcast_to([128, nb, 16])
            A3 = A_.rearrange("p (nb b) -> p nb b", b=16)
            e.tensor_scalar(C_, A_, 5.0, None, ALU.is_gt)
            e.scalar_tensor_tensor(C_, A_, 7.0, C_, ALU.is_gt, ALU.add)
            e.scalar_tensor_tensor(B_, C_, 2.0, B_, ALU.mult, ALU.add)
            e.tensor_scalar(C_, A_, 10.0, 4.0, ALU.is_gt, ALU.mult)
            e.tensor_tensor(B_, B_, C_, ALU.add)
            e.tensor_scalar(C_, src, 0.0, None, ALU.is_ge)
            e.scalar_tensor_tensor(A_, C_, -0.5, B_, ALU.add, ALU.mult)
            return e.tensor_tensor(A3, A3, scbc, ALU.mult)

        def q_split_g(e, n, hi_out, lo_out):
            A_ = qA[:, 0:n]
            e.tensor_copy(hi_out, A_)
            return e.tensor_tensor(lo_out, A_, hi_out, ALU.subtract)

        def quant_seq(e, src, n, hi_out, lo_out):
            q_stage1(e, src, n)
            q_stage2(e, src, n)
            q_stage3(e, src, n, hi_out, lo_out)
            return q_split_g(e, n, hi_out, lo_out)

        _p3a = []

        def _p3a_tile(idx, wh, hh, ll, r0, c0, n, stok="scr", lw=()):
            p = idx % 2

            def em1(p=p, wh=wh, r0=r0, c0=c0, n=n, lw=tuple(lw)):
                dma(lambda: wh[r0:r0 + 128, c0:c0 + n],
                    lambda: wst3[:, p, 0:n], writes=[f"fst{p}"],
                    waits=rd(*lw))
                comp("vector",
                     lambda e, p=p, n=n: q_stage1(e, wst3[:, p, 0:n], n),
                     reads=[f"fst{p}"], writes=["qstg"])

            def em2(p=p, n=n):
                comp("vector",
                     lambda e, p=p, n=n: q_stage2(e, wst3[:, p, 0:n], n),
                     reads=["qstg"], writes=["qstg"])

            def em3(p=p, hh=hh, ll=ll, r0=r0, c0=c0, n=n, stok=stok):
                def qn(e, p=p, n=n):
                    q_stage3(e, wst3[:, p, 0:n], n, None, None)
                    return q_split_g(e, n, q3h[:, p, 0:n], q3l[:, p, 0:n])
                comp("vector", qn, reads=["qstg", f"fst{p}"],
                     writes=[f"q3{p}"])
                dma(lambda: q3h[:, p, 0:n],
                    lambda: hh[r0:r0 + 128, c0:c0 + n],
                    reads=[f"q3{p}"], writes=[stok])
                dma(lambda: q3l[:, p, 0:n],
                    lambda: ll[r0:r0 + 128, c0:c0 + n],
                    reads=[f"q3{p}"], writes=[stok])
            _p3a.extend((em1, em2, em3))

        _qt_n = [0]
        if _on("P3a"):
            for wh, hh, ll in ((wg_h, wgq_hi_h, wgq_lo_h),
                               (wu_h, wuq_hi_h, wuq_lo_h)):
                for ch in range(22):
                    for s2 in range(2):
                        _p3a_tile(_qt_n[0], wh, hh, ll, ch * 128,
                                  s2 * 512, 512)
                        _qt_n[0] += 1
            for ch in range(8):
                for s2 in range(5):
                    _p3a_tile(_qt_n[0], wd_h, wdq_hi_h, wdq_lo_h,
                              ch * 128, s2 * 512, 512)
                    _qt_n[0] += 1
                _p3a_tile(_qt_n[0], wd_h, wdq_hi_h, wdq_lo_h,
                          ch * 128, 2560, 256)
                _qt_n[0] += 1
        _p3a_i = [0]

        def drain_p3a(k):
            while k > 0 and _p3a_i[0] < len(_p3a):
                _p3a[_p3a_i[0]]()
                _p3a_i[0] += 1
                k -= 1

        for tc in range(8 if _on("P1") else 0):
            for tt in range(4):
                x_split_store(x_all, xa16h, xa16l, (tc * 4 + tt) * 128)
            xfm_load(xa16h, xa16l, tc * 512)
            ts = slice(tc * 512, (tc + 1) * 512)
            for mt in range(2):
                def k_mm(e, mt=mt):
                    for cc in range(8):
                        for i3, (l, r) in enumerate(mm3_list(
                                wk_hi[:, cc, mt * 128:(mt + 1) * 128],
                                wk_lo[:, cc, mt * 128:(mt + 1) * 128],
                                xfm_hi[:, cc, :], xfm_lo[:, cc, :])):
                            e.matmul(ps_q[:], l, r,
                                     start=(cc == 0 and i3 == 0),
                                     stop=(cc == 7 and i3 == 2))
                    return None
                comp("tensor", k_mm, reads=["xfm"], writes=["ps_q"],
                     waits=rd("wk"))

                def k_hi_ev(e, mt=mt, ts=ts):
                    e.activation(K_st[0:64, 2 * mt, ts], ps_q[0:64, :],
                                 ACT.Copy)
                    e.activation(K_st[0:64, 2 * mt + 1, ts], ps_q[64:128, :],
                                 ACT.Copy)
                    return e.activation(ktmp[64:128, :], ps_q[64:128, :],
                                        ACT.Copy)
                comp("scalar", k_hi_ev, writes=[f"ksth{mt}"],
                     waits=rd("ps_q") + wr("kbuf"))

                def k_lo_ev(e, mt=mt, ts=ts):
                    e.tensor_tensor(K_st[64:128, 2 * mt, ts], ps_q[0:64, :],
                                    K_st[0:64, 2 * mt, ts], ALU.subtract)
                    return e.tensor_tensor(K_st[64:128, 2 * mt + 1, ts],
                                           ps_q[64:128, :], ktmp[64:128, :],
                                           ALU.subtract)
                comp("vector", k_lo_ev, reads=["ps_q"], writes=[f"kst{mt}"],
                     waits=rd(f"ksth{mt}") + wr("kbuf"))
            drain_p3a(2)
            for mt in range(2):
                def v_mm(e, mt=mt):
                    for cc in range(8):
                        for i3, (l, r) in enumerate(mm3_list(
                                wv_hi[:, cc, mt * 128:(mt + 1) * 128],
                                wv_lo[:, cc, mt * 128:(mt + 1) * 128],
                                xfm_hi[:, cc, :], xfm_lo[:, cc, :])):
                            e.matmul(ps_q[:], l, r,
                                     start=(cc == 0 and i3 == 0),
                                     stop=(cc == 7 and i3 == 2))
                    return None
                comp("tensor", v_mm, reads=["xfm"], writes=["ps_q"],
                     waits=rd("wv"))

                def v_split(e):
                    e.tensor_copy(v16h[:], ps_q[:])
                    return e.tensor_tensor(v16l[:], ps_q[:], v16h[:],
                                           ALU.subtract)
                comp("vector", v_split, reads=["ps_q"], writes=["v16"])
                dma(lambda mt=mt, ts=ts: v16h[:],
                    lambda mt=mt, ts=ts: v16h_hbm[mt * 128:(mt + 1) * 128, ts],
                    reads=["v16"], writes=["v16s"])
                dma(lambda mt=mt, ts=ts: v16l[:],
                    lambda mt=mt, ts=ts: v16l_hbm[mt * 128:(mt + 1) * 128, ts],
                    reads=["v16"], writes=["v16s"])
        if _on("P1"):
            for kt in range(NKT):
                pp = kt % 2
                for i, (hbm, vdst) in enumerate(((v16h_hbm, Vp_hi),
                                                 (v16l_hbm, Vp_lo))):
                    dma_t(lambda kt=kt, hbm=hbm:
                          hbm[:, kt * 128:(kt + 1) * 128],
                          lambda pp=pp, i=i: vstg[:, pp, i, :],
                          writes=[f"vstg{pp}{i}"],
                          waits=rd("v16s") + wr(f"vstg{pp}{i}"))

                    def vslot(e, kt=kt, pp=pp, i=i, vdst=vdst):
                        return e.tensor_copy(
                            vdst[:, kt, :, 0:64],
                            vstg[:, pp, i, :].rearrange(
                                "p (a b) -> p a b", a=4))
                    comp("gpsimd", vslot, reads=[f"vstg{pp}{i}"],
                         writes=["vp"], waits=rd("vaug"))

        esP1.close()
        P.barrier()
        esP2 = contextlib.ExitStack()
        sbP2 = mk_sb(esP2)
        wqp_hi = sbP2("wqp_hi", [128, 2, 8, 128], BF16)
        wqp_lo = sbP2("wqp_lo", [128, 2, 8, 128], BF16)
        wo_hi = sbP2("wo_hi", [128, 8, D], BF16)
        wo_lo = sbP2("wo_lo", [128, 8, D], BF16)

        def wq_load(pair):
            hA, hB = PAIRS[pair]
            buf = pair % 2
            for hf, hd in ((0, hA), (1, hB)):
                for hbm, dst in ((wqs_hi_h, wqp_hi), (wqs_lo_h, wqp_lo)):
                    dma(lambda hbm=hbm, hd=hd:
                        hbm[:, hd * 64:(hd + 1) * 64]
                        .rearrange("(cc p) d -> p cc d", p=128),
                        lambda dst=dst, buf=buf, hf=hf:
                        dst[:, buf, :, hf * 64:hf * 64 + 64],
                        writes=[f"wqp{buf}"],
                        waits=rd("wscr") + wr(f"wqp{buf}"))
        for hbm, sbuf in ((wos_hi_h, wo_hi), (wos_lo_h, wo_lo)):
            for pr, (hA, hB) in enumerate(PAIRS):
                for hf, hd in ((0, hA), (1, hB)):
                    dma(lambda hbm=hbm, hd=hd: hbm[hd * 64:(hd + 1) * 64, :],
                        lambda sbuf=sbuf, pr=pr, hf=hf:
                            sbuf[hf * 64:(hf + 1) * 64, pr, :],
                        writes=["wo"], waits=rd("wscr"))

        SCL = float(HD) ** -0.5

        def q_proj(pair):
            buf = pair % 2
            sl = pair % 2

            def q_mm(e, buf=buf):
                for cc in range(8):
                    for i3, (l, r) in enumerate(mm3_list(
                            wqp_hi[:, buf, cc, :], wqp_lo[:, buf, cc, :],
                            xfm_hi[:, cc, :], xfm_lo[:, cc, :])):
                        e.matmul(ps_q[:], l, r,
                                 start=(cc == 0 and i3 == 0),
                                 stop=(cc == 7 and i3 == 2))
                return None
            comp("tensor", q_mm, reads=["xfm", f"wqp{buf}"],
                 writes=["ps_q"],
                 waits=wr(f"qphi{sl}", f"qplov{sl}", f"qplo{sl}"))
            if pair + 2 < NP_:
                wq_load(pair + 2)

            def q_hi_ev(e, sl=sl):
                e.activation(Qstk[0:64, sl, 0, :], ps_q[0:64, :], ACT.Copy,
                             scale=SCL)
                e.activation(Qstk[64:128, sl, 0, :], ps_q[0:64, :], ACT.Copy,
                             scale=SCL)
                e.activation(Qstk[64:128, sl, 1, :], ps_q[64:128, :],
                             ACT.Copy, scale=SCL)
                return e.activation(Qstk[0:64, sl, 1, :], ps_q[64:128, :],
                                    ACT.Copy, scale=SCL)
            comp("scalar", q_hi_ev, writes=[f"qphi{sl}"],
                 waits=rd("ps_q"))

            def q_lo_ev(e, sl=sl):
                e.scalar_tensor_tensor(
                    Qlo2[0:64, sl, 0, :], ps_q[0:64, :], SCL,
                    Qstk[0:64, sl, 0, :], ALU.mult, ALU.subtract)
                return e.scalar_tensor_tensor(
                    Qlo2[64:128, sl, 1, :], ps_q[64:128, :], SCL,
                    Qstk[64:128, sl, 1, :], ALU.mult, ALU.subtract)
            comp("vector", q_lo_ev, reads=["ps_q"],
                 writes=[f"qplov{sl}"], waits=rd(f"qphi{sl}"))

            def q_lo_rep(e, sl=sl):
                e.tensor_copy(Qlo2[64:128, sl, 0, :], Qlo2[0:64, sl, 0, :])
                return e.tensor_copy(Qlo2[0:64, sl, 1, :],
                                     Qlo2[64:128, sl, 1, :])
            comp("gpsimd", q_lo_rep, reads=[f"qplov{sl}"],
                 writes=[f"qplo{sl}"])

        PVLAG = 2
        for qc in range(NQC if _on("P2") else 0):
            for tt in range(4):
                x_split_store(x_own, xo16h, xo16l, (qc * 4 + tt) * 128)
            xfm_load(xo16h, xo16l, qc * 512)
            wq_load(0)
            wq_load(1)
            q_proj(0)
            q_proj(1)
            def finalize(fp, qc=qc):
                for hf, (po, pon) in enumerate(((ps_o, "ps_o"),
                                                (ps_ob, "ps_ob"))):
                    def recip(e, po=po):
                        e.reciprocal(rden_r[:], po[64:65, :])
                        e.tensor_copy(rd16h, rden_r[:])
                        return e.tensor_tensor(rd16l, rden_r[:],
                                               rd16h, ALU.subtract)
                    comp("vector", recip, reads=[pon], writes=["rden"])

                    def d_mm(e):
                        e.matmul(ps_rt[0:64, :], ones64[:], rd16h,
                                 start=True, stop=False)
                        return e.matmul(ps_rt[0:64, :], ones64[:],
                                        rd16l, start=False, stop=True)
                    comp("tensor", d_mm, reads=["rden"], writes=["ps_rt"],
                         waits=rd("ones64"))
                    comp("vector",
                         lambda e: e.tensor_copy(D_sb[:], ps_rt[0:64, :]),
                         reads=["ps_rt"], writes=["D_sb"])

                    def n_ev(e, hf=hf, po=po):
                        return e.tensor_tensor(
                            attn_st[hf * 64:hf * 64 + 64, :],
                            po[0:64, :], D_sb[:], ALU.mult)
                    comp("vector", n_ev, reads=[pon, "D_sb"],
                         writes=["atst"])

                def pair_ev(e, fp=fp):
                    e.tensor_copy(attn_hi[:, fp, :], attn_st[:])
                    return e.tensor_tensor(attn_lo[:, fp, :], attn_st[:],
                                           attn_hi[:, fp, :], ALU.subtract)
                comp("vector", pair_ev, reads=["atst"], writes=[f"at{fp}"])
                if DEBUG_TAPS and qc == 0 and fp == 0:
                    dma(lambda: attn_st[:], lambda: dbg_at[:], reads=["at0"])
                    dma(lambda: rden_r[:], lambda: dbg_rd[:],
                        reads=["rden"])

            for pair in range(NP_):
                hA, hB = PAIRS[pair]
                gt = 0 if hA < 8 else 1
                sl = pair % 2

                def emit_pv(kt, gt=gt):
                    par = kt % 3

                    def pv_mm(e, gt=gt, kt=kt, par=par):
                        pvc = ((Vp_hi, p16h, 66), (Vp_lo, p16h, 64),
                               (Vp_hi, p16l, 66))
                        for i3, (vv, pp, w) in enumerate(pvc):
                            e.matmul(ps_o[0:w, :],
                                     vv[:, kt, gt * 2, 0:w],
                                     pp[:, par, 0, :],
                                     start=(kt == 0 and i3 == 0),
                                     stop=(kt == NKT - 1 and i3 == 2),
                                     tile_position=(0, 0))
                            e.matmul(ps_ob[0:w, :],
                                     vv[:, kt, gt * 2 + 1, 0:w],
                                     pp[:, par, 1, :],
                                     start=(kt == 0 and i3 == 0),
                                     stop=(kt == NKT - 1 and i3 == 2),
                                     tile_position=(0, 0))
                        return None
                    extra = rd("vp")
                    if kt == 0:
                        extra += wr("ps_o", "ps_ob")
                    wrts = ["ps_o", "ps_ob"] if kt == NKT - 1 else []
                    comp("tensor", pv_mm,
                         reads=[f"ph{par}", f"plv{par}", f"plg{par}"],
                         writes=wrts, waits=extra)

                for kt in range(NKT):
                    par = kt % 3
                    pb2 = kt % 2
                    bk = kt % 2
                    sA, sAn = (ps_a, "ps_a") if bk == 0 else (ps_b, "ps_b")
                    sB, sBn = (ps_c, "ps_c") if bk == 0 else (ps_d, "ps_d")
                    ks = slice(kt * 128, (kt + 1) * 128)

                    def sc_mm(e, gt=gt, sl=sl, ks=ks, sA=sA, sB=sB):
                        e.matmul(sA[:], K_st[:, gt * 2, ks],
                                 Qstk[:, sl, 0, :], start=True, stop=False)
                        e.matmul(sB[:], K_st[:, gt * 2 + 1, ks],
                                 Qstk[:, sl, 1, :], start=True, stop=False)
                        e.matmul(sA[:], K_st[0:64, gt * 2, ks],
                                 Qlo2[0:64, sl, 0, :], start=False,
                                 stop=True, tile_position=(0, 0))
                        e.matmul(sB[:], K_st[0:64, gt * 2 + 1, ks],
                                 Qlo2[0:64, sl, 1, :], start=False,
                                 stop=True, tile_position=(0, 0))
                        return None
                    extra = []
                    if kt == 0:
                        extra = rd("kst0", "kst1")
                    comp("tensor", sc_mm, reads=[f"qphi{sl}", f"qplo{sl}"],
                         writes=[sAn, sBn], waits=extra)

                    def expo(e, sA=sA, sB=sB, par=par, pb2=pb2):
                        e.activation(p_buf[:, pb2, 0, :], sA[:], ACT.Exp)
                        e.activation(p16h[:, par, 0, :], sA[:], ACT.Exp)
                        e.activation(p_buf[:, pb2, 1, :], sB[:], ACT.Exp)
                        return e.activation(p16h[:, par, 1, :], sB[:],
                                            ACT.Exp)
                    comp("scalar", expo, reads=[sAn, sBn],
                         writes=[f"pbf{pb2}", f"pbg{pb2}", f"ph{par}"])

                    def pspl_sub0(e, par=par, pb2=pb2):
                        return e.tensor_tensor(p16l[:, par, 0, :],
                                               p_buf[:, pb2, 0, :],
                                               p16h[:, par, 0, :],
                                               ALU.subtract)
                    comp("vector", pspl_sub0,
                         reads=[f"pbf{pb2}", f"ph{par}"],
                         writes=[f"plv{par}"])

                    def pspl_sub1(e, par=par, pb2=pb2):
                        return e.tensor_tensor(p16l[:, par, 1, :],
                                               p_buf[:, pb2, 1, :],
                                               p16h[:, par, 1, :],
                                               ALU.subtract)
                    comp("gpsimd", pspl_sub1,
                         reads=[f"pbg{pb2}", f"ph{par}"],
                         writes=[f"plg{par}"])

                    if kt == 1 and pair > 0:
                        finalize(pair - 1)
                        if pair + 1 < NP_:
                            q_proj(pair + 1)
                    if kt % 2 == 1:
                        drain_p3a(1)
                    if kt >= PVLAG:
                        emit_pv(kt - PVLAG)
                for kt in range(NKT - PVLAG, NKT):
                    emit_pv(kt)
                if DEBUG_TAPS and qc == 0 and pair == 0:
                    dma(lambda: Qstk[:, 0, 0, :], lambda: dbg_qph[:],
                        reads=["qphi0"])
                    dma(lambda: Qlo2[:, 0, 0, :], lambda: dbg_qpl[:],
                        reads=["qplo0"])
                    dma(lambda: p16h[:, 1, :, :], lambda: dbg_ph[:],
                        waits=rd("ph1"))
                    dma(lambda: p_buf[:, 1, :, :], lambda: dbg_pf[:],
                        waits=rd("pbf1"))
                    dma(lambda: K_st[:, 0:2, :], lambda: dbg_kh[:],
                        waits=rd("kst0", "kst1"))
            finalize(NP_ - 1)


            for qs in range(4):
                j = qc * 4 + qs
                dma(lambda r0=j * 128: x_own[r0:r0 + 128, :],
                    lambda: x_stage[:], writes=["xs0"])
                for dc in range(2):
                    def wo_mm(e, qs=qs, dc=dc):
                        qsl = slice(qs * 128, (qs + 1) * 128)
                        dsl = slice(dc * 512, (dc + 1) * 512)
                        for pr in range(NP_):
                            for i3, (l, r) in enumerate(mm3_list(
                                    attn_hi[:, pr, qsl], attn_lo[:, pr, qsl],
                                    wo_hi[:, pr, dsl], wo_lo[:, pr, dsl])):
                                e.matmul(ps_q[:], l, r,
                                         start=(pr == 0 and i3 == 0),
                                         stop=(pr == NP_ - 1 and i3 == 2))
                        return None
                    comp("tensor", wo_mm,
                         reads=[f"at{pr}" for pr in range(NP_)],
                         writes=["ps_q"], waits=rd("wo"))

                    def y_ev(e, dc=dc):
                        dsl = slice(dc * 512, (dc + 1) * 512)
                        return e.tensor_tensor(y_sb[:, dsl], x_stage[:, dsl],
                                               ps_q[:], ALU.add)
                    comp("vector", y_ev, reads=["ps_q", "xs0"],
                         writes=[f"ydc{dc}"])

                if DEBUG_TAPS and j == 0:
                    dma(lambda: y_sb[:], lambda: dbg_y[:],
                        waits=rd("ydc0", "ydc1"))
                emit_ln("x1s0")
                dma(lambda: x1_sb[:],
                    lambda r0=j * 128: x1_hbm[r0:r0 + 128, :],
                    reads=["x1s0"], writes=["x1h"])
            if _on("P3b"):
                for rb in range(4):
                    for s2 in range(2):
                        _p3a_tile(_qt_n[0], x1_hbm, x1q16h, x1q16l,
                                  (qc * 4 + rb) * 128, s2 * 512, 512,
                                  stok="x1qs", lw=("x1h",))
                        _qt_n[0] += 1

        drain_p3a(len(_p3a))

        esP2.close()
        esA.close()
        P.barrier()
        P.barrier()
        esP3b = contextlib.ExitStack()
        sbP3b = mk_sb(esP3b)
        x1q_hi = sbP3b("x1q_hi_s", [128, 8, 1024], BF16)
        x1q_lo = sbP3b("x1q_lo_s", [128, 8, 1024], BF16)
        wgq_hi = sbP3b("wgq_hi_s", [128, 2, 8, 512], BF16)
        wgq_lo = sbP3b("wgq_lo_s", [128, 2, 8, 512], BF16)
        wuq_hi = sbP3b("wuq_hi_s", [128, 2, 8, 512], BF16)
        wuq_lo = sbP3b("wuq_lo_s", [128, 2, 8, 512], BF16)
        wdq_hi = sbP3b("wdq_hi_s", [128, 2, 4, 1024], BF16)
        wdq_lo = sbP3b("wdq_lo_s", [128, 2, 4, 1024], BF16)
        dacc = sbP3b("dacc", [128, FTT, D], F32)
        gs_sb = sbP3b("gs_sb", [128, 512], F32)
        h_sb = sbP3b("h_sb", [128, 512], F32)
        hq_hi = sbP3b("hq_hi", [128, 512], BF16)
        hq_lo = sbP3b("hq_lo", [128, 512], BF16)
        hqf_hi = sbP3b("hqf_hi", [128, 4, 1024], BF16)
        hqf_lo = sbP3b("hqf_lo", [128, 4, 1024], BF16)

        def _hq_stages(nn):
            subs = [(s * 512, min(512, nn - s * 512))
                    for s in range((nn + 511) // 512)]
            for b0, w in subs:
                comp("vector",
                     lambda e, b0=b0, w=w:
                     q_stage1(e, h_sb[:, b0:b0 + w], w),
                     reads=["h_sb"], writes=["qstg"])
                comp("vector",
                     lambda e, b0=b0, w=w:
                     q_stage2(e, h_sb[:, b0:b0 + w], w),
                     reads=["qstg"], writes=["qstg"])
                def s3(e, b0=b0, w=w):
                    q_stage3(e, h_sb[:, b0:b0 + w], w, None, None)
                    return q_split_g(e, w, hq_hi[:, b0:b0 + w],
                                     hq_lo[:, b0:b0 + w])
                comp("vector", s3, reads=["qstg", "h_sb"], writes=["hq"])

        for tc in range(FTC if _on("P3b") else 0):
            for cc in range(8):
                dma_t(lambda tc=tc, cc=cc:
                      x1q16h[tc * 1024:(tc + 1) * 1024,
                             cc * 128:(cc + 1) * 128],
                      lambda cc=cc: x1q_hi[:, cc, :],
                      writes=["x1q"], waits=rd("x1qs") + wr("x1q"))
                dma_t(lambda tc=tc, cc=cc:
                      x1q16l[tc * 1024:(tc + 1) * 1024,
                             cc * 128:(cc + 1) * 128],
                      lambda cc=cc: x1q_lo[:, cc, :],
                      writes=["x1q"])

            dn_pend = []
            for nci, (n0, nn) in enumerate(NCH):
                nnt = nn // 128
                bp = nci % 2
                for cc in range(8):
                    dma_t(lambda n0=n0, nn=nn, cc=cc:
                          wgq_hi_h[n0:n0 + nn, cc * 128:(cc + 1) * 128],
                          lambda nn=nn, cc=cc, bp=bp:
                          wgq_hi[:, bp, cc, 0:nn],
                          writes=[f"wgq{bp}"],
                          waits=rd("scr") + wr(f"wgq{bp}"))
                    dma_t(lambda n0=n0, nn=nn, cc=cc:
                          wgq_lo_h[n0:n0 + nn, cc * 128:(cc + 1) * 128],
                          lambda nn=nn, cc=cc, bp=bp:
                          wgq_lo[:, bp, cc, 0:nn],
                          writes=[f"wgq{bp}"])
                    dma_t(lambda n0=n0, nn=nn, cc=cc:
                          wuq_hi_h[n0:n0 + nn, cc * 128:(cc + 1) * 128],
                          lambda nn=nn, cc=cc, bp=bp:
                          wuq_hi[:, bp, cc, 0:nn],
                          writes=[f"wgq{bp}"])
                    dma_t(lambda n0=n0, nn=nn, cc=cc:
                          wuq_lo_h[n0:n0 + nn, cc * 128:(cc + 1) * 128],
                          lambda nn=nn, cc=cc, bp=bp:
                          wuq_lo[:, bp, cc, 0:nn],
                          writes=[f"wgq{bp}"])
                for nt in range(nnt):
                    dma_t(lambda n0=n0, nt=nt:
                          wdq_hi_h[:, n0 + nt * 128:n0 + (nt + 1) * 128],
                          lambda nt=nt, bp=bp: wdq_hi[:, bp, nt, :],
                          writes=[f"wdq{bp}"],
                          waits=rd("scr") + wr(f"wdq{bp}"))
                    dma_t(lambda n0=n0, nt=nt:
                          wdq_lo_h[:, n0 + nt * 128:n0 + (nt + 1) * 128],
                          lambda nt=nt, bp=bp: wdq_lo[:, bp, nt, :],
                          writes=[f"wdq{bp}"])

                for tt in range(FTT):
                    par = tt % 2
                    g_ps, gn = ((ps_a, "ps_a") if par == 0
                                else (ps_b, "ps_b"))
                    u_ps, un = ((ps_c, "ps_c") if par == 0
                                else (ps_d, "ps_d"))
                    tsl = slice(tt * 128, (tt + 1) * 128)

                    def gu_mm(e, g_ps=g_ps, u_ps=u_ps, tsl=tsl, nn=nn,
                              bp=bp):
                        for cc in range(8):
                            for i3, (l, r) in enumerate(mm3_list(
                                    x1q_hi[:, cc, tsl],
                                    x1q_lo[:, cc, tsl],
                                    wgq_hi[:, bp, cc, 0:nn],
                                    wgq_lo[:, bp, cc, 0:nn])):
                                e.matmul(g_ps[:, 0:nn], l, r,
                                         start=(cc == 0 and i3 == 0),
                                         stop=(cc == 7 and i3 == 2))
                            for i3, (l, r) in enumerate(mm3_list(
                                    x1q_hi[:, cc, tsl],
                                    x1q_lo[:, cc, tsl],
                                    wuq_hi[:, bp, cc, 0:nn],
                                    wuq_lo[:, bp, cc, 0:nn])):
                                e.matmul(u_ps[:, 0:nn], l, r,
                                         start=(cc == 0 and i3 == 0),
                                         stop=(cc == 7 and i3 == 2))
                        return None
                    comp("tensor", gu_mm, reads=[f"wgq{bp}", "x1q"],
                         writes=[gn, un])

                    def silu(e, g_ps=g_ps, nn=nn):
                        return e.activation(gs_sb[:, 0:nn], g_ps[:, 0:nn],
                                            ACT.Silu)
                    comp("scalar", silu, reads=[gn], writes=["gs"])

                    def hmul(e, u_ps=u_ps, nn=nn):
                        return e.tensor_tensor(h_sb[:, 0:nn],
                                               gs_sb[:, 0:nn],
                                               u_ps[:, 0:nn], ALU.mult)
                    comp("vector", hmul, reads=[un, "gs"],
                         writes=["h_sb"])

                    _hq_stages(nn)

                    dma(lambda nn=nn: hq_hi[:, 0:nn],
                        lambda tt=tt, nn=nn:
                        hq16h[tt * 128:(tt + 1) * 128, 0:nn],
                        reads=["hq"], writes=["hqs"])
                    dma(lambda nn=nn: hq_lo[:, 0:nn],
                        lambda tt=tt, nn=nn:
                        hq16l[tt * 128:(tt + 1) * 128, 0:nn],
                        reads=["hq"], writes=["hqs"])

                    for _ in range(2):
                        if dn_pend:
                            dn_pend.pop(0)()
                while dn_pend:
                    dn_pend.pop(0)()

                for nt in range(nnt):
                    dma_t(lambda nt=nt:
                          hq16h[:, nt * 128:(nt + 1) * 128],
                          lambda nt=nt: hqf_hi[:, nt, :],
                          reads=["hqs"], writes=["hqf"])
                    dma_t(lambda nt=nt:
                          hq16l[:, nt * 128:(nt + 1) * 128],
                          lambda nt=nt: hqf_lo[:, nt, :],
                          reads=["hqs"], writes=["hqf"])

                def mk_dn(tt, dc, nci=nci, nnt=nnt, bp=bp):
                    def emit():
                        tsl = slice(tt * 128, (tt + 1) * 128)
                        po, pon = ((ps_o, "ps_o") if dc == 0
                                   else (ps_ob, "ps_ob"))

                        def dn_mm(e, tsl=tsl, dc=dc, nnt=nnt, bp=bp,
                                  po=po):
                            dsl = slice(dc * 512, (dc + 1) * 512)
                            for nt in range(nnt):
                                for i3, (l, r) in enumerate(mm3_list(
                                        hqf_hi[:, nt, tsl],
                                        hqf_lo[:, nt, tsl],
                                        wdq_hi[:, bp, nt, dsl],
                                        wdq_lo[:, bp, nt, dsl])):
                                    e.matmul(po[:], l, r,
                                             start=(nt == 0 and i3 == 0),
                                             stop=(nt == nnt - 1
                                                   and i3 == 2))
                            return None
                        rds = (["hqf", f"wdq{bp}"]
                               if (tt == FTT - 1 and dc == 1) else [])
                        comp("tensor", dn_mm, reads=rds, writes=[pon],
                             waits=rd("hqf", f"wdq{bp}"))

                        def d_add(e, tt=tt, dc=dc, nci=nci, po=po):
                            dsl = slice(dc * 512, (dc + 1) * 512)
                            if nci == 0:
                                return e.tensor_copy(dacc[:, tt, dsl],
                                                     po[:])
                            return e.tensor_tensor(dacc[:, tt, dsl],
                                                   dacc[:, tt, dsl],
                                                   po[:], ALU.add)
                        comp("vector", d_add, reads=[pon],
                             writes=[f"da{tt}_{dc}"])
                    return emit
                dn_pend = [mk_dn(tt, dc)
                           for tt in range(FTT) for dc in range(2)]
            while dn_pend:
                dn_pend.pop(0)()

            for tt in range(FTT):
                j = tc * FTT + tt
                r0 = j * 128
                dma(lambda r0=r0: x1_hbm[r0:r0 + 128, :],
                    lambda: x_stage[:], writes=["xs0"], waits=rd("x1h"))

                def y2(e, tt=tt):
                    return e.tensor_tensor(y_sb[:], x_stage[:],
                                           dacc[:, tt, :], ALU.add)
                comp("vector", y2,
                     reads=["xs0", f"da{tt}_0", f"da{tt}_1"],
                     writes=["ydc0", "ydc1"])

                emit_ln("x1s0")
                dma(lambda: x1_sb[:],
                    lambda r0=r0: out_h[r0:r0 + 128, :], reads=["x1s0"])

        final = [("dm", P.ctr["dm"]), ("da", P.ctr["da"])]
        P.op("sync", lambda e: None, final, noinc=True)

        sems = {s: es.enter_context(nc.semaphore(s)) for s in SEMS}
        blk = es.enter_context(nc.Block())

        @blk.sync
        def _(e):
            P.emit("sync", e, sems)

        @blk.tensor
        def _(e):
            P.emit("tensor", e, sems)

        @blk.vector
        def _(e):
            P.emit("vector", e, sems)

        @blk.scalar
        def _(e):
            P.emit("scalar", e, sems)

        @blk.gpsimd
        def _(e):
            P.emit("gpsimd", e, sems)

        esP3b.close()

    return nc


def host_consts():
    return {}


_PROG = None


def _get_prog():
    global _PROG
    if _PROG is None:
        _PROG = build_program()
    return _PROG


def kernel(L0_attn_x, L0_attn_Wq, L0_attn_Wk, L0_attn_Wv, L0_attn_Wo,
           L0_ffn_w_gate, L0_ffn_w_up, L0_ffn_w_down):
    x = np.ascontiguousarray(np.asarray(L0_attn_x, dtype=np.float32))
    wq = np.ascontiguousarray(np.asarray(L0_attn_Wq, dtype=np.float32))
    wk = np.ascontiguousarray(np.asarray(L0_attn_Wk, dtype=np.float32))
    wv = np.ascontiguousarray(np.asarray(L0_attn_Wv, dtype=np.float32))
    wo = np.ascontiguousarray(np.asarray(L0_attn_Wo, dtype=np.float32))
    wg = np.ascontiguousarray(np.asarray(L0_ffn_w_gate, dtype=np.float32))
    wu = np.ascontiguousarray(np.asarray(L0_ffn_w_up, dtype=np.float32))
    wd = np.ascontiguousarray(np.asarray(L0_ffn_w_down, dtype=np.float32))

    nc = _get_prog()
    in_maps = []
    for c in range(NCORES):
        b, half = c // 2, c % 2
        in_maps.append({
            "x_all": x[b],
            "x_own": np.ascontiguousarray(x[b, half * TQ:(half + 1) * TQ]),
            "wq": wq, "wk": wk, "wv": wv, "wo": wo,
            "wg": wg, "wu": wu, "wd": wd,
        })
    res = run_bass_kernel_spmd(nc, in_maps, core_ids=list(range(NCORES)))
    out = np.empty((B, T, D), dtype=np.float32)
    for c in range(NCORES):
        b, half = c // 2, c % 2
        out[b, half * TQ:(half + 1) * TQ] = np.asarray(res.results[c]["out"])
    return out


# version 22 (final)
# speedup vs baseline: 1.0410x; 1.0077x over previous
import contextlib

import numpy as np
import ml_dtypes

import concourse.bass as bass
import concourse.mybir as mybir
from concourse.bass_utils import run_bass_kernel_spmd

F32 = mybir.dt.float32
BF16 = mybir.dt.bfloat16
ALU = mybir.AluOpType
ACT = mybir.ActivationFunctionType
AX = mybir.AxisListType

B, T, D = 4, 4096, 1024
TQ = T // 2
H, HD, G = 16, 64, 4
DK = G * HD
DF = 2816
NCORES = 8
EPS = 1e-5

PAIRS = [(0, 4), (1, 5), (2, 6), (3, 7), (8, 12), (9, 13), (10, 14), (11, 15)]
NP_ = len(PAIRS)
NQC = TQ // 512
NKT = T // 128
FTC = 2
FTT = 8
NCH = [(0, 512), (512, 512), (1024, 512), (1536, 512), (2048, 512), (2560, 256)]

STREAMS = {"sync": "dm", "tensor": "tp", "vector": "vp", "scalar": "sp",
           "gpsimd": "gp"}
SEMS = ("dm", "da", "tp", "vp", "sp", "gp", "hzv", "hzs", "hzt", "hzg")

DEBUG_TAPS = False
DEBUG_OUT = False
STOP_AFTER = None
_PHASES = ["P0", "P1", "P2", "P3a", "P3b"]


def _on(ph):
    return STOP_AFTER is None or _PHASES.index(ph) <= _PHASES.index(STOP_AFTER)


class Plan:
    def __init__(self):
        self.ops = []
        self.ctr = {s: 0 for s in SEMS}
        self.pending = {}

    def barrier(self):
        snap = [(s, v) for s, v in self.ctr.items() if v > 0]
        for eng in STREAMS:
            self.pending[eng] = list(snap)

    def op(self, engine, body, waits, dma=False, noinc=False):
        sem = STREAMS[engine]
        dma_sem = None
        if dma:
            dma_sem = "da" if engine == "scalar" else "dm"
        waits = list(waits) + self.pending.pop(engine, [])
        w = []
        for sv in waits:
            if sv is None:
                continue
            s, v = sv
            if v <= 0:
                continue
            if s not in ("dm", "da") and s == sem:
                continue
            w.append((s, v))
        self.ops.append((engine, w, body, dma_sem, noinc))
        if noinc:
            return None
        if dma_sem:
            self.ctr[dma_sem] += 16
            return (dma_sem, self.ctr[dma_sem])
        self.ctr[sem] += 1
        return (sem, self.ctr[sem])

    def emit(self, engine, eng, sems):
        hz = {"vector": "hzv", "scalar": "hzs", "tensor": "hzt",
              "gpsimd": "hzg"}.get(engine, "hzv")
        rec = _Rec(eng, sems[hz], sems[STREAMS[engine]])
        hw = {}
        for (oe, waits, body, dma_sem, noinc) in self.ops:
            if oe != engine:
                continue
            for (s, v) in waits:
                if hw.get(s, -1) >= v:
                    continue
                eng.wait_ge(sems[s], v)
                hw[s] = v
            rec.last = None
            body(rec)
            if noinc:
                continue
            if dma_sem:
                rec.last.then_inc(sems[dma_sem], 16)
            elif rec.last is not None:
                rec.last.then_inc(sems[STREAMS[engine]], 1)
                rec.scount += 1
                rec.inced = rec.last
            else:
                eng.nop().then_inc(sems[STREAMS[engine]], 1)
                rec.scount += 1


class _Rec:

    _OPS = frozenset((
        "tensor_copy", "tensor_tensor", "tensor_add", "tensor_sub",
        "tensor_mul", "tensor_max", "tensor_scalar", "tensor_single_scalar",
        "tensor_scalar_add", "tensor_scalar_sub", "tensor_scalar_mul",
        "tensor_scalar_max", "tensor_scalar_min", "tensor_relu",
        "tensor_reduce", "reduce_max", "reduce_sum", "scalar_tensor_tensor",
        "reciprocal", "memset", "memzero", "activation", "sign", "square",
        "sqrt", "mul", "copy", "add", "bn_stats", "bn_aggr", "select",
        "transpose", "iota",
    ))

    def __init__(self, eng, hz_sem, stream_sem):
        self._e = eng
        self._hz = hz_sem
        self._hzc = 0
        self._ssem = stream_sem
        self.scount = 0
        self.inced = None
        self.last = None
        self.lastf = None
        self._dirty_w = set()
        self._dirty_r = set()

    @staticmethod
    def _apname(a):
        t = getattr(a, "tensor", None)
        if t is None:
            return None
        n = getattr(t, "name", None)
        return n if isinstance(n, str) else None

    def _fence(self):
        if self.lastf is not None:
            if self.lastf is self.inced:
                self._e.wait_ge(self._ssem, self.scount)
            else:
                self._hzc += 1
                self.lastf.then_inc(self._hz, 1)
                self._e.wait_ge(self._hz, self._hzc)
        self._dirty_w.clear()
        self._dirty_r.clear()

    def __getattr__(self, k):
        a = getattr(self._e, k)
        if not callable(a):
            return a
        if k not in self._OPS:
            def w0(*args, **kw):
                r = a(*args, **kw)
                if hasattr(r, "then_inc"):
                    self.last = r
                return r
            return w0

        def w(*args, **kw):
            outs = set()
            ins = set()
            for i, ar in enumerate(args):
                n = self._apname(ar)
                if n:
                    (outs if i == 0 else ins).add(n)
            for key, ar in kw.items():
                n = self._apname(ar)
                if n:
                    (outs if key == "accum_out" else ins).add(n)
            if (ins & self._dirty_w) or (outs & (self._dirty_w
                                                | self._dirty_r)):
                self._fence()
            r = a(*args, **kw)
            if hasattr(r, "then_inc"):
                self.last = r
                self.lastf = r
            self._dirty_w |= outs
            self._dirty_r |= ins
            return r
        return w


class Tok:
    __slots__ = ("w", "r")

    def __init__(self):
        self.w = None
        self.r = None


def mm3_list(hi_l, lo_l, hi_r, lo_r):
    return ((hi_l, hi_r), (hi_l, lo_r), (lo_l, hi_r))


def build_program():
    nc = bass.Bass("TRN2", target_bir_lowering=False,
                   detect_race_conditions=False)

    x_all = nc.dram_tensor("x_all", [T, D], F32, kind="ExternalInput")
    x_own = nc.dram_tensor("x_own", [TQ, D], F32, kind="ExternalInput")
    wq_h = nc.dram_tensor("wq", [D, D], F32, kind="ExternalInput")
    wk_h = nc.dram_tensor("wk", [D, DK], F32, kind="ExternalInput")
    wv_h = nc.dram_tensor("wv", [D, DK], F32, kind="ExternalInput")
    wo_h = nc.dram_tensor("wo", [D, D], F32, kind="ExternalInput")
    wg_h = nc.dram_tensor("wg", [DF, D], F32, kind="ExternalInput")
    wu_h = nc.dram_tensor("wu", [DF, D], F32, kind="ExternalInput")
    wd_h = nc.dram_tensor("wd", [D, DF], F32, kind="ExternalInput")
    out_h = nc.dram_tensor("out", [TQ, D], F32, kind="ExternalOutput")
    _k = dict(kind="ExternalOutput") if DEBUG_OUT else {}

    def _k2():
        return dict(kind="ExternalOutput")

    x1_hbm = nc.dram_tensor("x1_hbm", [TQ, D], F32, **_k)
    xa16h = nc.dram_tensor("xa16h", [T, D], BF16, **_k)
    xa16l = nc.dram_tensor("xa16l", [T, D], BF16)
    xo16h = nc.dram_tensor("xo16h", [TQ, D], BF16)
    xo16l = nc.dram_tensor("xo16l", [TQ, D], BF16)
    v16h_hbm = nc.dram_tensor("v16h_hbm", [DK, T], BF16, **_k)
    v16l_hbm = nc.dram_tensor("v16l_hbm", [DK, T], BF16)
    wqs_hi_h = nc.dram_tensor("wqs_hi", [D, D], BF16, **_k)
    wqs_lo_h = nc.dram_tensor("wqs_lo", [D, D], BF16)
    wos_hi_h = nc.dram_tensor("wos_hi", [D, D], BF16)
    wos_lo_h = nc.dram_tensor("wos_lo", [D, D], BF16)
    wgq_hi_h = nc.dram_tensor("wgq_hi", [DF, D], BF16, **_k)
    wgq_lo_h = nc.dram_tensor("wgq_lo", [DF, D], BF16)
    wuq_hi_h = nc.dram_tensor("wuq_hi", [DF, D], BF16)
    wuq_lo_h = nc.dram_tensor("wuq_lo", [DF, D], BF16)
    wdq_hi_h = nc.dram_tensor("wdq_hi", [D, DF], BF16)
    wdq_lo_h = nc.dram_tensor("wdq_lo", [D, DF], BF16)
    x1q16h = nc.dram_tensor("x1q16h", [TQ, D], BF16)
    x1q16l = nc.dram_tensor("x1q16l", [TQ, D], BF16)
    hq16h = nc.dram_tensor("hq16h", [1024, 512], BF16)
    hq16l = nc.dram_tensor("hq16l", [1024, 512], BF16)
    if DEBUG_TAPS:
        dbg_qph = nc.dram_tensor("dbg_qph", [128, 512], BF16, **_k2())
        dbg_qpl = nc.dram_tensor("dbg_qpl", [128, 512], BF16, **_k2())
        dbg_ph = nc.dram_tensor("dbg_ph", [128, 2, 512], BF16, **_k2())
        dbg_pf = nc.dram_tensor("dbg_pf", [128, 2, 512], F32, **_k2())
        dbg_at = nc.dram_tensor("dbg_at", [128, 512], F32, **_k2())
        dbg_rd = nc.dram_tensor("dbg_rd", [1, 512], F32, **_k2())
        dbg_y = nc.dram_tensor("dbg_y", [128, 1024], F32, **_k2())
        dbg_kh = nc.dram_tensor("dbg_kh", [128, 2, 4096], BF16, **_k2())

    es = contextlib.ExitStack()

    def mk_sb(stack):
        def f(name, shape, dtype):
            return stack.enter_context(nc.sbuf_tensor(name, shape, dtype))
        return f

    with es:
        sb = mk_sb(es)
        ones64 = sb("ones64", [1, 64], BF16)
        qA = sb("qA", [128, 512], F32)
        qB = sb("qB", [128, 512], F32)
        qC = sb("qC", [128, 512], F32)
        qsc = sb("qsc", [128, 32], F32)
        qr2 = sb("qr2", [128, 32], F32)
        x_stage = sb("x_stage", [128, D], F32)
        y_sb = sb("y_sb", [128, D], F32)
        x1_sb = sb("x1_sb", [128, D], F32)
        sq_scr = sb("sq_scr", [128, D], BF16)
        ln1 = sb("ln1", [128, 1], F32)
        ln2 = sb("ln2", [128, 1], F32)
        ln3 = sb("ln3", [128, 1], F32)
        ln4 = sb("ln4", [128, 1], F32)
        ln5 = sb("ln5", [128, 1], F32)

        ps_a = es.enter_context(nc.psum_tensor("ps_a", [128, 512], F32))
        ps_b = es.enter_context(nc.psum_tensor("ps_b", [128, 512], F32))
        ps_c = es.enter_context(nc.psum_tensor("ps_c", [128, 512], F32))
        ps_d = es.enter_context(nc.psum_tensor("ps_d", [128, 512], F32))
        ps_o = es.enter_context(nc.psum_tensor("ps_o", [128, 512], F32))
        ps_ob = es.enter_context(nc.psum_tensor("ps_ob", [128, 512], F32))
        ps_q = es.enter_context(nc.psum_tensor("ps_q", [128, 512], F32))
        ps_rt = es.enter_context(nc.psum_tensor("ps_rt", [128, 512], F32))

        P = Plan()
        toks = {}

        def tok(name):
            if name not in toks:
                toks[name] = Tok()
            return toks[name]

        def rd(*names):
            return [tok(n).w for n in names]

        def wr(*names):
            return [tok(n).r for n in names]

        def _mark(sv, reads, writes):
            for n in writes:
                tok(n).w = sv
            for n in reads:
                tok(n).r = sv
            return sv

        def dma(src_fn, dst_fn, reads=(), writes=(), waits=()):
            w = list(waits) + rd(*reads) + wr(*writes)
            sv = P.op("sync", lambda e, a=src_fn, b=dst_fn:
                      e.dma_start(out=b(), in_=a()), w, dma=True)
            return _mark(sv, reads, writes)

        def dma_t(src_fn, dst_fn, reads=(), writes=(), waits=()):
            w = list(waits) + rd(*reads) + wr(*writes)
            sv = P.op("scalar", lambda e, a=src_fn, b=dst_fn:
                      e.dma_start_transpose(b(), a()), w, dma=True)
            return _mark(sv, reads, writes)

        def comp(engine, body, reads=(), writes=(), waits=()):
            w = list(waits) + rd(*reads) + wr(*writes)
            sv = P.op(engine, body, w)
            return _mark(sv, reads, writes)

        def emit_ln(store_tok):
            comp("vector",
                 lambda e: e.tensor_reduce(ln1[:], y_sb[:], AX.X, ALU.add),
                 writes=["ln1"], waits=rd("ydc0", "ydc1"))
            comp("scalar",
                 lambda e: e.activation(sq_scr[:], y_sb[:], ACT.Square,
                                        accum_out=ln2[:]),
                 writes=["ln2"], waits=rd("ydc0", "ydc1"))

            def ln_var(e):
                e.tensor_scalar(ln3[:], ln1[:], 1.0 / D, None, ALU.mult)
                e.tensor_tensor(ln4[:], ln3[:], ln3[:], ALU.mult)
                e.scalar_tensor_tensor(ln4[:], ln2[:], 1.0 / D, ln4[:],
                                       ALU.mult, ALU.subtract)
                return e.tensor_scalar(ln4[:], ln4[:], EPS, None, ALU.add)
            comp("vector", ln_var, reads=["ln1", "ln2"], writes=["ln3", "ln4"])
            comp("scalar", lambda e: e.activation(ln5[:], ln4[:], ACT.Sqrt),
                 reads=["ln4"], writes=["ln5"])

            def ln_fin(e):
                e.reciprocal(ln4[:], ln5[:])
                return e.scalar_tensor_tensor(ln3[:], ln3[:], -1.0, ln4[:],
                                              ALU.mult, ALU.mult)
            comp("vector", ln_fin, reads=["ln5"], writes=["lnf"])

            def ln_app(e):
                return e.activation(x1_sb[:], y_sb[:], ACT.Identity,
                                    bias=ln3[:], scale=ln4[:])
            sv = comp("scalar", ln_app, reads=["lnf"], writes=[store_tok])
            for n in ("ydc0", "ydc1", "ln3", "ln4"):
                tok(n).r = sv

        def x_split_store(src_h, h16, l16, row0):
            dma(lambda: src_h[row0:row0 + 128, :], lambda: x_stage[:],
                writes=["xs0"])

            def split(e):
                e.tensor_copy(xt_hi[:], x_stage[:])
                return e.tensor_tensor(xt_lo[:], x_stage[:], xt_hi[:],
                                       ALU.subtract)
            comp("vector", split, reads=["xs0"], writes=["xt0"])
            dma(lambda: xt_hi[:], lambda: h16[row0:row0 + 128, :],
                reads=["xt0"], writes=["x16s"])
            dma(lambda: xt_lo[:], lambda: l16[row0:row0 + 128, :],
                reads=["xt0"], writes=["x16s"])

        def xfm_load(h16, l16, row0):
            for cc in range(8):
                dma_t(lambda h16=h16, row0=row0, cc=cc:
                      h16[row0:row0 + 512, cc * 128:(cc + 1) * 128],
                      lambda cc=cc: xfm_hi[:, cc, :],
                      writes=["xfm"], waits=rd("x16s"))
                dma_t(lambda l16=l16, row0=row0, cc=cc:
                      l16[row0:row0 + 512, cc * 128:(cc + 1) * 128],
                      lambda cc=cc: xfm_lo[:, cc, :],
                      writes=["xfm"], waits=rd("x16s"))

        esA = contextlib.ExitStack()
        sbA = mk_sb(esA)
        K_st = sbA("K_st", [128, 4, T], BF16)
        wst3 = sbA("wst3", [128, 2, 512], F32)
        q3h = sbA("q3h", [128, 2, 512], BF16)
        q3l = sbA("q3l", [128, 2, 512], BF16)
        xt_hi = sbA("xt_hi", [128, D], BF16)
        xt_lo = sbA("xt_lo", [128, D], BF16)
        xfm_hi = sbA("xfm_hi", [128, 8, 512], BF16)
        xfm_lo = sbA("xfm_lo", [128, 8, 512], BF16)
        ktmp = sbA("ktmp", [128, 512], BF16)
        Vp_hi = sbA("Vp_hi", [128, NKT, 4, 66], BF16)
        Vp_lo = sbA("Vp_lo", [128, NKT, 4, 64], BF16)
        Qstk = sbA("Qstk", [128, 2, 2, 512], BF16)
        Qlo2 = sbA("Qlo2", [128, 2, 2, 512], BF16)
        p_buf = sbA("p_buf", [128, 2, 2, 512], F32)
        p16h = sbA("p16h", [128, 3, 2, 512], BF16)
        p16l = sbA("p16l", [128, 3, 2, 512], BF16)
        attn_hi = sbA("attn_hi", [128, NP_, 512], BF16)
        attn_lo = sbA("attn_lo", [128, NP_, 512], BF16)
        attn_st = sbA("attn_st", [128, 512], F32)
        rden_r = sbA("rden_r", [1, 512], F32)
        D_sb = sbA("D_sb", [64, 512], F32)
        rd16h = D_sb[:].bitcast(BF16)[0:1, 0:512]
        rd16l = D_sb[:].bitcast(BF16)[0:1, 512:1024]
        vstg = sbA("vstg", [128, 1, 2, 256], BF16)

        esP1 = contextlib.ExitStack()
        sbP1 = mk_sb(esP1)
        wstage = sbP1("wstage", [128, 4096], F32)
        wk_hi = sbP1("wk_hi", [128, 8, DK], BF16)
        wk_lo = sbP1("wk_lo", [128, 8, DK], BF16)
        wv_hi = sbP1("wv_hi", [128, 8, DK], BF16)
        wv_lo = sbP1("wv_lo", [128, 8, DK], BF16)
        v16h = sbP1("v16h", [128, 512], BF16)
        v16l = sbP1("v16l", [128, 512], BF16)

        comp("vector", lambda e: e.memset(ones64[:], 1.0), writes=["ones64"])

        def vaug_init(e):
            e.memset(Vp_hi[:, :, :, 64:66], 0.0)
            return e.memset(Vp_hi[:, :, :, 64:65], 1.0)
        comp("vector", vaug_init, writes=["vaug"])

        for wh, shh, slh in ((wq_h, wqs_hi_h, wqs_lo_h),
                             (wo_h, wos_hi_h, wos_lo_h)):
            for hv in range(2):
                dma(lambda wh=wh, hv=hv:
                    wh[hv * 512:(hv + 1) * 512, :]
                    .rearrange("(cc p) d -> p cc d", p=128),
                    lambda: wstage[:].rearrange("p (cc d) -> p cc d", cc=4),
                    writes=["wstage"])

                def wsp(e):
                    flat = K_st[:].rearrange("p a b -> p (a b)")
                    khf = flat[:, 0:4096]
                    klf = flat[:, 8192:12288]
                    e.tensor_copy(khf, wstage[:])
                    return e.tensor_tensor(klf, wstage[:], khf,
                                           ALU.subtract)
                comp("vector", wsp, reads=["wstage"], writes=["kbuf"])

                dma(lambda: K_st[:].rearrange("p a b -> p (a b)")
                    [:, 0:4096].rearrange("p (c d) -> p c d", c=4),
                    lambda shh=shh, hv=hv:
                    shh[hv * 512:(hv + 1) * 512, :]
                    .rearrange("(cc p) d -> p cc d", p=128),
                    reads=["kbuf"], writes=["wscr"])
                dma(lambda: K_st[:].rearrange("p a b -> p (a b)")
                    [:, 8192:12288].rearrange("p (c d) -> p c d", c=4),
                    lambda slh=slh, hv=hv:
                    slh[hv * 512:(hv + 1) * 512, :]
                    .rearrange("(cc p) d -> p cc d", p=128),
                    reads=["kbuf"], writes=["wscr"])

        for wname, wh, whi, wlo in (("wk", wk_h, wk_hi, wk_lo),
                                    ("wv", wv_h, wv_hi, wv_lo)):
            st3 = wstage[:, 0:8 * DK].rearrange("p (cc d) -> p cc d", d=DK)
            dma(lambda wh=wh: wh[:].rearrange("(cc p) d -> p cc d", p=128),
                lambda st3=st3: st3, writes=["wstage"])

            def wsplit(e, whi=whi, wlo=wlo):
                flat = wstage[:, 0:8 * DK]
                hi_f = whi[:].rearrange("p c d -> p (c d)")
                lo_f = wlo[:].rearrange("p c d -> p (c d)")
                e.tensor_copy(hi_f, flat)
                return e.tensor_tensor(lo_f, flat, hi_f, ALU.subtract)
            comp("vector", wsplit, reads=["wstage"], writes=[wname])

        def q_stage1(e, src, n):
            nb = n // 16
            sc, r2 = qsc[:, 0:nb], qr2[:, 0:nb]
            src3 = src.rearrange("p (nb b) -> p nb b", b=16)
            e.tensor_reduce(sc, src3, AX.X, ALU.max,
                            apply_absolute_value=True)
            e.reciprocal(r2, sc)
            e.tensor_scalar(r2, r2, 6.0, None, ALU.mult)
            e.reciprocal(sc, r2)
            e.tensor_scalar(sc, sc, 1e-12, None, ALU.max)
            e.reciprocal(r2, sc)
            return e.tensor_scalar(r2, r2, 2.0, None, ALU.mult)

        def q_stage2(e, src, n):
            nb = n // 16
            A_, B_ = qA[:, 0:n], qB[:, 0:n]
            r2bc = qr2[:, 0:nb].unsqueeze(2).broadcast_to([128, nb, 16])
            A3 = A_.rearrange("p (nb b) -> p nb b", b=16)
            e.scalar_tensor_tensor(A_, src, -1.0, src, ALU.mult, ALU.max)
            e.tensor_tensor(A3, A3, r2bc, ALU.mult)
            e.tensor_scalar(B_, A_, 0.5, None, ALU.is_gt)
            e.scalar_tensor_tensor(B_, A_, 1.5, B_, ALU.is_gt, ALU.add)
            e.scalar_tensor_tensor(B_, A_, 2.5, B_, ALU.is_gt, ALU.add)
            return e.scalar_tensor_tensor(B_, A_, 3.5, B_, ALU.is_gt,
                                          ALU.add)

        def q_stage3(e, src, n, hi_out, lo_out):
            nb = n // 16
            A_, B_, C_ = qA[:, 0:n], qB[:, 0:n], qC[:, 0:n]
            scbc = qsc[:, 0:nb].unsqueeze(2).broadcast_to([128, nb, 16])
            A3 = A_.rearrange("p (nb b) -> p nb b", b=16)
            e.tensor_scalar(C_, A_, 5.0, None, ALU.is_gt)
            e.scalar_tensor_tensor(C_, A_, 7.0, C_, ALU.is_gt, ALU.add)
            e.scalar_tensor_tensor(B_, C_, 2.0, B_, ALU.mult, ALU.add)
            e.tensor_scalar(C_, A_, 10.0, 4.0, ALU.is_gt, ALU.mult)
            e.tensor_tensor(B_, B_, C_, ALU.add)
            e.tensor_scalar(C_, src, 0.0, None, ALU.is_ge)
            e.scalar_tensor_tensor(A_, C_, -0.5, B_, ALU.add, ALU.mult)
            return e.tensor_tensor(A3, A3, scbc, ALU.mult)

        def q_split_g(e, n, hi_out, lo_out):
            A_ = qA[:, 0:n]
            e.tensor_copy(hi_out, A_)
            return e.tensor_tensor(lo_out, A_, hi_out, ALU.subtract)

        def quant_seq(e, src, n, hi_out, lo_out):
            q_stage1(e, src, n)
            q_stage2(e, src, n)
            q_stage3(e, src, n, hi_out, lo_out)
            return q_split_g(e, n, hi_out, lo_out)

        _p3a = []

        def _p3a_tile(idx, wh, hh, ll, r0, c0, n, stok="scr", lw=()):
            p = idx % 2

            def em1(p=p, wh=wh, r0=r0, c0=c0, n=n, lw=tuple(lw)):
                dma(lambda: wh[r0:r0 + 128, c0:c0 + n],
                    lambda: wst3[:, p, 0:n], writes=[f"fst{p}"],
                    waits=rd(*lw))
                comp("vector",
                     lambda e, p=p, n=n: q_stage1(e, wst3[:, p, 0:n], n),
                     reads=[f"fst{p}"], writes=["qstg"])

            def em2(p=p, n=n):
                comp("vector",
                     lambda e, p=p, n=n: q_stage2(e, wst3[:, p, 0:n], n),
                     reads=["qstg"], writes=["qstg"])

            def em3(p=p, hh=hh, ll=ll, r0=r0, c0=c0, n=n, stok=stok):
                def qn(e, p=p, n=n):
                    q_stage3(e, wst3[:, p, 0:n], n, None, None)
                    return q_split_g(e, n, q3h[:, p, 0:n], q3l[:, p, 0:n])
                comp("vector", qn, reads=["qstg", f"fst{p}"],
                     writes=[f"q3{p}"])
                dma(lambda: q3h[:, p, 0:n],
                    lambda: hh[r0:r0 + 128, c0:c0 + n],
                    reads=[f"q3{p}"], writes=[stok])
                dma(lambda: q3l[:, p, 0:n],
                    lambda: ll[r0:r0 + 128, c0:c0 + n],
                    reads=[f"q3{p}"], writes=[stok])
            _p3a.extend((em1, em2, em3))

        _qt_n = [0]
        if _on("P3a"):
            for wh, hh, ll in ((wg_h, wgq_hi_h, wgq_lo_h),
                               (wu_h, wuq_hi_h, wuq_lo_h)):
                for ch in range(22):
                    for s2 in range(2):
                        _p3a_tile(_qt_n[0], wh, hh, ll, ch * 128,
                                  s2 * 512, 512)
                        _qt_n[0] += 1
            for ch in range(8):
                for s2 in range(5):
                    _p3a_tile(_qt_n[0], wd_h, wdq_hi_h, wdq_lo_h,
                              ch * 128, s2 * 512, 512)
                    _qt_n[0] += 1
                _p3a_tile(_qt_n[0], wd_h, wdq_hi_h, wdq_lo_h,
                          ch * 128, 2560, 256)
                _qt_n[0] += 1
        _p3a_i = [0]

        def drain_p3a(k):
            while k > 0 and _p3a_i[0] < len(_p3a):
                _p3a[_p3a_i[0]]()
                _p3a_i[0] += 1
                k -= 1

        for tc in range(8 if _on("P1") else 0):
            for tt in range(4):
                x_split_store(x_all, xa16h, xa16l, (tc * 4 + tt) * 128)
            xfm_load(xa16h, xa16l, tc * 512)
            ts = slice(tc * 512, (tc + 1) * 512)
            for mt in range(2):
                def k_mm(e, mt=mt):
                    for cc in range(8):
                        for i3, (l, r) in enumerate(mm3_list(
                                wk_hi[:, cc, mt * 128:(mt + 1) * 128],
                                wk_lo[:, cc, mt * 128:(mt + 1) * 128],
                                xfm_hi[:, cc, :], xfm_lo[:, cc, :])):
                            e.matmul(ps_q[:], l, r,
                                     start=(cc == 0 and i3 == 0),
                                     stop=(cc == 7 and i3 == 2))
                    return None
                comp("tensor", k_mm, reads=["xfm"], writes=["ps_q"],
                     waits=rd("wk"))

                def k_hi_ev(e, mt=mt, ts=ts):
                    e.activation(K_st[0:64, 2 * mt, ts], ps_q[0:64, :],
                                 ACT.Copy)
                    e.activation(K_st[0:64, 2 * mt + 1, ts], ps_q[64:128, :],
                                 ACT.Copy)
                    return e.activation(ktmp[64:128, :], ps_q[64:128, :],
                                        ACT.Copy)
                comp("scalar", k_hi_ev, writes=[f"ksth{mt}"],
                     waits=rd("ps_q") + wr("kbuf"))

                def k_lo_ev(e, mt=mt, ts=ts):
                    e.tensor_tensor(K_st[64:128, 2 * mt, ts], ps_q[0:64, :],
                                    K_st[0:64, 2 * mt, ts], ALU.subtract)
                    return e.tensor_tensor(K_st[64:128, 2 * mt + 1, ts],
                                           ps_q[64:128, :], ktmp[64:128, :],
                                           ALU.subtract)
                comp("vector", k_lo_ev, reads=["ps_q"], writes=[f"kst{mt}"],
                     waits=rd(f"ksth{mt}") + wr("kbuf"))
            drain_p3a(2)
            for mt in range(2):
                def v_mm(e, mt=mt):
                    for cc in range(8):
                        for i3, (l, r) in enumerate(mm3_list(
                                wv_hi[:, cc, mt * 128:(mt + 1) * 128],
                                wv_lo[:, cc, mt * 128:(mt + 1) * 128],
                                xfm_hi[:, cc, :], xfm_lo[:, cc, :])):
                            e.matmul(ps_q[:], l, r,
                                     start=(cc == 0 and i3 == 0),
                                     stop=(cc == 7 and i3 == 2))
                    return None
                comp("tensor", v_mm, reads=["xfm"], writes=["ps_q"],
                     waits=rd("wv"))

                def v_split(e):
                    e.tensor_copy(v16h[:], ps_q[:])
                    return e.tensor_tensor(v16l[:], ps_q[:], v16h[:],
                                           ALU.subtract)
                comp("vector", v_split, reads=["ps_q"], writes=["v16"])
                dma(lambda mt=mt, ts=ts: v16h[:],
                    lambda mt=mt, ts=ts: v16h_hbm[mt * 128:(mt + 1) * 128, ts],
                    reads=["v16"], writes=["v16s"])
                dma(lambda mt=mt, ts=ts: v16l[:],
                    lambda mt=mt, ts=ts: v16l_hbm[mt * 128:(mt + 1) * 128, ts],
                    reads=["v16"], writes=["v16s"])
        if _on("P1"):
            for kt in range(NKT):
                pp = 0
                for i, (hbm, vdst) in enumerate(((v16h_hbm, Vp_hi),
                                                 (v16l_hbm, Vp_lo))):
                    dma_t(lambda kt=kt, hbm=hbm:
                          hbm[:, kt * 128:(kt + 1) * 128],
                          lambda pp=pp, i=i: vstg[:, pp, i, :],
                          writes=[f"vstg{pp}{i}"],
                          waits=rd("v16s") + wr(f"vstg{pp}{i}"))

                    def vslot(e, kt=kt, pp=pp, i=i, vdst=vdst):
                        return e.tensor_copy(
                            vdst[:, kt, :, 0:64],
                            vstg[:, pp, i, :].rearrange(
                                "p (a b) -> p a b", a=4))
                    comp("gpsimd", vslot, reads=[f"vstg{pp}{i}"],
                         writes=["vp"], waits=rd("vaug"))

        esP1.close()
        P.barrier()
        esP2 = contextlib.ExitStack()
        sbP2 = mk_sb(esP2)
        wqp_hi = sbP2("wqp_hi", [128, 2, 8, 128], BF16)
        wqp_lo = sbP2("wqp_lo", [128, 2, 8, 128], BF16)
        wo_hi = sbP2("wo_hi", [128, 8, D], BF16)
        wo_lo = sbP2("wo_lo", [128, 8, D], BF16)

        def wq_load(pair):
            hA, hB = PAIRS[pair]
            buf = pair % 2
            for hf, hd in ((0, hA), (1, hB)):
                for hbm, dst in ((wqs_hi_h, wqp_hi), (wqs_lo_h, wqp_lo)):
                    dma(lambda hbm=hbm, hd=hd:
                        hbm[:, hd * 64:(hd + 1) * 64]
                        .rearrange("(cc p) d -> p cc d", p=128),
                        lambda dst=dst, buf=buf, hf=hf:
                        dst[:, buf, :, hf * 64:hf * 64 + 64],
                        writes=[f"wqp{buf}"],
                        waits=rd("wscr") + wr(f"wqp{buf}"))
        for hbm, sbuf in ((wos_hi_h, wo_hi), (wos_lo_h, wo_lo)):
            for pr, (hA, hB) in enumerate(PAIRS):
                for hf, hd in ((0, hA), (1, hB)):
                    dma(lambda hbm=hbm, hd=hd: hbm[hd * 64:(hd + 1) * 64, :],
                        lambda sbuf=sbuf, pr=pr, hf=hf:
                            sbuf[hf * 64:(hf + 1) * 64, pr, :],
                        writes=["wo"], waits=rd("wscr"))

        SCL = float(HD) ** -0.5

        def q_proj(pair):
            buf = pair % 2
            sl = pair % 2

            def q_mm(e, buf=buf):
                for cc in range(8):
                    for i3, (l, r) in enumerate(mm3_list(
                            wqp_hi[:, buf, cc, :], wqp_lo[:, buf, cc, :],
                            xfm_hi[:, cc, :], xfm_lo[:, cc, :])):
                        e.matmul(ps_q[:], l, r,
                                 start=(cc == 0 and i3 == 0),
                                 stop=(cc == 7 and i3 == 2))
                return None
            comp("tensor", q_mm, reads=["xfm", f"wqp{buf}"],
                 writes=["ps_q"],
                 waits=wr(f"qphi{sl}", f"qplov{sl}", f"qplo{sl}"))
            if pair + 2 < NP_:
                wq_load(pair + 2)

            def q_hi_ev(e, sl=sl):
                e.activation(Qstk[0:64, sl, 0, :], ps_q[0:64, :], ACT.Copy,
                             scale=SCL)
                e.activation(Qstk[64:128, sl, 0, :], ps_q[0:64, :], ACT.Copy,
                             scale=SCL)
                e.activation(Qstk[64:128, sl, 1, :], ps_q[64:128, :],
                             ACT.Copy, scale=SCL)
                return e.activation(Qstk[0:64, sl, 1, :], ps_q[64:128, :],
                                    ACT.Copy, scale=SCL)
            comp("scalar", q_hi_ev, writes=[f"qphi{sl}"],
                 waits=rd("ps_q"))

            def q_lo_ev(e, sl=sl):
                e.scalar_tensor_tensor(
                    Qlo2[0:64, sl, 0, :], ps_q[0:64, :], SCL,
                    Qstk[0:64, sl, 0, :], ALU.mult, ALU.subtract)
                return e.scalar_tensor_tensor(
                    Qlo2[64:128, sl, 1, :], ps_q[64:128, :], SCL,
                    Qstk[64:128, sl, 1, :], ALU.mult, ALU.subtract)
            comp("vector", q_lo_ev, reads=["ps_q"],
                 writes=[f"qplov{sl}"], waits=rd(f"qphi{sl}"))

            def q_lo_rep(e, sl=sl):
                e.tensor_copy(Qlo2[64:128, sl, 0, :], Qlo2[0:64, sl, 0, :])
                return e.tensor_copy(Qlo2[0:64, sl, 1, :],
                                     Qlo2[64:128, sl, 1, :])
            comp("gpsimd", q_lo_rep, reads=[f"qplov{sl}"],
                 writes=[f"qplo{sl}"])

        PVLAG = 3
        for qc in range(NQC if _on("P2") else 0):
            for tt in range(4):
                x_split_store(x_own, xo16h, xo16l, (qc * 4 + tt) * 128)
            xfm_load(xo16h, xo16l, qc * 512)
            wq_load(0)
            wq_load(1)
            q_proj(0)
            q_proj(1)
            def finalize(fp, qc=qc):
                for hf, (po, pon) in enumerate(((ps_o, "ps_o"),
                                                (ps_ob, "ps_ob"))):
                    def recip(e, po=po):
                        e.reciprocal(rden_r[:], po[64:65, :])
                        e.tensor_copy(rd16h, rden_r[:])
                        return e.tensor_tensor(rd16l, rden_r[:],
                                               rd16h, ALU.subtract)
                    comp("vector", recip, reads=[pon], writes=["rden"])

                    def d_mm(e):
                        e.matmul(ps_rt[0:64, :], ones64[:], rd16h,
                                 start=True, stop=False)
                        return e.matmul(ps_rt[0:64, :], ones64[:],
                                        rd16l, start=False, stop=True)
                    comp("tensor", d_mm, reads=["rden"], writes=["ps_rt"],
                         waits=rd("ones64"))
                    comp("vector",
                         lambda e: e.tensor_copy(D_sb[:], ps_rt[0:64, :]),
                         reads=["ps_rt"], writes=["D_sb"])

                    def n_ev(e, hf=hf, po=po):
                        return e.tensor_tensor(
                            attn_st[hf * 64:hf * 64 + 64, :],
                            po[0:64, :], D_sb[:], ALU.mult)
                    comp("vector", n_ev, reads=[pon, "D_sb"],
                         writes=["atst"])

                def pair_ev(e, fp=fp):
                    e.tensor_copy(attn_hi[:, fp, :], attn_st[:])
                    return e.tensor_tensor(attn_lo[:, fp, :], attn_st[:],
                                           attn_hi[:, fp, :], ALU.subtract)
                comp("vector", pair_ev, reads=["atst"], writes=[f"at{fp}"])
                if DEBUG_TAPS and qc == 0 and fp == 0:
                    dma(lambda: attn_st[:], lambda: dbg_at[:], reads=["at0"])
                    dma(lambda: rden_r, lambda: dbg_rd[:],
                        reads=["rden"])

            for pair in range(NP_):
                hA, hB = PAIRS[pair]
                gt = 0 if hA < 8 else 1
                sl = pair % 2

                def emit_pv(kt, gt=gt):
                    par = kt % 3

                    def pv_mm(e, gt=gt, kt=kt, par=par):
                        pvc = ((Vp_hi, p16h, 66), (Vp_lo, p16h, 64),
                               (Vp_hi, p16l, 66))
                        for i3, (vv, pp, w) in enumerate(pvc):
                            e.matmul(ps_o[0:w, :],
                                     vv[:, kt, gt * 2, 0:w],
                                     pp[:, par, 0, :],
                                     start=(kt == 0 and i3 == 0),
                                     stop=(kt == NKT - 1 and i3 == 2),
                                     tile_position=(0, 0))
                            e.matmul(ps_ob[0:w, :],
                                     vv[:, kt, gt * 2 + 1, 0:w],
                                     pp[:, par, 1, :],
                                     start=(kt == 0 and i3 == 0),
                                     stop=(kt == NKT - 1 and i3 == 2),
                                     tile_position=(0, 0))
                        return None
                    extra = rd("vp")
                    if kt == 0:
                        extra += wr("ps_o", "ps_ob")
                    wrts = ["ps_o", "ps_ob"] if kt == NKT - 1 else []
                    comp("tensor", pv_mm,
                         reads=[f"ph{par}", f"plv{par}", f"plg{par}"],
                         writes=wrts, waits=extra)

                for kt in range(NKT):
                    par = kt % 3
                    pb2 = kt % 2
                    bk = kt % 2
                    if kt >= PVLAG:
                        emit_pv(kt - PVLAG)
                    sA, sAn = (ps_a, "ps_a") if bk == 0 else (ps_b, "ps_b")
                    sB, sBn = (ps_c, "ps_c") if bk == 0 else (ps_d, "ps_d")
                    ks = slice(kt * 128, (kt + 1) * 128)

                    def sc_mm(e, gt=gt, sl=sl, ks=ks, sA=sA, sB=sB):
                        e.matmul(sA[:], K_st[:, gt * 2, ks],
                                 Qstk[:, sl, 0, :], start=True, stop=False)
                        e.matmul(sB[:], K_st[:, gt * 2 + 1, ks],
                                 Qstk[:, sl, 1, :], start=True, stop=False)
                        e.matmul(sA[:], K_st[0:64, gt * 2, ks],
                                 Qlo2[0:64, sl, 0, :], start=False,
                                 stop=True, tile_position=(0, 0))
                        e.matmul(sB[:], K_st[0:64, gt * 2 + 1, ks],
                                 Qlo2[0:64, sl, 1, :], start=False,
                                 stop=True, tile_position=(0, 0))
                        return None
                    extra = []
                    if kt == 0:
                        extra = rd("kst0", "kst1")
                    comp("tensor", sc_mm, reads=[f"qphi{sl}", f"qplo{sl}"],
                         writes=[sAn, sBn], waits=extra)

                    def expo(e, sA=sA, sB=sB, par=par, pb2=pb2):
                        e.activation(p_buf[:, pb2, 0, :], sA[:], ACT.Exp)
                        e.activation(p16h[:, par, 0, :], sA[:], ACT.Exp)
                        e.activation(p_buf[:, pb2, 1, :], sB[:], ACT.Exp)
                        return e.activation(p16h[:, par, 1, :], sB[:],
                                            ACT.Exp)
                    comp("scalar", expo, reads=[sAn, sBn],
                         writes=[f"pbf{pb2}", f"pbg{pb2}", f"ph{par}"])

                    def pspl_sub0(e, par=par, pb2=pb2):
                        return e.tensor_tensor(p16l[:, par, 0, :],
                                               p_buf[:, pb2, 0, :],
                                               p16h[:, par, 0, :],
                                               ALU.subtract)
                    comp("vector", pspl_sub0,
                         reads=[f"pbf{pb2}", f"ph{par}"],
                         writes=[f"plv{par}"])

                    def pspl_sub1(e, par=par, pb2=pb2):
                        return e.tensor_tensor(p16l[:, par, 1, :],
                                               p_buf[:, pb2, 1, :],
                                               p16h[:, par, 1, :],
                                               ALU.subtract)
                    comp("gpsimd", pspl_sub1,
                         reads=[f"pbg{pb2}", f"ph{par}"],
                         writes=[f"plg{par}"])

                    if kt == 1 and pair > 0:
                        finalize(pair - 1)
                        if pair + 1 < NP_:
                            q_proj(pair + 1)
                    if kt % 2 == 1:
                        drain_p3a(1)
                for kt in range(NKT - PVLAG, NKT):
                    emit_pv(kt)
                if DEBUG_TAPS and qc == 0 and pair == 0:
                    dma(lambda: Qstk[:, 0, 0, :], lambda: dbg_qph[:],
                        reads=["qphi0"])
                    dma(lambda: Qlo2[:, 0, 0, :], lambda: dbg_qpl[:],
                        reads=["qplo0"])
                    dma(lambda: p16h[:, 1, :, :], lambda: dbg_ph[:],
                        waits=rd("ph1"))
                    dma(lambda: p_buf[:, 1, :, :], lambda: dbg_pf[:],
                        waits=rd("pbf1"))
                    dma(lambda: K_st[:, 0:2, :], lambda: dbg_kh[:],
                        waits=rd("kst0", "kst1"))
            finalize(NP_ - 1)


            for qs in range(4):
                j = qc * 4 + qs
                dma(lambda r0=j * 128: x_own[r0:r0 + 128, :],
                    lambda: x_stage[:], writes=["xs0"])
                for dc in range(2):
                    def wo_mm(e, qs=qs, dc=dc):
                        qsl = slice(qs * 128, (qs + 1) * 128)
                        dsl = slice(dc * 512, (dc + 1) * 512)
                        for pr in range(NP_):
                            for i3, (l, r) in enumerate(mm3_list(
                                    attn_hi[:, pr, qsl], attn_lo[:, pr, qsl],
                                    wo_hi[:, pr, dsl], wo_lo[:, pr, dsl])):
                                e.matmul(ps_q[:], l, r,
                                         start=(pr == 0 and i3 == 0),
                                         stop=(pr == NP_ - 1 and i3 == 2))
                        return None
                    comp("tensor", wo_mm,
                         reads=[f"at{pr}" for pr in range(NP_)],
                         writes=["ps_q"], waits=rd("wo"))

                    def y_ev(e, dc=dc):
                        dsl = slice(dc * 512, (dc + 1) * 512)
                        return e.tensor_tensor(y_sb[:, dsl], x_stage[:, dsl],
                                               ps_q[:], ALU.add)
                    comp("vector", y_ev, reads=["ps_q", "xs0"],
                         writes=[f"ydc{dc}"])

                if DEBUG_TAPS and j == 0:
                    dma(lambda: y_sb[:], lambda: dbg_y[:],
                        waits=rd("ydc0", "ydc1"))
                emit_ln("x1s0")
                dma(lambda: x1_sb[:],
                    lambda r0=j * 128: x1_hbm[r0:r0 + 128, :],
                    reads=["x1s0"], writes=["x1h"])
            if _on("P3b"):
                for rb in range(4):
                    for s2 in range(2):
                        _p3a_tile(_qt_n[0], x1_hbm, x1q16h, x1q16l,
                                  (qc * 4 + rb) * 128, s2 * 512, 512,
                                  stok="x1qs", lw=("x1h",))
                        _qt_n[0] += 1

        drain_p3a(len(_p3a))

        esP2.close()
        esA.close()
        P.barrier()
        P.barrier()
        esP3b = contextlib.ExitStack()
        sbP3b = mk_sb(esP3b)
        x1q_hi = sbP3b("x1q_hi_s", [128, 8, 1024], BF16)
        x1q_lo = sbP3b("x1q_lo_s", [128, 8, 1024], BF16)
        wgq_hi = sbP3b("wgq_hi_s", [128, 2, 8, 512], BF16)
        wgq_lo = sbP3b("wgq_lo_s", [128, 2, 8, 512], BF16)
        wuq_hi = sbP3b("wuq_hi_s", [128, 2, 8, 512], BF16)
        wuq_lo = sbP3b("wuq_lo_s", [128, 2, 8, 512], BF16)
        wdq_hi = sbP3b("wdq_hi_s", [128, 2, 4, 1024], BF16)
        wdq_lo = sbP3b("wdq_lo_s", [128, 2, 4, 1024], BF16)
        dacc = sbP3b("dacc", [128, FTT, D], F32)
        gs_sb = sbP3b("gs_sb", [128, 512], F32)
        h_sb = sbP3b("h_sb", [128, 512], F32)
        hq_hi = sbP3b("hq_hi", [128, 512], BF16)
        hq_lo = sbP3b("hq_lo", [128, 512], BF16)
        hqf_hi = sbP3b("hqf_hi", [128, 4, 1024], BF16)
        hqf_lo = sbP3b("hqf_lo", [128, 4, 1024], BF16)

        def _hq_stages(nn):
            subs = [(s * 512, min(512, nn - s * 512))
                    for s in range((nn + 511) // 512)]
            for b0, w in subs:
                comp("vector",
                     lambda e, b0=b0, w=w:
                     q_stage1(e, h_sb[:, b0:b0 + w], w),
                     reads=["h_sb"], writes=["qstg"])
                comp("vector",
                     lambda e, b0=b0, w=w:
                     q_stage2(e, h_sb[:, b0:b0 + w], w),
                     reads=["qstg"], writes=["qstg"])
                def s3(e, b0=b0, w=w):
                    q_stage3(e, h_sb[:, b0:b0 + w], w, None, None)
                    return q_split_g(e, w, hq_hi[:, b0:b0 + w],
                                     hq_lo[:, b0:b0 + w])
                comp("vector", s3, reads=["qstg", "h_sb"], writes=["hq"])

        for tc in range(FTC if _on("P3b") else 0):
            for cc in range(8):
                dma_t(lambda tc=tc, cc=cc:
                      x1q16h[tc * 1024:(tc + 1) * 1024,
                             cc * 128:(cc + 1) * 128],
                      lambda cc=cc: x1q_hi[:, cc, :],
                      writes=["x1q"], waits=rd("x1qs") + wr("x1q"))
                dma_t(lambda tc=tc, cc=cc:
                      x1q16l[tc * 1024:(tc + 1) * 1024,
                             cc * 128:(cc + 1) * 128],
                      lambda cc=cc: x1q_lo[:, cc, :],
                      writes=["x1q"])

            dn_pend = []
            for nci, (n0, nn) in enumerate(NCH):
                nnt = nn // 128
                bp = nci % 2
                for cc in range(8):
                    dma_t(lambda n0=n0, nn=nn, cc=cc:
                          wgq_hi_h[n0:n0 + nn, cc * 128:(cc + 1) * 128],
                          lambda nn=nn, cc=cc, bp=bp:
                          wgq_hi[:, bp, cc, 0:nn],
                          writes=[f"wgq{bp}"],
                          waits=rd("scr") + wr(f"wgq{bp}"))
                    dma_t(lambda n0=n0, nn=nn, cc=cc:
                          wgq_lo_h[n0:n0 + nn, cc * 128:(cc + 1) * 128],
                          lambda nn=nn, cc=cc, bp=bp:
                          wgq_lo[:, bp, cc, 0:nn],
                          writes=[f"wgq{bp}"])
                    dma_t(lambda n0=n0, nn=nn, cc=cc:
                          wuq_hi_h[n0:n0 + nn, cc * 128:(cc + 1) * 128],
                          lambda nn=nn, cc=cc, bp=bp:
                          wuq_hi[:, bp, cc, 0:nn],
                          writes=[f"wgq{bp}"])
                    dma_t(lambda n0=n0, nn=nn, cc=cc:
                          wuq_lo_h[n0:n0 + nn, cc * 128:(cc + 1) * 128],
                          lambda nn=nn, cc=cc, bp=bp:
                          wuq_lo[:, bp, cc, 0:nn],
                          writes=[f"wgq{bp}"])
                for nt in range(nnt):
                    dma_t(lambda n0=n0, nt=nt:
                          wdq_hi_h[:, n0 + nt * 128:n0 + (nt + 1) * 128],
                          lambda nt=nt, bp=bp: wdq_hi[:, bp, nt, :],
                          writes=[f"wdq{bp}"],
                          waits=rd("scr") + wr(f"wdq{bp}"))
                    dma_t(lambda n0=n0, nt=nt:
                          wdq_lo_h[:, n0 + nt * 128:n0 + (nt + 1) * 128],
                          lambda nt=nt, bp=bp: wdq_lo[:, bp, nt, :],
                          writes=[f"wdq{bp}"])

                for tt in range(FTT):
                    par = tt % 2
                    g_ps, gn = ((ps_a, "ps_a") if par == 0
                                else (ps_b, "ps_b"))
                    u_ps, un = ((ps_c, "ps_c") if par == 0
                                else (ps_d, "ps_d"))
                    tsl = slice(tt * 128, (tt + 1) * 128)

                    def gu_mm(e, g_ps=g_ps, u_ps=u_ps, tsl=tsl, nn=nn,
                              bp=bp):
                        for cc in range(8):
                            for i3, (l, r) in enumerate(mm3_list(
                                    x1q_hi[:, cc, tsl],
                                    x1q_lo[:, cc, tsl],
                                    wgq_hi[:, bp, cc, 0:nn],
                                    wgq_lo[:, bp, cc, 0:nn])):
                                e.matmul(g_ps[:, 0:nn], l, r,
                                         start=(cc == 0 and i3 == 0),
                                         stop=(cc == 7 and i3 == 2))
                            for i3, (l, r) in enumerate(mm3_list(
                                    x1q_hi[:, cc, tsl],
                                    x1q_lo[:, cc, tsl],
                                    wuq_hi[:, bp, cc, 0:nn],
                                    wuq_lo[:, bp, cc, 0:nn])):
                                e.matmul(u_ps[:, 0:nn], l, r,
                                         start=(cc == 0 and i3 == 0),
                                         stop=(cc == 7 and i3 == 2))
                        return None
                    comp("tensor", gu_mm, reads=[f"wgq{bp}", "x1q"],
                         writes=[gn, un])

                    def silu(e, g_ps=g_ps, nn=nn):
                        return e.activation(gs_sb[:, 0:nn], g_ps[:, 0:nn],
                                            ACT.Silu)
                    comp("scalar", silu, reads=[gn], writes=["gs"])

                    def hmul(e, u_ps=u_ps, nn=nn):
                        return e.tensor_tensor(h_sb[:, 0:nn],
                                               gs_sb[:, 0:nn],
                                               u_ps[:, 0:nn], ALU.mult)
                    comp("vector", hmul, reads=[un, "gs"],
                         writes=["h_sb"])

                    _hq_stages(nn)

                    dma(lambda nn=nn: hq_hi[:, 0:nn],
                        lambda tt=tt, nn=nn:
                        hq16h[tt * 128:(tt + 1) * 128, 0:nn],
                        reads=["hq"], writes=["hqs"])
                    dma(lambda nn=nn: hq_lo[:, 0:nn],
                        lambda tt=tt, nn=nn:
                        hq16l[tt * 128:(tt + 1) * 128, 0:nn],
                        reads=["hq"], writes=["hqs"])

                    for _ in range(2):
                        if dn_pend:
                            dn_pend.pop(0)()
                while dn_pend:
                    dn_pend.pop(0)()

                for nt in range(nnt):
                    dma_t(lambda nt=nt:
                          hq16h[:, nt * 128:(nt + 1) * 128],
                          lambda nt=nt: hqf_hi[:, nt, :],
                          reads=["hqs"], writes=["hqf"])
                    dma_t(lambda nt=nt:
                          hq16l[:, nt * 128:(nt + 1) * 128],
                          lambda nt=nt: hqf_lo[:, nt, :],
                          reads=["hqs"], writes=["hqf"])

                def mk_dn(tt, dc, nci=nci, nnt=nnt, bp=bp):
                    def emit():
                        tsl = slice(tt * 128, (tt + 1) * 128)
                        po, pon = ((ps_o, "ps_o") if dc == 0
                                   else (ps_ob, "ps_ob"))

                        def dn_mm(e, tsl=tsl, dc=dc, nnt=nnt, bp=bp,
                                  po=po):
                            dsl = slice(dc * 512, (dc + 1) * 512)
                            for nt in range(nnt):
                                for i3, (l, r) in enumerate(mm3_list(
                                        hqf_hi[:, nt, tsl],
                                        hqf_lo[:, nt, tsl],
                                        wdq_hi[:, bp, nt, dsl],
                                        wdq_lo[:, bp, nt, dsl])):
                                    e.matmul(po[:], l, r,
                                             start=(nt == 0 and i3 == 0),
                                             stop=(nt == nnt - 1
                                                   and i3 == 2))
                            return None
                        rds = (["hqf", f"wdq{bp}"]
                               if (tt == FTT - 1 and dc == 1) else [])
                        comp("tensor", dn_mm, reads=rds, writes=[pon],
                             waits=rd("hqf", f"wdq{bp}"))

                        def d_add(e, tt=tt, dc=dc, nci=nci, po=po):
                            dsl = slice(dc * 512, (dc + 1) * 512)
                            if nci == 0:
                                return e.tensor_copy(dacc[:, tt, dsl],
                                                     po[:])
                            return e.tensor_tensor(dacc[:, tt, dsl],
                                                   dacc[:, tt, dsl],
                                                   po[:], ALU.add)
                        comp("vector", d_add, reads=[pon],
                             writes=[f"da{tt}_{dc}"])
                    return emit
                dn_pend = [mk_dn(tt, dc)
                           for tt in range(FTT) for dc in range(2)]
            while dn_pend:
                dn_pend.pop(0)()

            for tt in range(FTT):
                j = tc * FTT + tt
                r0 = j * 128
                dma(lambda r0=r0: x1_hbm[r0:r0 + 128, :],
                    lambda: x_stage[:], writes=["xs0"], waits=rd("x1h"))

                def y2(e, tt=tt):
                    return e.tensor_tensor(y_sb[:], x_stage[:],
                                           dacc[:, tt, :], ALU.add)
                comp("vector", y2,
                     reads=["xs0", f"da{tt}_0", f"da{tt}_1"],
                     writes=["ydc0", "ydc1"])

                emit_ln("x1s0")
                dma(lambda: x1_sb[:],
                    lambda r0=r0: out_h[r0:r0 + 128, :], reads=["x1s0"])

        final = [("dm", P.ctr["dm"]), ("da", P.ctr["da"])]
        P.op("sync", lambda e: None, final, noinc=True)

        sems = {s: es.enter_context(nc.semaphore(s)) for s in SEMS}
        blk = es.enter_context(nc.Block())

        @blk.sync
        def _(e):
            P.emit("sync", e, sems)

        @blk.tensor
        def _(e):
            P.emit("tensor", e, sems)

        @blk.vector
        def _(e):
            P.emit("vector", e, sems)

        @blk.scalar
        def _(e):
            P.emit("scalar", e, sems)

        @blk.gpsimd
        def _(e):
            P.emit("gpsimd", e, sems)

        esP3b.close()

    return nc


def host_consts():
    return {}


_PROG = None


def _get_prog():
    global _PROG
    if _PROG is None:
        _PROG = build_program()
    return _PROG


def kernel(L0_attn_x, L0_attn_Wq, L0_attn_Wk, L0_attn_Wv, L0_attn_Wo,
           L0_ffn_w_gate, L0_ffn_w_up, L0_ffn_w_down):
    x = np.ascontiguousarray(np.asarray(L0_attn_x, dtype=np.float32))
    wq = np.ascontiguousarray(np.asarray(L0_attn_Wq, dtype=np.float32))
    wk = np.ascontiguousarray(np.asarray(L0_attn_Wk, dtype=np.float32))
    wv = np.ascontiguousarray(np.asarray(L0_attn_Wv, dtype=np.float32))
    wo = np.ascontiguousarray(np.asarray(L0_attn_Wo, dtype=np.float32))
    wg = np.ascontiguousarray(np.asarray(L0_ffn_w_gate, dtype=np.float32))
    wu = np.ascontiguousarray(np.asarray(L0_ffn_w_up, dtype=np.float32))
    wd = np.ascontiguousarray(np.asarray(L0_ffn_w_down, dtype=np.float32))

    nc = _get_prog()
    in_maps = []
    for c in range(NCORES):
        b, half = c // 2, c % 2
        in_maps.append({
            "x_all": x[b],
            "x_own": np.ascontiguousarray(x[b, half * TQ:(half + 1) * TQ]),
            "wq": wq, "wk": wk, "wv": wv, "wo": wo,
            "wg": wg, "wu": wu, "wd": wd,
        })
    res = run_bass_kernel_spmd(nc, in_maps, core_ids=list(range(NCORES)))
    out = np.empty((B, T, D), dtype=np.float32)
    for c in range(NCORES):
        b, half = c // 2, c % 2
        out[b, half * TQ:(half + 1) * TQ] = np.asarray(res.results[c]["out"])
    return out
